# version 05
# baseline (speedup 1.0000x reference)
import numpy as np
import ml_dtypes
from contextlib import ExitStack

import concourse.bass as bass
import concourse.mybir as mybir
from concourse.bass_utils import run_bass_kernel_spmd
from concourse.library_overlay import lower_extended_insts
from concourse.bass import IndirectOffsetOnAxis

F32 = mybir.dt.float32
BF16 = mybir.dt.bfloat16
FP8 = mybir.dt.float8e4
I32 = mybir.dt.int32
AF = mybir.ActivationFunctionType
ALU = mybir.AluOpType
AX = mybir.AxisListType

NCORES = 8
B_FULL, S, D = 32, 2048, 1024
B = B_FULL // NCORES
T = B * S
NT = T // 128
DC = D // 128
E = 8
CEXP = 1536
F_MOE = 2048
FC = F_MOE // 128
NH, DK = 16, 64
LN_EPS = 1e-5
BISECT_ITERS = 40

ENGINES = ("T", "V", "S", "G", "Y")


NDMASEM = 8


class Op:
    __slots__ = ("eng", "fn", "deps", "dma", "idx", "cum", "need_inc", "tag",
                 "coll", "osz", "epos", "slot")

    def __init__(self, eng, fn, deps, dma, idx, tag="", coll=False, osz=0):
        self.eng, self.fn, self.deps, self.dma, self.idx = eng, fn, list(deps), dma, idx
        self.cum, self.need_inc, self.tag, self.coll = 0, False, tag, coll
        self.osz, self.epos, self.slot = osz, 0, 0

    @property
    def asyn(self):
        return self.dma or self.coll

    @property
    def incval(self):
        return 16 if self.dma else 1


class Buf:
    def __init__(self):
        self.writers = []
        self.readers = []

    def rdeps(self):
        return list(self.writers)

    def wdeps(self):
        return self.writers + self.readers

    def did_read(self, op):
        self.readers.append(op)

    def did_write(self, op):
        self.writers.append(op)
        if len(self.writers) > 96:
            self.writers = self.writers[-96:]
        self.readers = []


class Prog:
    def __init__(self):
        self.ops = []

    @staticmethod
    def _flat(bs):
        out = []
        for b in bs:
            if isinstance(b, (tuple, list)):
                out.extend(b)
            else:
                out.append(b)
        return out

    def add(self, eng, fn, deps=(), dma=False, tag="", reads=(), writes=(),
            coll=False, osz=0):
        reads = self._flat(reads)
        writes = self._flat(writes)
        dd = list(deps)
        for b in reads:
            dd += b.rdeps()
        for b in writes:
            dd += b.wdeps()
        seen, deps2 = set(), []
        for d in dd:
            if d is not None and id(d) not in seen:
                seen.add(id(d))
                deps2.append(d)
        op = Op(eng, fn, deps2, dma, len(self.ops), tag, coll, osz)
        self.ops.append(op)
        for b in reads:
            b.did_read(op)
        for b in writes:
            b.did_write(op)
        return op

    def emit(self, block, sems, dsems):
        for op in self.ops:
            if op.asyn:
                op.need_inc = True
            for d in op.deps:
                if d.eng != op.eng or d.asyn:
                    d.need_inc = True
        per_eng = {e: [o for o in self.ops if o.eng == e] for e in ENGINES}
        for e in ENGINES:
            for i, o in enumerate(per_eng[e]):
                o.epos = i
        cnt = {e: 0 for e in ENGINES}
        dcnt = {e: [0] * NDMASEM for e in ENGINES}
        dseq = {e: 0 for e in ENGINES}
        for op in self.ops:
            if op.dma:
                op.slot = dseq[op.eng] % NDMASEM
                dseq[op.eng] += 1
                dcnt[op.eng][op.slot] += 16
                op.cum = dcnt[op.eng][op.slot]
            else:
                if op.need_inc:
                    cnt[op.eng] += op.incval
                op.cum = cnt[op.eng]
        for e in ENGINES:
            nd = [o for o in per_eng[e] if not o.dma]
            if nd and not nd[-1].need_inc:
                nd[-1].need_inc = True
                cnt[e] += nd[-1].incval
                nd[-1].cum = cnt[e]
        final = dict(cnt)

        def emit_stream(eng_obj, ename):
            waited = {e: 0 for e in ENGINES}
            dwaited = {e: [0] * NDMASEM for e in ENGINES}
            for op in per_eng[ename]:
                need_drain = False
                for d in op.deps:
                    if d.dma:
                        if d.cum > dwaited[d.eng][d.slot]:
                            eng_obj.wait_ge(dsems[d.eng][d.slot], d.cum)
                            dwaited[d.eng][d.slot] = d.cum
                    elif (d.eng != ename or d.asyn) and d.cum > waited[d.eng]:
                        eng_obj.wait_ge(sems[d.eng], d.cum)
                        waited[d.eng] = d.cum
                    if (d.eng == ename and not d.asyn
                            and op.epos - d.epos <= 2 and d.osz <= 256):
                        need_drain = True
                if need_drain and ename in ("V", "S", "G"):
                    eng_obj.drain()
                inst = op.fn(eng_obj)
                if op.need_inc:
                    sem = dsems[ename][op.slot] if op.dma else sems[ename]
                    if inst is None:
                        eng_obj.sem_inc(sem, op.incval)
                    else:
                        inst.then_inc(sem, op.incval)
            if ename == "Y":
                for e in ENGINES:
                    if final.get(e, 0) > waited.get(e, 0):
                        eng_obj.wait_ge(sems[e], final[e])
                    for sl in range(NDMASEM):
                        if dcnt[e][sl] > dwaited[e][sl]:
                            eng_obj.wait_ge(dsems[e][sl], dcnt[e][sl])

        @block.tensor
        def _(t):
            emit_stream(t, "T")

        @block.vector
        def _(v):
            emit_stream(v, "V")

        @block.scalar
        def _(s):
            emit_stream(s, "S")

        @block.gpsimd
        def _(g):
            emit_stream(g, "G")

        @block.sync
        def _(y):
            emit_stream(y, "Y")


def build_program(enable):
    nc = bass.Bass("TRN2", target_bir_lowering=False)
    P = Prog()

    x_in = nc.dram_tensor("x_in", [T, D], F32, kind="ExternalInput").ap()
    out = nc.dram_tensor("out", [T, D], F32, kind="ExternalOutput").ap()
    w = {}
    for name, shape in [
        ("l0a_logdt", [1, D]), ("l0a_logA", [D, 32]), ("l0a_Aim", [D, 32]),
        ("l0a_C", [D, 64]), ("l0a_D", [1, D]),
        ("l0f_Wg", [D, E]), ("l0f_W1", [E, D, F_MOE]), ("l0f_W2", [E, F_MOE, D]),
        ("l1a_wq", [D, D]), ("l1a_bq", [1, D]), ("l1a_wk", [D, D]),
        ("l1a_bk", [1, D]), ("l1a_wv", [D, D]), ("l1a_bv", [1, D]),
        ("l1a_wo", [D, D]), ("l1a_bo", [1, D]),
        ("l1f_w", [D, D]), ("l1f_b", [1, D]),
    ]:
        w[name] = nc.dram_tensor(name, shape, F32, kind="ExternalInput").ap()
    ident_bf = nc.dram_tensor("ident_bf", [128, 128], BF16, kind="ExternalInput").ap()
    ident_f32 = nc.dram_tensor("ident_f32", [128, 128], F32, kind="ExternalInput").ap()
    consts = nc.dram_tensor("consts", [128, 64], F32, kind="ExternalInput").ap()
    ones_row = nc.dram_tensor("ones_row", [1, 128], F32, kind="ExternalInput").ap()
    rt_c = nc.dram_tensor("rt_c", [128, 128], BF16, kind="ExternalInput").ap()
    gmat_t = nc.dram_tensor("gmat_t", [8, 128], F32, kind="ExternalInput").ap()
    utri_c = nc.dram_tensor("utri_c", [128, 128], F32, kind="ExternalInput").ap()
    utri64_c = nc.dram_tensor("utri64_c", [64, 64], F32, kind="ExternalInput").ap()
    rope_c = nc.dram_tensor("rope_c", [128, 4 * S], F32, kind="ExternalInput").ap()

    dbg = nc.dram_tensor("dbg", [128, 160], F32, kind="ExternalOutput").ap()
    xres = nc.dram_tensor("xres", [T, D], F32, kind="Internal").ap()
    scr_f32 = nc.dram_tensor("scr_f32", [T, D], F32, kind="Internal").ap()
    lnbf = nc.dram_tensor("lnbf", [T, D], BF16, kind="Internal").ap()
    lnx_ext = nc.dram_tensor("lnx_ext", [T, D + 64], F32, kind="Internal").ap()
    xsort_d = nc.dram_tensor("xsort_d", [E * CEXP + 128, D + 64], F32, kind="Internal").ap()
    ysort_d = nc.dram_tensor("ysort_d", [E * CEXP + 1, D], BF16, kind="Internal").ap()
    vd_d = nc.dram_tensor("vd_d", [D, 129, 64], F32, kind="Internal").ap()
    wtab_d = nc.dram_tensor("wtab_d", [D, 64, 132], F32, kind="Internal").ap()
    kpad_d = nc.dram_tensor("kpad_d", [D, 256], F32, kind="Internal").ap()
    ktoep_d = nc.dram_tensor("ktoep_d", [D, 128, 128], BF16, kind="Internal").ap()
    vw_d = nc.dram_tensor("vw_d", [D, 128, 64], BF16, kind="Internal").ap()
    lnT_d = nc.dram_tensor("lnT_d", [D, T], BF16, kind="Internal").ap()
    ycm_d = nc.dram_tensor("ycm_d", [D, T], BF16, kind="Internal").ap()
    w1bf_d = nc.dram_tensor("w1bf_d", [E * F_MOE, D], BF16, kind="Internal").ap()
    w2bf_d = nc.dram_tensor("w2bf_d", [E * F_MOE, D], BF16, kind="Internal").ap()
    prbT = nc.dram_tensor("prbT", [1, E * T], F32, kind="Internal").ap()
    prbA = nc.dram_tensor("prbA", [NCORES, E * T], F32, kind="Internal").ap()
    qkT = nc.dram_tensor("qkT", [2 * D, T], BF16, kind="Internal").ap()
    vtok = nc.dram_tensor("vtok", [T, D], BF16, kind="Internal").ap()

    with ExitStack() as ES:
        En = ES.enter_context
        sb = {}

        def S_(name, shape, dt):
            sb[name] = En(nc.sbuf_tensor(name, shape, dt))
            return sb[name]

        bufs = {}

        def BT(name):
            if name not in bufs:
                bufs[name] = Buf()
            return bufs[name]

        ARN = S_("ARN", [128, 16 * 2048], BF16)

        def AR(c0, ncols, dt=BF16):
            ap = ARN[:, c0:c0 + ncols]
            if dt == F32:
                ap = ap.bitcast(F32)
            elif dt == FP8:
                ap = ap.bitcast(FP8)
            regs = tuple(BT(f"ARr{r}") for r in
                         range(c0 // 2048, (c0 + ncols + 2047) // 2048))
            return ap, regs

        xt = [S_(f"xt{i}", [128, D], F32) for i in range(2)]
        xnew = [S_(f"xnew{i}", [128, D], F32) for i in range(2)]
        lnt = [S_("lnt0", [128, D], F32)] * 2
        ybf = [S_(f"ybf{i}", [128, D], BF16) for i in range(2)]
        xq8 = [S_(f"xq8_{i}", [128, DC, 128], FP8) for i in range(2)]
        tiny = S_("tiny", [128, 16], F32)
        tiny2 = S_("tiny2", [128, 16], F32)
        stats = S_("stats", [128, 2 * NT], F32)
        amx = S_("amx", [128, 1], F32)
        scrow = S_("scrow", [1, 32], F32)
        k128 = S_("k128", [128, 1], F32)
        s2_128 = S_("s2_128", [128, 1], F32)
        bias128 = S_("bias128", [128, D], F32)
        idn_f32 = S_("idn_f32", [128, 128], F32)
        idn_bfs = S_("idn_bfs", [128, 128], BF16)
        cons = S_("cons", [128, 64], F32)
        wrow = S_("wrow", [128, D], F32)
        onesr = S_("onesr", [1, 128], F32)
        wqt8, wqt8_r = AR(12 * 2048, 4096, FP8)
        S_("dbgt", [128, 64], F32)
        S_("dbgt2", [4, 64], F32)
        S_("vtmp2", [128, 16, 129], F32)
        wg_sb = S_("wg_sb", [128, DC, E], F32)
        gmt = S_("gmt", [8, 128], F32)
        utri = S_("utri", [128, 128], F32)
        utri64 = S_("utri64", [64, 64], F32)
        p8s = S_("p8s", [128, 128], F32)
        lo8 = S_("lo8", [8, 4], F32)
        ge8 = S_("ge8", [8, 4], F32)
        thrB = S_("thrB", [128, 8], F32)
        mid128 = S_("mid128", [128, 1], F32)
        dI = [S_(f"dI{i}", [128, 1], I32) for i in range(2)]
        xls = S_("xls", [128, D + 64], F32)
        gsort = [S_(f"gsort{i}", [128, 2], F32) for i in range(2)]

        wT = {n: S_(f"wT_{n}", [128, DC * D], BF16) for n in ("q", "k", "v")}
        wT["o"] = wT["q"]
        bqk = S_("bqk", [128, 16], F32)
        bv128 = S_("bv128", [128, D], F32)
        bo128 = S_("bo128", [128, D], F32)
        rtc = S_("rtc", [128, 128], BF16)
        ropes, ropes_r = AR(8 * 2048, 8192, F32)
        _lnaT_ap, lnaT_r = AR(0, 8192)
        lnaT = [_lnaT_ap.rearrange("p (a b) -> p a b", a=DC)] * 2
        qstg = [S_(f"qstg{i}", [128, 1024], BF16) for i in range(2)]
        ropt1 = [S_("ropt1_0", [128, 1024], F32)] * 2
        ropt2 = [S_("ropt2_0", [128, 1024], F32)] * 2
        qrt = [S_(f"qrt{i}", [128, 1024], BF16) for i in range(2)]
        qtb = [S_(f"qtb{i}", [64, S], BF16) for i in range(2)]
        ktb = [S_(f"ktb{i}", [64, S], BF16) for i in range(2)]
        vtb = [AR(28672, 1040)[0].rearrange("p (h c) -> p h c", h=16)] * 2
        ptile = [S_("ptile0", [128, S], BF16)] * 2
        otch = [AR(i * 2048, 2048)[0] for i in range(DC)]
        PSP = [En(nc.psum_tensor(f"ps{i}", [128, 1024], F32)) for i in range(4)]
        psB = [PSP[0][:, :], PSP[1][:, :]]
        psT = PSP[2][:, 0:128]
        psU = PSP[3][:, 0:128]

        sems = {e: En(nc.semaphore(f"sem{e}")) for e in ENGINES}
        dsems = {e: [En(nc.semaphore(f"dsem{e}{i}")) for i in range(NDMASEM)]
                 for e in "YGS"}
        dsems["V"] = dsems["Y"]; dsems["T"] = dsems["Y"]
        block = En(nc.Block())

        xres_b = [BT(f"xres{i}") for i in range(NT)]

        def dma(eng, dst, src, deps=(), reads=(), writes=(), tag="dma"):
            return P.add(eng, lambda E_, dst=dst, src=src: E_.dma_start(out=dst, in_=src),
                         deps, dma=True, tag=tag, reads=reads, writes=writes)

        op_idnf = dma("Y", idn_f32[:, :], ident_f32, writes=(BT("idnf"),))
        dma("Y", idn_bfs[:, :], ident_bf, writes=(BT("idnbf"),))
        op_cons = dma("Y", cons[:, :], consts, writes=(BT("cons"),))
        dma("Y", onesr[:, :], ones_row, writes=(BT("ones"),))
        for i in range(NT):
            dma("Y", xres[i * 128:(i + 1) * 128, :], x_in[i * 128:(i + 1) * 128, :],
                writes=(xres_b[i],), tag="xinit")

        def prep_wT(dst, src_ap, tag, extra=()):
            for oc in range(DC):
                ld = dma("G", wrow[:, :], src_ap[oc * 128:(oc + 1) * 128, :],
                         writes=(BT("wrow"),), tag=f"{tag}ld")
                for dc in range(DC):
                    pst = psT if dc % 2 == 0 else psU
                    bpst = BT("ps2") if dc % 2 == 0 else BT("ps3")
                    P.add("T", lambda t, dc=dc, pst=pst: t.transpose(
                        pst[:, :], wrow[:, dc * 128:(dc + 1) * 128], idn_f32[:, :]),
                        reads=(BT("wrow"), BT("idnf")), writes=(bpst,), osz=16384)
                    eng = "V" if dc % 2 == 0 else "S"
                    if eng == "V":
                        P.add("V", lambda v, dc=dc, oc=oc, pst=pst: v.tensor_copy(
                            dst[:, dc * D + oc * 128: dc * D + oc * 128 + 128], pst[:, :]),
                            reads=(bpst,), writes=(BT(f"{tag}_w"),) + extra, osz=16384)
                    else:
                        P.add("S", lambda s, dc=dc, oc=oc, pst=pst: s.copy(
                            dst[:, dc * D + oc * 128: dc * D + oc * 128 + 128], pst[:, :]),
                            reads=(bpst,), writes=(BT(f"{tag}_w"),) + extra, osz=16384)

        if "fp8a" in enable or "fp8b" in enable:
            P.add("V", lambda v: v.memset(amx[:, :], 0.0), writes=(BT("amx"),))
            for oc in range(DC):
                ld = dma("G", wrow[:, :], w["l1f_w"][oc * 128:(oc + 1) * 128, :],
                         writes=(BT("wrow"),), tag="wld")
                P.add("V", lambda v: v.tensor_reduce(
                    tiny[:, 0:1], wrow[:, :], AX.X, ALU.max, apply_absolute_value=True),
                    reads=(BT("wrow"),), writes=(BT("tiny"),))
                P.add("V", lambda v: v.tensor_tensor(
                    amx[:, 0:1], amx[:, 0:1], tiny[:, 0:1], ALU.max),
                    reads=(BT("tiny"), BT("amx")), writes=(BT("amx"),))
            P.add("T", lambda t: t.transpose(psT[0:1, :], amx[:, 0:1], idn_f32[:, :]),
                  reads=(BT("amx"), BT("idnf")), writes=(BT("ps2"),))
            P.add("V", lambda v: v.reduce_max(scrow[0:1, 0:1], psT[0:1, 0:128], AX.X),
                  reads=(BT("ps2"),), writes=(BT("wsc"),))
            P.add("T", lambda t: t.matmul(psT[:, 1:2], onesr[0:1, :],
                  scrow[0:1, 0:1], start=True, stop=True),
                  reads=(BT("wsc"), BT("ones")), writes=(BT("ps2"),))
            wsc128 = S_("wsc128", [128, 4], F32)
            P.add("V", lambda v: v.tensor_scalar(wsc128[:, 0:1], psT[:, 1:2],
                  1e-12, None, ALU.max), reads=(BT("ps2"),), writes=(BT("wsc128"),))
            P.add("V", lambda v: v.reciprocal(wsc128[:, 1:2], wsc128[:, 0:1]),
                  reads=(BT("wsc128"),), writes=(BT("wsc128"),))
            P.add("V", lambda v: v.tensor_scalar(wsc128[:, 1:2], wsc128[:, 1:2],
                  448.0, 448.0, ALU.mult, op1=ALU.min),
                  reads=(BT("wsc128"),), writes=(BT("wsc128"),))
            P.add("V", lambda v: v.tensor_scalar(wsc128[:, 2:3], wsc128[:, 1:2],
                  0.5, None, ALU.mult), reads=(BT("wsc128"),), writes=(BT("wsc128"),))
            P.add("V", lambda v: v.reciprocal(wsc128[:, 3:4], wsc128[:, 1:2]),
                  reads=(BT("wsc128"),), writes=(BT("wsc128"),))
            for oc in range(DC):
                ld2 = dma("G", wrow[:, :], w["l1f_w"][oc * 128:(oc + 1) * 128, :],
                          writes=(BT("wrow"),), tag="wld2")
                for dc in range(DC):
                    pst = psT if dc % 2 == 0 else psU
                    bpst = BT("ps2") if dc % 2 == 0 else BT("ps3")
                    P.add("T", lambda t, dc=dc, pst=pst: t.transpose(
                        pst[:, :], wrow[:, dc * 128:(dc + 1) * 128], idn_f32[:, :]),
                        reads=(BT("wrow"), BT("idnf")), writes=(bpst,), osz=16384)
                    P.add("V" if dc % 2 == 0 else "S",
                          (lambda v, dc=dc, oc=oc, pst=pst: v.tensor_scalar(
                              wqt8[:, dc * D + oc * 128: dc * D + oc * 128 + 128],
                              pst[:, :], wsc128[:, 2:3], None, ALU.mult))
                          if dc % 2 == 0 else
                          (lambda s, dc=dc, oc=oc, pst=pst: s.activation(
                              wqt8[:, dc * D + oc * 128: dc * D + oc * 128 + 128],
                              pst[:, :], AF.Identity, scale=wsc128[:, 2:3])),
                          reads=(bpst, BT("wsc128")), writes=(wqt8_r,), osz=16384)
            dma("G", ropt2[0][0:1, 0:D], w["l1f_b"], writes=(BT("ropt2_0"),))
            for hb in range(2):
                P.add("T", lambda t, hb=hb: t.matmul(psB[0][:, hb * 512:(hb + 1) * 512],
                      onesr[0:1, :], ropt2[0][0:1, hb * 512:hb * 512 + 512],
                      start=True, stop=True),
                      reads=(BT("ropt2_0"), BT("ones")), writes=(BT("ps0"),))
            P.add("V", lambda v: v.tensor_copy(bias128[:, :], psB[0][:, 0:D]),
                  reads=(BT("ps0"),), writes=(BT("bias128"),))

        def ln_stats_pass(phase, with_absmax=False):
            if with_absmax:
                P.add("V", lambda v: v.memset(amx[:, :], 0.0), writes=(BT("amx"),))
            for i in range(NT):
                sl = xt[i % 2]
                bsl = BT(f"xt{i % 2}")
                ld = dma("Y", sl[:, :], xres[i * 128:(i + 1) * 128, :],
                         reads=(xres_b[i],), writes=(bsl,), tag=f"{phase}lnld")
                P.add("V", lambda v, sl=sl: v.reduce_sum(tiny[:, 0:1], sl[:, :], AX.X),
                      reads=(bsl,), writes=(BT("tiny"),))
                P.add("V", lambda v, sl=sl: v.tensor_tensor(
                    lnt[0][:, :], sl[:, :], sl[:, :], ALU.mult),
                    reads=(bsl,), writes=(BT("lnt0"),))
                P.add("V", lambda v: v.reduce_sum(tiny[:, 1:2], lnt[0][:, :], AX.X),
                      reads=(BT("lnt0"),), writes=(BT("tiny"),))
                P.add("V", lambda v: v.tensor_scalar(tiny[:, 2:3], tiny[:, 0:1],
                      1.0 / D, None, ALU.mult), reads=(BT("tiny"),), writes=(BT("tiny"),))
                P.add("V", lambda v: v.tensor_scalar(tiny[:, 3:4], tiny[:, 1:2],
                      1.0 / D, None, ALU.mult), reads=(BT("tiny"),), writes=(BT("tiny"),))
                P.add("V", lambda v: v.tensor_tensor(tiny[:, 4:5], tiny[:, 2:3],
                      tiny[:, 2:3], ALU.mult), reads=(BT("tiny"),), writes=(BT("tiny"),))
                P.add("V", lambda v: v.tensor_tensor(tiny[:, 5:6], tiny[:, 3:4],
                      tiny[:, 4:5], ALU.subtract), reads=(BT("tiny"),), writes=(BT("tiny"),))
                P.add("V", lambda v: v.tensor_scalar(tiny[:, 6:7], tiny[:, 5:6],
                      LN_EPS, None, ALU.add), reads=(BT("tiny"),), writes=(BT("tiny"),))
                sq = P.add("S", lambda s: s.activation(tiny2[:, 0:1], tiny[:, 6:7], AF.Sqrt),
                      reads=(BT("tiny"),), writes=(BT("tiny2"),))
                P.add("V", lambda v, i=i: v.reciprocal(stats[:, 2 * i:2 * i + 1], tiny2[:, 0:1]),
                      reads=(BT("tiny2"),), writes=(BT(f"st{i}"),))
                P.add("V", lambda v, i=i: v.tensor_scalar(
                    stats[:, 2 * i + 1:2 * i + 2], tiny[:, 2:3],
                    -1.0, stats[:, 2 * i:2 * i + 1], ALU.mult, op1=ALU.mult),
                    reads=(BT("tiny"), BT(f"st{i}")), writes=(BT(f"st{i}"),))
                if with_absmax:
                    nm = P.add("S", lambda s, sl=sl, i=i: s.activation(
                        xnew[1][:, :], sl[:, :], AF.Identity,
                        bias=stats[:, 2 * i + 1:2 * i + 2],
                        scale=stats[:, 2 * i:2 * i + 1]),
                        reads=(bsl, BT(f"st{i}")), writes=(BT("xnew1"),))
                    P.add("V", lambda v: v.tensor_reduce(
                        tiny[:, 7:8], xnew[1][:, :], AX.X, ALU.max,
                        apply_absolute_value=True),
                        reads=(BT("xnew1"),), writes=(BT("tiny"),))
                    P.add("V", lambda v: v.tensor_tensor(
                        amx[:, 0:1], amx[:, 0:1], tiny[:, 7:8], ALU.max),
                        reads=(BT("tiny"), BT("amx")), writes=(BT("amx"),))

        def fp8_phase(phase):
            amax_loc = nc.dram_tensor(f"amax_loc_{phase}", [1, 8], F32, kind="Internal").ap()
            amax_all = nc.dram_tensor(f"amax_all_{phase}", [1, 8], F32, kind="Internal").ap()
            ln_stats_pass(phase, with_absmax=True)
            P.add("T", lambda t: t.transpose(psT[0:1, :], amx[:, 0:1], idn_f32[:, :]),
                  reads=(BT("amx"), BT("idnf")), writes=(BT("ps2"),))
            P.add("V", lambda v: v.reduce_max(scrow[0:1, 8:9], psT[0:1, 0:128], AX.X),
                  reads=(BT("ps2"),), writes=(BT("xsc"),))
            dma("G", amax_loc[0:1, 0:1], scrow[0:1, 8:9], reads=(BT("xsc"),), writes=(BT("amaxd"),))
            cc = P.add("G", lambda g: g.collective_compute(
                "AllReduce", ALU.max, [list(range(NCORES))], [amax_loc], [amax_all]),
                reads=(BT("amaxd"),), writes=(BT("amaxd"),), coll=True)
            dma("G", scrow[0:1, 8:9], amax_all[0:1, 0:1], reads=(BT("amaxd"),), writes=(BT("xsc"),))
            P.add("T", lambda t: t.matmul(psT[:, 1:2], onesr[0:1, :],
                  scrow[0:1, 8:9], start=True, stop=True),
                  reads=(BT("xsc"), BT("ones")), writes=(BT("ps2"),))
            xsc128 = tiny2[:, 8:10]
            P.add("V", lambda v: v.tensor_scalar(xsc128[:, 0:1], psT[:, 1:2],
                  1e-12, None, ALU.max), reads=(BT("ps2"),), writes=(BT("xsc128"),))
            P.add("V", lambda v: v.reciprocal(xsc128[:, 0:1], xsc128[:, 0:1]),
                  reads=(BT("xsc128"),), writes=(BT("xsc128"),))
            P.add("V", lambda v: v.tensor_scalar(xsc128[:, 0:1], xsc128[:, 0:1],
                  448.0, 448.0, ALU.mult, op1=ALU.min),
                  reads=(BT("xsc128"),), writes=(BT("xsc128"),))
            P.add("V", lambda v: v.tensor_scalar(s2_128[:, 0:1], xsc128[:, 0:1],
                  0.5, None, ALU.mult), reads=(BT("xsc128"),), writes=(BT("s2b"),))
            P.add("V", lambda v: v.reciprocal(xsc128[:, 1:2], xsc128[:, 0:1]),
                  reads=(BT("xsc128"),), writes=(BT("xsc128"),))
            P.add("V", lambda v: v.tensor_tensor(k128[:, 0:1], xsc128[:, 1:2],
                  wsc128[:, 3:4], ALU.mult),
                  reads=(BT("xsc128"), BT("wsc128")), writes=(BT("k4b"),))
            P.add("V", lambda v: v.tensor_scalar(k128[:, 0:1], k128[:, 0:1],
                  4.0, None, ALU.mult), reads=(BT("k4b"),), writes=(BT("k4b"),))
            for i in range(NT):
                sl, bsl = xt[i % 2], BT(f"xt{i % 2}")
                lsl, blsl = lnt[0], BT("lnt0")
                q8, bq8 = xq8[i % 2], BT(f"xq8_{i % 2}")
                ps, bps = psB[i % 2], BT(f"ps{i % 2}")
                yb, byb = ybf[i % 2], BT(f"ybf{i % 2}")
                xn, bxn = xnew[i % 2], BT(f"xnew{i % 2}")
                ld = dma("Y", sl[:, :], xres[i * 128:(i + 1) * 128, :],
                         reads=(xres_b[i],), writes=(bsl,), tag="fp8ld")
                P.add("S", lambda s, sl=sl, lsl=lsl, i=i: s.activation(
                    lsl[:, :], sl[:, :], AF.Identity,
                    bias=stats[:, 2 * i + 1:2 * i + 2], scale=stats[:, 2 * i:2 * i + 1]),
                    reads=(bsl, BT(f"st{i}")), writes=(blsl,))
                for dc in range(DC):
                    pst = psT if dc % 2 == 0 else psU
                    bpst = BT("ps2") if dc % 2 == 0 else BT("ps3")
                    P.add("T", lambda t, lsl=lsl, dc=dc, pst=pst: t.transpose(
                        pst[:, :], lsl[:, dc * 128:(dc + 1) * 128], idn_f32[:, :]),
                        reads=(blsl, BT("idnf")), writes=(bpst,))
                    if dc % 2 == 0:
                        P.add("V", lambda v, q8=q8, dc=dc, pst=pst: v.tensor_scalar(
                            q8[:, dc, :], pst[:, :], s2_128[:, 0:1], None, ALU.mult),
                            reads=(bpst, BT("s2b")), writes=(bq8,))
                    else:
                        P.add("S", lambda s, q8=q8, dc=dc, pst=pst: s.activation(
                            q8[:, dc, :], pst[:, :], AF.Copy, scale=s2_128[:, 0:1]),
                            reads=(bpst, BT("s2b")), writes=(bq8,))
                for dc in range(DC):
                    for hb in range(2):
                        P.add("T", lambda t, q8=q8, dc=dc, ps=ps, hb=hb: t.matmul(
                            ps[:, hb * 512:(hb + 1) * 512], q8[:, dc, :],
                            wqt8[:, dc * D + hb * 512: dc * D + (hb + 1) * 512],
                            start=(dc == 0), stop=(dc == DC - 1)),
                            reads=(bq8, wqt8_r), writes=(bps,))
                P.add("V", lambda v, ps=ps, yb=yb: v.scalar_tensor_tensor(
                    yb[:, :], ps[:, :], k128[:, 0:1], bias128[:, :],
                    ALU.mult, ALU.add),
                    reads=(bps, BT("k4b"), BT("bias128")), writes=(byb,))
                P.add("V", lambda v, sl=sl, yb=yb, xn=xn: v.scalar_tensor_tensor(
                    xn[:, :], yb[:, :], 0.5, sl[:, :], ALU.mult, ALU.add),
                    reads=(byb, bsl), writes=(bxn,))
                dma("Y", xres[i * 128:(i + 1) * 128, :], xn[:, :],
                    reads=(bxn,), writes=(xres_b[i],), tag="fp8st")



        def ssm_setup():
            PI_HI = float(np.float32(np.pi))
            TPI_HI = float(np.float32(2 * np.pi))
            TPI_LO = float(np.float32(2 * np.pi - np.float64(np.float32(2 * np.pi))))
            ldt128 = S_("ldt128", [128, 8], F32)
            P.add("Y", lambda y: y.dma_start(
                out=ldt128[:, :], in_=bass.AP(w["l0a_logdt"].tensor, 0, [[1, 128], [128, 8]]),
                allow_slow_non_contiguous=True), dma=True, writes=(BT("ldt"),))
            drow = wrow[0:1, :]
            dma("Y", drow[:, :], w["l0a_D"], reads=(BT("wrow"),), writes=(BT("wrow"),))
            dmat = bv128
            for hb in range(2):
                P.add("T", lambda t, hb=hb: t.matmul(
                    PSP[0][:, hb * 512:(hb + 1) * 512], onesr[0:1, :],
                    drow[0:1, hb * 512:hb * 512 + 512], start=True, stop=True),
                    reads=(BT("wrow"), BT("ones")), writes=(BT("ps0"),), osz=512)
            P.add("V", lambda v: v.tensor_copy(dmat[:, :], PSP[0][:, 0:D]),
                  reads=(BT("ps0"),), writes=(BT("dmat"), BT("bv128")), osz=D)
            dt8 = S_("dt8", [128, 8], F32)
            P.add("S", lambda s: s.activation(dt8[:, :], ldt128[:, :], AF.Exp),
                  reads=(BT("ldt"),), writes=(BT("dt8"),))
            smv, smv_r = AR(14 * 2048, 1024, F32)
            la = smv[:, 0:32]
            ai = smv[:, 32:64]
            ct = smv[:, 64:128]
            cw = smv[:, 128:192]
            pw2 = smv[:, 192:256]
            t32 = [smv[:, 256 + 32 * i:288 + 32 * i] for i in range(8)]
            tb = [BT(f"t{i}") for i in range(8)]
            _kkv = sb["ptile0"][:, 0:528].bitcast(F32)
            kknl = _kkv[:, 0:132]
            kk2 = _kkv[:, 132:264]
            vand, vand_r = AR(0, 16512, F32)
            vandv = vand.rearrange("p (l r) -> p l r", l=129)
            wblk, wblk_r = AR(16512, 4224, F32)
            wbv = wblk.rearrange("p (r l) -> p r l", r=16)
            vtmpv = wblk[:, 0:2048].rearrange("p (a b) -> p a b", a=64)

            def TT(dst, a, b, op, rd, wr, osz=64):
                P.add("V", lambda v, dst=dst, a=a, b=b, op=op: v.tensor_tensor(
                    dst, a, b, op), reads=rd, writes=wr, osz=osz)

            def TS(dst, a, s1, s2, op, op1, rd, wr, osz=64):
                if op1 is None:
                    P.add("V", lambda v, dst=dst, a=a: v.tensor_scalar(
                        dst, a, s1, None, op), reads=rd, writes=wr, osz=osz)
                else:
                    P.add("V", lambda v, dst=dst, a=a: v.tensor_scalar(
                        dst, a, s1, s2, op, op1=op1), reads=rd, writes=wr, osz=osz)

            def STT(dst, a, sc, b, op0, op1, rd, wr, osz=64):
                P.add("V", lambda v, dst=dst, a=a, sc=sc, b=b: v.scalar_tensor_tensor(
                    dst, a, sc, b, op0, op1), reads=rd, writes=wr, osz=osz)

            P.add("V", lambda v: v.memset(kknl[:, :], 0.0), writes=(BT("kknl"),))
            P.add("V", lambda v: v.memset(kk2[:, :], 0.0), writes=(BT("kk2"),))
            for ht in range(8):
                P.add("Y", lambda y, ht=ht: y.dma_start(
                    out=kpad_d[ht * 128:(ht + 1) * 128, 0:128], in_=kknl[:, 0:128]),
                    dma=True, reads=(BT("kknl"),), writes=(BT("kpad"),), tag="kz")
                P.add("Y", lambda y, ht=ht: y.dma_start(
                    out=kpad_d[ht * 128:(ht + 1) * 128, 128:256], in_=kk2[:, 0:128]),
                    dma=True, reads=(BT("kk2"),), writes=(BT("kpad"),), tag="kz2")
            for ht in range(8):
                r0 = ht * 128
                dma("Y", la[:, :], w["l0a_logA"][r0:r0 + 128, :], writes=(BT("la"),))
                dma("Y", ai[:, :], w["l0a_Aim"][r0:r0 + 128, :], writes=(BT("ai"),))
                dma("Y", ct[:, :], w["l0a_C"][r0:r0 + 128, :], writes=(BT("ct"),))
                P.add("S", lambda s: s.activation(t32[0][:, :], la[:, :], AF.Exp),
                      reads=(BT("la"),), writes=(tb[0],))
                P.add("V", lambda v, ht=ht: v.tensor_scalar(t32[1][:, :], t32[0][:, :],
                      dt8[:, ht:ht + 1], -1.0, ALU.mult, op1=ALU.mult),
                      reads=(tb[0], BT("dt8")), writes=(tb[1],))
                P.add("V", lambda v, ht=ht: v.tensor_scalar(t32[2][:, :], ai[:, :],
                      dt8[:, ht:ht + 1], None, ALU.mult),
                      reads=(BT("ai"), BT("dt8")), writes=(tb[2],))
                for mlt in (16.0, 8.0, 4.0, 2.0, 1.0, 1.0):
                    TS(t32[3][:, :], t32[2][:, :], mlt * PI_HI, None, ALU.is_gt, None,
                       (tb[2],), (tb[3],))
                    STT(t32[2][:, :], t32[3][:, :], -mlt * TPI_HI, t32[2][:, :],
                        ALU.mult, ALU.add, (tb[3], tb[2]), (tb[2],))
                    STT(t32[2][:, :], t32[3][:, :], -mlt * TPI_LO, t32[2][:, :],
                        ALU.mult, ALU.add, (tb[3], tb[2]), (tb[2],))
                    TS(t32[3][:, :], t32[2][:, :], -mlt * PI_HI, None, ALU.is_lt, None,
                       (tb[2],), (tb[3],))
                    STT(t32[2][:, :], t32[3][:, :], mlt * TPI_HI, t32[2][:, :],
                        ALU.mult, ALU.add, (tb[3], tb[2]), (tb[2],))
                    STT(t32[2][:, :], t32[3][:, :], mlt * TPI_LO, t32[2][:, :],
                        ALU.mult, ALU.add, (tb[3], tb[2]), (tb[2],))
                P.add("S", lambda s: s.activation(t32[4][:, :], t32[1][:, :], AF.Exp),
                      reads=(tb[1],), writes=(tb[4],))
                P.add("S", lambda s: s.activation(t32[5][:, :], t32[2][:, :], AF.Sin),
                      reads=(tb[2],), writes=(tb[5],))
                P.add("S", lambda s: s.activation(t32[6][:, :], t32[2][:, :], AF.Abs),
                      reads=(tb[2],), writes=(tb[6],))
                TS(t32[6][:, :], t32[6][:, :], -1.0, PI_HI / 2, ALU.mult, ALU.add,
                   (tb[6],), (tb[6],))
                P.add("S", lambda s: s.activation(t32[6][:, :], t32[6][:, :], AF.Sin),
                      reads=(tb[6],), writes=(tb[6],))
                TT(pw2[:, 0:32], t32[4][:, :], t32[6][:, :], ALU.mult,
                   (tb[4], tb[6]), (BT("pw2"),))
                TT(pw2[:, 32:64], t32[4][:, :], t32[5][:, :], ALU.mult,
                   (tb[4], tb[5]), (BT("pw2"),))
                TT(t32[7][:, :], t32[0][:, :], t32[0][:, :], ALU.mult, (tb[0],), (tb[7],))
                STT(t32[7][:, :], ai[:, :], 1.0, t32[7][:, :], ALU.mult, ALU.mult,
                    (BT("ai"), tb[7]), (tb[7],))
                TT(t32[7][:, :], t32[0][:, :], t32[0][:, :], ALU.mult, (tb[0],), (tb[7],))
                TT(t32[3][:, :], ai[:, :], ai[:, :], ALU.mult, (BT("ai"),), (tb[3],))
                TT(t32[7][:, :], t32[7][:, :], t32[3][:, :], ALU.add, (tb[7], tb[3]), (tb[7],))
                P.add("V", lambda v: v.reciprocal(t32[7][:, :], t32[7][:, :]),
                      reads=(tb[7],), writes=(tb[7],))
                TS(t32[1][:, :], pw2[:, 0:32], 1.0, None, ALU.subtract, None,
                   (BT("pw2"),), (tb[1],))
                ccr = ct.rearrange("p (n two) -> p n two", two=2)
                TT(cw[:, 0:32], ccr[:, :, 0], t32[1][:, :], ALU.mult,
                   (BT("ct"), tb[1]), (BT("cw"),))
                STT(t32[3][:, :], ccr[:, :, 1], -1.0, pw2[:, 32:64], ALU.mult, ALU.mult,
                    (BT("ct"), BT("pw2")), (tb[3],))
                TT(cw[:, 0:32], cw[:, 0:32], t32[3][:, :], ALU.add,
                   (BT("cw"), tb[3]), (BT("cw"),))
                TT(t32[3][:, :], ccr[:, :, 0], pw2[:, 32:64], ALU.mult,
                   (BT("ct"), BT("pw2")), (tb[3],))
                TT(cw[:, 32:64], ccr[:, :, 1], t32[1][:, :], ALU.mult,
                   (BT("ct"), tb[1]), (BT("cw"),))
                TT(cw[:, 32:64], cw[:, 32:64], t32[3][:, :], ALU.add,
                   (BT("cw"), tb[3]), (BT("cw"),))
                STT(t32[3][:, :], cw[:, 0:32], -1.0, t32[0][:, :], ALU.mult, ALU.mult,
                    (BT("cw"), tb[0]), (tb[3],))
                TT(t32[2][:, :], cw[:, 32:64], ai[:, :], ALU.mult,
                   (BT("cw"), BT("ai")), (tb[2],))
                TT(t32[2][:, :], t32[2][:, :], t32[3][:, :], ALU.add, (tb[2], tb[3]), (tb[2],))
                STT(t32[3][:, :], cw[:, 0:32], -1.0, ai[:, :], ALU.mult, ALU.mult,
                    (BT("cw"), BT("ai")), (tb[3],))
                STT(t32[5][:, :], cw[:, 32:64], -1.0, t32[0][:, :], ALU.mult, ALU.mult,
                    (BT("cw"), tb[0]), (tb[5],))
                TT(t32[3][:, :], t32[3][:, :], t32[5][:, :], ALU.add, (tb[3], tb[5]), (tb[3],))
                TT(cw[:, 0:32], t32[2][:, :], t32[7][:, :], ALU.mult, (tb[2], tb[7]), (BT("cw"),))
                TT(cw[:, 32:64], t32[3][:, :], t32[7][:, :], ALU.mult, (tb[3], tb[7]), (BT("cw"),))
                P.add("V", lambda v: v.memset(vandv[:, 0, 0:32], 1.0), writes=(vand_r,))
                P.add("V", lambda v: v.memset(vandv[:, 0, 32:64], 0.0), writes=(vand_r,))
                vt = vtmpv
                Lc = 1
                while Lc < 129:
                    nn = min(Lc, 129 - Lc)
                    pwre = pw2[:, 0:32].unsqueeze(1).to_broadcast([128, nn, 32])
                    pwim = pw2[:, 32:64].unsqueeze(1).to_broadcast([128, nn, 32])
                    osn = nn * 32
                    TT(vandv[:, Lc:Lc + nn, 0:32], vandv[:, 0:nn, 0:32], pwre,
                       ALU.mult, (vand_r, BT("pw2")), (vand_r,), osn)
                    TT(vt[:, 0:nn, 0:32], vandv[:, 0:nn, 32:64], pwim,
                       ALU.mult, (vand_r, BT("pw2")), (wblk_r,), osn)
                    TT(vandv[:, Lc:Lc + nn, 0:32], vandv[:, Lc:Lc + nn, 0:32],
                       vt[:, 0:nn, 0:32], ALU.subtract, (vand_r, wblk_r), (vand_r,), osn)
                    TT(vandv[:, Lc:Lc + nn, 32:64], vandv[:, 0:nn, 0:32], pwim,
                       ALU.mult, (vand_r, BT("pw2")), (vand_r,), osn)
                    TT(vt[:, 0:nn, 0:32], vandv[:, 0:nn, 32:64], pwre,
                       ALU.mult, (vand_r, BT("pw2")), (wblk_r,), osn)
                    TT(vandv[:, Lc:Lc + nn, 32:64], vandv[:, Lc:Lc + nn, 32:64],
                       vt[:, 0:nn, 0:32], ALU.add, (vand_r, wblk_r), (vand_r,), osn)
                    if 2 * Lc < 129:
                        TT(vt[:, 0, 0:32], pw2[:, 0:32], pw2[:, 0:32], ALU.mult,
                           (BT("pw2"),), (wblk_r,))
                        TT(vt[:, 0, 32:64] if False else vt[:, 1, 0:32],
                           pw2[:, 32:64], pw2[:, 32:64], ALU.mult,
                           (BT("pw2"),), (wblk_r,))
                        TT(vt[:, 2, 0:32], pw2[:, 0:32], pw2[:, 32:64], ALU.mult,
                           (BT("pw2"),), (wblk_r,))
                        TT(pw2[:, 32:64], vt[:, 2, 0:32], vt[:, 2, 0:32], ALU.add,
                           (wblk_r,), (BT("pw2"),))
                        TT(pw2[:, 0:32], vt[:, 0, 0:32], vt[:, 1, 0:32], ALU.subtract,
                           (wblk_r,), (BT("pw2"),))
                    Lc *= 2
                P.add("Y", lambda y, ht=ht: y.dma_start(
                    out=vd_d[ht * 128:(ht + 1) * 128, :, :], in_=vandv[:, :, :]),
                    dma=True, reads=(vand_r,), writes=(BT("vd"),), tag="vdst")
                for half in range(2):
                    for q2 in range(2):
                        vre = vandv[:, :, q2 * 16:(q2 + 1) * 16].transpose([0, 2, 1])
                        vim = vandv[:, :, 32 + q2 * 16:32 + (q2 + 1) * 16].transpose([0, 2, 1])
                        cre = cw[:, q2 * 16:(q2 + 1) * 16].unsqueeze(2).to_broadcast(
                            [128, 16, 129])
                        cim = cw[:, 32 + q2 * 16:32 + (q2 + 1) * 16].unsqueeze(2).to_broadcast(
                            [128, 16, 129])
                        if half == 0:
                            TT(wbv[:, :, 0:129], vre, cre, ALU.mult,
                               (vand_r, BT("cw")), (wblk_r,), 2064)
                            TT(sb["vtmp2"][:, 0:16, 0:129], vim, cim, ALU.mult,
                               (vand_r, BT("cw")), (BT("vtmp2"),), 2064)
                            TT(wbv[:, :, 0:129], wbv[:, :, 0:129],
                               sb["vtmp2"][:, 0:16, 0:129],
                               ALU.subtract, (wblk_r, BT("vtmp2")), (wblk_r,), 2064)
                            TS(wbv[:, :, 0:129], wbv[:, :, 0:129], 2.0, None, ALU.mult,
                               None, (wblk_r,), (wblk_r,), 2064)
                            if q2 == 0:
                                P.add("V", lambda v: v.tensor_reduce(
                                    kknl[:, 0:129], wbv[:, :, 0:129].transpose([0, 2, 1]),
                                    AX.X, ALU.add), reads=(wblk_r,),
                                    writes=(BT("kknl"),), osz=129)
                            else:
                                P.add("V", lambda v: v.tensor_reduce(
                                    kk2[:, 0:129], wbv[:, :, 0:129].transpose([0, 2, 1]),
                                    AX.X, ALU.add), reads=(wblk_r,),
                                    writes=(BT("kk2"),), osz=129)
                                TT(kknl[:, 0:129], kknl[:, 0:129], kk2[:, 0:129],
                                   ALU.add, (BT("kknl"), BT("kk2")), (BT("kknl"),), 129)
                        else:
                            TT(wbv[:, :, 0:129], vim, cre, ALU.mult,
                               (vand_r, BT("cw")), (wblk_r,), 2064)
                            TT(sb["vtmp2"][:, 0:16, 0:129], vre, cim, ALU.mult,
                               (vand_r, BT("cw")), (BT("vtmp2"),), 2064)
                            TT(wbv[:, :, 0:129], wbv[:, :, 0:129],
                               sb["vtmp2"][:, 0:16, 0:129],
                               ALU.add, (wblk_r, BT("vtmp2")), (wblk_r,), 2064)
                            TS(wbv[:, :, 0:129], wbv[:, :, 0:129], -2.0, None, ALU.mult,
                               None, (wblk_r,), (wblk_r,), 2064)
                        P.add("Y", lambda y, ht=ht, half=half, q2=q2: y.dma_start(
                            out=wtab_d[ht * 128:(ht + 1) * 128,
                                       half * 32 + q2 * 16:half * 32 + (q2 + 1) * 16,
                                       0:129],
                            in_=wbv[:, :, 0:129]),
                            dma=True, reads=(wblk_r,), writes=(BT("wtab"),), tag="wtst")
                P.add("Y", lambda y, ht=ht: y.dma_start(
                    out=kpad_d[ht * 128:(ht + 1) * 128, 127:255],
                    in_=kknl[:, 0:128]),
                    dma=True, reads=(BT("kknl"),), writes=(BT("kpad"),), tag="kst")
            for j in range(128):
                P.add("G", lambda g, j=j: g.dma_start(
                    out=bass.AP(ktoep_d.tensor, j * 128, [[16384, 1024], [1, 128]]),
                    in_=bass.AP(kpad_d.tensor, 127 - j, [[256, 1024], [1, 128]])),
                    dma=True, reads=(BT("kpad"),), writes=(BT("ktoep"),), tag="tbuild")
            for cc in range(16):
                P.add("G", lambda g, cc=cc: g.dma_start(
                    out=vw_d[cc * 64:(cc + 1) * 64, :, :],
                    in_=bass.AP(vd_d.tensor, cc * 64 * 8256 + 127 * 64,
                                [[8256, 64], [-64, 128], [1, 64]])),
                    dma=True, reads=(BT("vd"),), writes=(BT("vw"),), tag="vwbuild")

        def ssm_phase():
            ln_stats_pass("ssm", with_absmax=False)
            for i in range(NT):
                sl, bsl = xt[i % 2], BT(f"xt{i % 2}")
                lb, blb = ybf[i % 2], BT(f"ybf{i % 2}")
                dma("Y", sl[:, :], xres[i * 128:(i + 1) * 128, :],
                    reads=(xres_b[i],), writes=(bsl,), tag="ssmld")
                P.add("S", lambda s, sl=sl, lb=lb, i=i: s.activation(
                    lb[:, :], sl[:, :], AF.Identity,
                    bias=stats[:, 2 * i + 1:2 * i + 2], scale=stats[:, 2 * i:2 * i + 1]),
                    reads=(bsl, BT(f"st{i}")), writes=(blb,), osz=D)
                dma("Y", lnbf[i * 128:(i + 1) * 128, :], lb[:, :],
                    reads=(blb,), writes=(BT(f"lnbf{i}"),), tag="ssmst")
            stg = [AR(0, 8192), AR(8192, 8192)]
            for dc in range(DC):
                sg, sg_r = stg[dc % 2]
                for tb in range(8):
                    rd = tuple(BT(f"lnbf{8 * tb + k}") for k in range(8))
                    P.add("Y", lambda y, dc=dc, tb=tb, sg=sg: y.dma_start(
                        out=sg[:, tb * 1024:(tb + 1) * 1024],
                        in_=lnbf[tb * 1024:(tb + 1) * 1024, dc * 128:(dc + 1) * 128],
                        transpose=True),
                        dma=True, reads=rd, writes=sg_r, tag="stgT")
                P.add("G", lambda g, dc=dc, sg=sg: g.dma_start(
                    out=lnT_d[dc * 128:(dc + 1) * 128, :], in_=sg[:, :]),
                    dma=True, reads=sg_r, writes=(BT("lnT"),), tag="stgS")
            QRe, QRe_r = AR(0, 8192, F32)
            QIm, QIm_r = AR(8192, 8192, F32)
            xjgs = [AR(16384, 1024), AR(17408, 1024)]
            tblks = [AR(18432, 2048), AR(28672, 2048)]
            wjblk_f, wjblk_r = AR(20480, 2048)
            wjblk = wjblk_f[0:64, :]
            ytmp, ytmp_r = AR(22528, 2048, F32)
            ocm, ocm_r = AR(30720, 1024)
            lamqR = p8s[:, 0:64]
            lamqI = p8s[:, 64:128]
            BLQ = BT("p8s")
            sst = [ropt1[0][:, 0:512], ropt1[0][:, 512:1024]]
            BSS = BT("ropt1_0")
            sscr = ropt2[0][:, 0:512]
            BSC = BT("ropt2_0")
            crhs = qstg[1]
            BCR = BT("qstg1")
            vwb = qstg[0]
            BVW = BT("qstg0")
            xjl = [qrt[0], qrt[1]]
            QRv = QRe.rearrange("p (gc b c) -> p gc b c", gc=64, b=4)
            QIv = QIm.rearrange("p (gc b c) -> p gc b c", gc=64, b=4)
            lamRb = lamqR[:, :].unsqueeze(2).to_broadcast([128, 64, 4])
            lamIb = lamqI[:, :].unsqueeze(2).to_broadcast([128, 64, 4])

            def xj_pass(grp, xjg, dst_r):
                for cp in range(8):
                    xl, bxl = xjl[cp % 2], BT(f"qrt{cp % 2}")
                    P.add("Y", lambda y, grp=grp, cp=cp, xl=xl: y.dma_start(
                        out=xl[:, 0:128],
                        in_=bass.AP(lnT_d.tensor, (grp * 16 + cp * 2) * T,
                                    [[T, 2], [128, 64], [1, 128]])),
                        dma=True, reads=(BT("lnT"),), writes=(bxl,), tag="xjld")
                    pst, bpst = (psT, BT("ps2")) if cp % 2 == 0 else (psU, BT("ps3"))
                    P.add("T", lambda t, xl=xl, pst=pst: t.matmul(
                        pst[:, :], xl[:, 0:128], idn_bfs[:, :], start=True, stop=True),
                        reads=(bxl, BT("idnbf")), writes=(bpst,), osz=16384)
                    P.add("S", lambda s, cp=cp, pst=pst, xjg=xjg: s.copy(
                        xjg[:, cp * 128:(cp + 1) * 128], pst[:, :]),
                        reads=(bpst,), writes=dst_r, osz=16384)

            for q in range(4):
                for gl in range(16):
                    grp = q * 16 + gl
                    gq, glo = gl >> 2, gl & 3
                    P.add("G", lambda g, grp=grp: g.dma_start(
                        out=vwb[:, :],
                        in_=bass.AP(vw_d.tensor, grp * 16 * 8192,
                                    [[64, 128], [8192, 16], [1, 64]])),
                        dma=True, reads=(BT("vw"),), writes=(BVW,), tag="vwld")
                    xjg, xjg_r = xjgs[gl % 2]
                    xj_pass(grp, xjg, (xjg_r,))
                    for ch in range(16):
                        P.add("T", lambda t, ch=ch, xjg=xjg: t.matmul(
                            PSP[0][0:64, ch * 64:(ch + 1) * 64],
                            vwb[:, ch * 64:(ch + 1) * 64],
                            xjg[:, ch * 64:(ch + 1) * 64],
                            start=True, stop=True),
                            reads=(BVW, xjg_r), writes=(BT("ps0"),), osz=64)
                    P.add("V", lambda v, gq=gq, glo=glo: v.tensor_copy(
                        QRe[gq * 32:(gq + 1) * 32, glo * 1024:(glo + 1) * 1024],
                        PSP[0][0:32, 0:1024]),
                        reads=(BT("ps0"),), writes=(QRe_r,), osz=1024)
                    P.add("V", lambda v, gq=gq, glo=glo: v.tensor_copy(
                        QIm[gq * 32:(gq + 1) * 32, glo * 1024:(glo + 1) * 1024],
                        PSP[0][32:64, 0:1024]),
                        reads=(BT("ps0"),), writes=(QIm_r,), osz=1024)
                for half in range(2):
                    lq = wrow[:, 0:64]
                    P.add("Y", lambda y, q=q, half=half, lq=lq: y.dma_start(
                        out=lq,
                        in_=bass.AP(vd_d.tensor,
                                    (q * 256 + half * 128) * 8256 + 8192,
                                    [[8256, 128], [1, 64]])),
                        dma=True, reads=(BT("vd"), BT("wrow")), writes=(BT("wrow"),),
                        tag="lamld")
                    P.add("T", lambda t, lq=lq: t.transpose(
                        psT[0:64, 0:128], lq, idn_f32[:, :]),
                        reads=(BT("wrow"), BT("idnf")), writes=(BT("ps2"),), osz=8192)
                    for q1 in range(2):
                        gQ = half * 2 + q1
                        P.add("V", lambda v, gQ=gQ, q1=q1: v.tensor_copy(
                            lamqR[gQ * 32:(gQ + 1) * 32, 0:64],
                            psT[0:32, q1 * 64:(q1 + 1) * 64]),
                            reads=(BT("ps2"),), writes=(BLQ,), osz=64)
                        P.add("V", lambda v, gQ=gQ, q1=q1: v.tensor_copy(
                            lamqI[gQ * 32:(gQ + 1) * 32, 0:64],
                            psT[32:64, q1 * 64:(q1 + 1) * 64]),
                            reads=(BT("ps2"),), writes=(BLQ,), osz=64)
                P.add("V", lambda v: v.memset(sst[0][:, :], 0.0), writes=(BSS,))
                for c in range(16):
                    cur, nxt = sst[c % 2], sst[(c + 1) % 2]
                    bcur, bnxt = BSS, BSS
                    cRe = cur[:, 0:256].rearrange("p (gc b) -> p gc b", gc=64)
                    cIm = cur[:, 256:512].rearrange("p (gc b) -> p gc b", gc=64)
                    nRe = nxt[:, 0:256].rearrange("p (gc b) -> p gc b", gc=64)
                    nIm = nxt[:, 256:512].rearrange("p (gc b) -> p gc b", gc=64)
                    m1 = sscr[:, 0:256].rearrange("p (gc b) -> p gc b", gc=64)
                    m2 = sscr[:, 256:512].rearrange("p (gc b) -> p gc b", gc=64)
                    uR = QRv[:, :, :, c]
                    uI = QIv[:, :, :, c]
                    P.add("V", lambda v, nRe=nRe, cRe=cRe: v.tensor_tensor(
                        nRe, cRe, lamRb, ALU.mult),
                        reads=(bcur, BLQ), writes=(bnxt,), osz=64)
                    P.add("V", lambda v, nIm=nIm, cIm=cIm: v.tensor_tensor(
                        nIm, cIm, lamRb, ALU.mult),
                        reads=(bcur, BLQ), writes=(bnxt,), osz=64)
                    P.add("V", lambda v, m1=m1, cIm=cIm: v.tensor_tensor(
                        m1, cIm, lamIb, ALU.mult),
                        reads=(bcur, BLQ), writes=(BSC,), osz=64)
                    P.add("V", lambda v, m2=m2, cRe=cRe: v.tensor_tensor(
                        m2, cRe, lamIb, ALU.mult),
                        reads=(bcur, BLQ), writes=(BSC,), osz=64)
                    P.add("V", lambda v, nRe=nRe, m1=m1: v.tensor_tensor(
                        nRe, nRe, m1, ALU.subtract),
                        reads=(bnxt, BSC), writes=(bnxt,), osz=64)
                    P.add("V", lambda v, nIm=nIm, m2=m2: v.tensor_tensor(
                        nIm, nIm, m2, ALU.add),
                        reads=(bnxt, BSC), writes=(bnxt,), osz=64)
                    P.add("V", lambda v, nRe=nRe, uR=uR: v.tensor_tensor(
                        nRe, nRe, uR, ALU.add),
                        reads=(bnxt, QRe_r), writes=(bnxt,), osz=64)
                    P.add("V", lambda v, nIm=nIm, uI=uI: v.tensor_tensor(
                        nIm, nIm, uI, ALU.add),
                        reads=(bnxt, QIm_r), writes=(bnxt,), osz=64)
                    P.add("V", lambda v, uR=uR, cRe=cRe: v.tensor_copy(uR, cRe),
                          reads=(bcur,), writes=QRe_r, osz=64)
                    P.add("V", lambda v, uI=uI, cIm=cIm: v.tensor_copy(uI, cIm),
                          reads=(bcur,), writes=QIm_r, osz=64)
                for gl in range(16):
                    grp = q * 16 + gl
                    gq, glo = gl >> 2, gl & 3
                    tblk, tblk_r = tblks[gl % 2]
                    xjg, xjg_r = xjgs[gl % 2]
                    P.add("Y", lambda y, grp=grp, tblk=tblk: y.dma_start(
                        out=tblk[:, :],
                        in_=bass.AP(ktoep_d.tensor, grp * 16 * 16384,
                                    [[128, 128], [16384, 16], [1, 128]])),
                        dma=True, reads=(BT("ktoep"),), writes=(tblk_r,), tag="tbld")
                    P.add("G", lambda g, grp=grp: g.dma_start(
                        out=wjblk[:, :],
                        in_=bass.AP(wtab_d.tensor, grp * 16 * 8448 + 1,
                                    [[132, 64], [8448, 16], [1, 128]])),
                        dma=True, reads=(BT("wtab"),), writes=(wjblk_r,), tag="wjld")
                    P.add("V", lambda v, gq=gq, glo=glo: v.tensor_copy(
                        crhs[0:32, :],
                        QRe[gq * 32:(gq + 1) * 32, glo * 1024:(glo + 1) * 1024]),
                        reads=(QRe_r,), writes=(BCR,), osz=1024)
                    P.add("V", lambda v, gq=gq, glo=glo: v.tensor_copy(
                        crhs[32:64, :],
                        QIm[gq * 32:(gq + 1) * 32, glo * 1024:(glo + 1) * 1024]),
                        reads=(QIm_r,), writes=(BCR,), osz=1024)
                    xj_pass(grp, xjg, (xjg_r,))
                    for ch in range(16):
                        P.add("T", lambda t, ch=ch, tblk=tblk, xjg=xjg: t.matmul(
                            PSP[1][:, ch * 64:(ch + 1) * 64],
                            tblk[:, ch * 128:(ch + 1) * 128],
                            xjg[:, ch * 64:(ch + 1) * 64],
                            start=True, stop=False),
                            reads=(tblk_r, xjg_r), writes=(BT("ps1"),), osz=64)
                        P.add("T", lambda t, ch=ch: t.matmul(
                            PSP[1][:, ch * 64:(ch + 1) * 64],
                            wjblk[:, ch * 128:(ch + 1) * 128],
                            crhs[0:64, ch * 64:(ch + 1) * 64],
                            start=False, stop=True),
                            reads=(wjblk_r, BCR), writes=(BT("ps1"),), osz=64)
                    P.add("V", lambda v: v.tensor_copy(ytmp[:, :], PSP[1][:, 0:1024]),
                          reads=(BT("ps1"),), writes=(ytmp_r,), osz=1024)
                    for k in range(8):
                        pst, bpst = (psT, BT("ps2")) if k % 2 == 0 else (psU, BT("ps3"))
                        P.add("T", lambda t, k=k, pst=pst: t.transpose(
                            pst[:, :], ytmp[:, k * 128:(k + 1) * 128], idn_f32[:, :]),
                            reads=(ytmp_r, BT("idnf")), writes=(bpst,), osz=16384)
                        P.add("S", lambda s, k=k, pst=pst: s.copy(
                            ocm[:, k * 128:(k + 1) * 128], pst[:, :]),
                            reads=(bpst,), writes=(ocm_r,), osz=16384)
                    P.add("Y", lambda y, grp=grp: y.dma_start(
                        out=bass.AP(ycm_d.tensor, grp * 16 * T,
                                    [[T, 2], [128, 64], [2 * T, 8], [1, 128]]),
                        in_=ocm[:, :].rearrange("p (k j) -> p k j", k=8)),
                        dma=True, reads=(ocm_r,), writes=(BT("ycm"),), tag="ycst")
            for i in range(NT):
                sl, bsl = xt[i % 2], BT(f"xt{i % 2}")
                lb, blb = ybf[i % 2], BT(f"ybf{i % 2}")
                yt, byt = qrt[i % 2], BT(f"qrt{i % 2}")
                xn, bxn = xnew[i % 2], BT(f"xnew{i % 2}")
                dma("Y", sl[:, :], xres[i * 128:(i + 1) * 128, :],
                    reads=(xres_b[i],), writes=(bsl,), tag="ssmr1")
                P.add("Y", lambda y, i=i, yt=yt: y.dma_start(
                    out=yt[:, :], in_=ycm_d[:, i * 128:(i + 1) * 128], transpose=True),
                    dma=True, reads=(BT("ycm"),), writes=(byt,), tag="ssmr2")
                dma("G", lb[:, :], lnbf[i * 128:(i + 1) * 128, :],
                    reads=(BT(f"lnbf{i}"),), writes=(blb,), tag="ssmr3")
                P.add("V", lambda v, xn=xn, sl=sl, yt=yt: v.tensor_tensor(
                    xn[:, :], sl[:, :], yt[:, :], ALU.add),
                    reads=(bsl, byt), writes=(bxn,), osz=D)
                P.add("V", lambda v, lb=lb: v.tensor_tensor(
                    lnt[0][:, :], lb[:, :], bv128[:, :], ALU.mult),
                    reads=(blb, BT("dmat"), BT("bv128")), writes=(BT("lnt0"),), osz=D)
                P.add("V", lambda v, xn=xn: v.tensor_tensor(
                    xn[:, :], xn[:, :], lnt[0][:, :], ALU.add),
                    reads=(bxn, BT("lnt0")), writes=(bxn,), osz=D)
                dma("Y", xres[i * 128:(i + 1) * 128, :], xn[:, :],
                    reads=(bxn,), writes=(xres_b[i],), tag="ssmr4")

        DUMP = float(E * CEXP)

        def moe_setup():
            w1p = [AR(11 * 2048, 1024), AR(11 * 2048 + 1024, 1024)]
            w2p = [AR(15 * 2048, 1024), AR(15 * 2048 + 1024, 1024)]
            for e in range(E):
                for fc in range(FC):
                    k = e * FC + fc
                    wa, wa_r = w1p[k % 2]
                    wb, wb_r = w2p[k % 2]
                    P.add("G", lambda g, wa=wa, e=e, fc=fc: g.dma_start(
                        out=wa[:, :],
                        in_=bass.AP(w["l0f_W1"].tensor, e * D * F_MOE + fc * 128,
                                    [[F_MOE, 128], [128 * F_MOE, 8], [1, 128]])),
                        dma=True, reads=(wa_r,), writes=(wa_r,), tag="w1p")
                    P.add("G", lambda g, wb=wb, e=e, fc=fc: g.dma_start(
                        out=wb[:, :],
                        in_=bass.AP(w["l0f_W2"].tensor, e * F_MOE * D + fc * 128 * D,
                                    [[D, 128], [1, D]])),
                        dma=True, reads=(wb_r,), writes=(wb_r,), tag="w2p")
                    P.add("Y", lambda y, wa=wa, k=k: y.dma_start(
                        out=w1bf_d[k * 128:(k + 1) * 128, :], in_=wa[:, :]),
                        dma=True, reads=(wa_r,), writes=(BT("w1bf"),), tag="w1ps")
                    P.add("Y", lambda y, wb=wb, k=k: y.dma_start(
                        out=w2bf_d[k * 128:(k + 1) * 128, :], in_=wb[:, :]),
                        dma=True, reads=(wb_r,), writes=(BT("w2bf"),), tag="w2ps")
            dma("Y", wg_sb[:, :, :],
                bass.AP(w["l0f_Wg"].tensor, 0, [[8, 128], [1024, 8], [1, 8]]),
                writes=(BT("wg"),), tag="wgld")
            dma("Y", gmt[:, :], gmat_t, writes=(BT("gmt"),))
            dma("Y", utri[:, :], utri_c, writes=(BT("utri"),))
            dma("Y", utri64[:, :], utri64_c, writes=(BT("utri64"),))
            zt = xnew[0]
            zop = P.add("V", lambda v: v.memset(zt[:, :], 0.0), writes=(BT("xnew0"),))
            for r in range(0, E * CEXP + 128, 128):
                n = min(128, E * CEXP + 128 - r)
                P.add("G", lambda g, r=r, n=n: g.dma_start(
                    out=xsort_d[r:r + n, 0:D], in_=zt[0:n, :]),
                    dma=True, reads=(BT("xnew0"),), writes=(BT("xs_z"),), tag="xsz")
                P.add("G", lambda g, r=r, n=n: g.dma_start(
                    out=xsort_d[r:r + n, D:D + 64], in_=zt[0:n, 0:64]),
                    dma=True, reads=(BT("xnew0"),), writes=(BT("xs_z"),), tag="xsz2")
            P.add("V", lambda v: v.memset(ybf[0][0:1, :], 0.0), writes=(BT("ybf0"),))
            P.add("G", lambda g: g.dma_start(out=ysort_d[E * CEXP:E * CEXP + 1, :],
                  in_=ybf[0][0:1, :]), dma=True, reads=(BT("ybf0"),),
                  writes=(BT("ys_z"),), tag="ysz")

        def moe_phase(phase):
            pbis, pbis_r = AR(0, 8192, F32)
            bisj, bisj_r = AR(4 * 2048, 4096)
            mxT = [AR(6 * 2048, 2048, F32), AR(7 * 2048, 2048, F32)]
            probs_all, probs_r = AR(14 * 2048, 1024, F32)
            gate_all, gate_r = AR(14 * 2048 + 1024, 128, F32)
            dest_all, dest_r = AR(14 * 2048 + 1152, 128, F32)
            bpm, bpm_r = AR(15 * 2048, 128, F32)
            bem, bem_r = AR(15 * 2048 + 128, 128, F32)
            bpu, bpu_r = AR(15 * 2048 + 256, 128, F32)
            beu, beu_r = AR(15 * 2048 + 384, 128, F32)
            meq, meq_r = AR(15 * 2048 + 512, 128, F32)
            sc1, sc1_r = AR(15 * 2048 + 576, 128, F32)
            sc2, sc2_r = AR(15 * 2048 + 704, 128, F32)
            rnk, rnk_r = AR(15 * 2048 + 832, 128, F32)
            sc3, sc3_r = AR(15 * 2048 + 960, 128, F32)
            ln_stats_pass(phase, with_absmax=False)
            for i in range(NT):
                sl, bsl = xt[i % 2], BT(f"xt{i % 2}")
                dma("Y", sl[:, :], xres[i * 128:(i + 1) * 128, :],
                    reads=(xres_b[i],), writes=(bsl,), tag="mrout")
                P.add("S", lambda s, sl=sl, i=i: s.activation(
                    lnt[0][:, :], sl[:, :], AF.Identity,
                    bias=stats[:, 2 * i + 1:2 * i + 2], scale=stats[:, 2 * i:2 * i + 1]),
                    reads=(bsl, BT(f"st{i}")), writes=(BT("lnt0"),), osz=D)
                dma("Y", lnx_ext[i * 128:(i + 1) * 128, 0:D], lnt[0][:, :],
                    reads=(BT("lnt0"),), writes=(BT(f"lnx{i}"),), tag="mlnst")

                mx, mx_r = mxT[i % 2]
                mxv = mx.rearrange("p (a b) -> p a b", a=DC)
                for dc in range(DC):
                    pst, bpst = (psT, BT("ps2")) if dc % 2 == 0 else (psU, BT("ps3"))
                    P.add("T", lambda t, dc=dc, pst=pst: t.transpose(
                        pst[:, :], lnt[0][:, dc * 128:(dc + 1) * 128], idn_f32[:, :]),
                        reads=(BT("lnt0"), BT("idnf")), writes=(bpst,), osz=16384)
                    P.add("V" if dc % 2 == 0 else "S",
                          (lambda v, dc=dc, pst=pst, mxv=mxv: v.tensor_copy(
                              mxv[:, dc, :], pst[:, :])) if dc % 2 == 0 else
                          (lambda s, dc=dc, pst=pst, mxv=mxv: s.copy(
                              mxv[:, dc, :], pst[:, :])),
                          reads=(bpst,), writes=(mx_r,), osz=16384)
                for dc in range(DC):
                    P.add("T", lambda t, dc=dc, mxv=mxv: t.matmul(
                        PSP[1][:, 0:8], mxv[:, dc, :], wg_sb[:, dc, :],
                        start=(dc == 0), stop=(dc == DC - 1)),
                        reads=(mx_r, BT("wg")), writes=(BT("ps1"),), osz=8)
                P.add("V", lambda v: v.reduce_max(tiny[:, 9:10], PSP[1][:, 0:8], AX.X),
                      reads=(BT("ps1"),), writes=(BT("tiny"),))
                P.add("V", lambda v: v.tensor_scalar(tiny[:, 10:11], tiny[:, 9:10],
                      -1.0, None, ALU.mult), reads=(BT("tiny"),), writes=(BT("tiny"),))
                P.add("S", lambda s, i=i: s.activation(
                    probs_all[:, i * 8:(i + 1) * 8], PSP[1][:, 0:8], AF.Exp,
                    bias=tiny[:, 10:11]),
                    reads=(BT("ps1"), BT("tiny")), writes=(probs_r,), osz=8)
                P.add("V", lambda v, i=i: v.reduce_sum(tiny[:, 11:12],
                      probs_all[:, i * 8:(i + 1) * 8], AX.X),
                      reads=(probs_r,), writes=(BT("tiny"),))
                P.add("V", lambda v: v.reciprocal(tiny[:, 12:13], tiny[:, 11:12]),
                      reads=(BT("tiny"),), writes=(BT("tiny"),))
                P.add("V", lambda v, i=i: v.tensor_scalar(
                    probs_all[:, i * 8:(i + 1) * 8], probs_all[:, i * 8:(i + 1) * 8],
                    tiny[:, 12:13], None, ALU.mult),
                    reads=(probs_r, BT("tiny")), writes=(probs_r,), osz=8)
                if i % 16 == 15:
                    i0 = i - 15
                    P.add("T", lambda t, i0=i0: t.transpose(
                        PSP[2][:, 0:128], probs_all[:, i0 * 8:(i0 + 16) * 8],
                        idn_f32[:, :]),
                        reads=(probs_r, BT("idnf")), writes=(BT("ps2"),), osz=16384)
                    P.add("V", lambda v: v.tensor_copy(p8s[:, :], PSP[2][:, 0:128]),
                          reads=(BT("ps2"),), writes=(BT("p8s"),), osz=16384)
                    P.add("G", lambda g, i0=i0: g.dma_start(
                        out=bass.AP(prbT.tensor, i0 * 128,
                                    [[128, 16], [T, 8], [1, 128]]),
                        in_=p8s[:, :]),
                        dma=True, reads=(BT("p8s"),), writes=(BT("prbT"),), tag="prbst")
            if phase == "m1":
                dma("Y", sb["dbgt2"][3:4, 0:64], lnx_ext[0:1, 0:64],
                    reads=(BT("lnx0"),), writes=(BT("dbgt2pre"),), tag="dbgpre")
            cc = P.add("G", lambda g: g.collective_compute(
                "AllGather", ALU.bypass, [list(range(NCORES))],
                [prbT.opt()], [prbA.opt()]),
                reads=(BT("prbT"),), writes=(BT("prbA"),), coll=True)
            for e in range(E):
                P.add("Y", lambda y, e=e: y.dma_start(
                    out=pbis[e * 16:(e + 1) * 16, :],
                    in_=bass.AP(prbA.tensor, e * T,
                                [[E * T, 8], [T // 2, 2], [1, T // 2]])),
                    dma=True, reads=(BT("prbA"),), writes=(pbis_r,), tag="pbld")
            P.add("V", lambda v: v.memset(mid128[:, :], 0.5), writes=(BT("mid128"),))
            P.add("V", lambda v: v.memset(lo8[:, 0:1], 0.0), writes=(BT("lo8"),))
            P.add("V", lambda v: v.memset(lo8[:, 1:2], 1.0), writes=(BT("lo8"),))
            P.add("V", lambda v: v.memset(lo8[:, 2:3], 0.5), writes=(BT("lo8"),))
            for it in range(BISECT_ITERS):
                msrc = mid128[:, 0:1] if it == 0 else PSP[3][:, 0:1]
                mb = BT("mid128") if it == 0 else BT("ps3")
                P.add("V", lambda v, msrc=msrc: v.tensor_scalar(
                    bisj[:, :], pbis[:, :], msrc, 0.0, ALU.is_ge, op1=ALU.add,
                    accum_out=tiny[:, 8:9]),
                    reads=(pbis_r, mb), writes=(bisj_r, BT("tiny")), osz=4096)
                P.add("T", lambda t: t.matmul(PSP[1][0:8, 0:1], cons[:, 0:8],
                      tiny[:, 8:9], start=True, stop=True),
                      reads=(BT("tiny"), BT("cons")), writes=(BT("ps1"),))
                P.add("V", lambda v: v.tensor_scalar(ge8[:, 0:1], PSP[1][0:8, 0:1],
                      float(E * CEXP * 0 + 8192), None, ALU.is_ge),
                      reads=(BT("ps1"),), writes=(BT("ge8"),))
                P.add("V", lambda v: v.tensor_tensor(lo8[:, 3:4], lo8[:, 2:3],
                      lo8[:, 0:1], ALU.subtract), reads=(BT("lo8"),), writes=(BT("lo8"),))
                P.add("V", lambda v: v.scalar_tensor_tensor(lo8[:, 0:1], ge8[:, 0:1],
                      lo8[:, 3:4], lo8[:, 0:1], ALU.mult, ALU.add),
                      reads=(BT("ge8"), BT("lo8")), writes=(BT("lo8"),))
                P.add("V", lambda v: v.tensor_scalar(ge8[:, 1:2], ge8[:, 0:1],
                      -1.0, 1.0, ALU.mult, op1=ALU.add),
                      reads=(BT("ge8"),), writes=(BT("ge8"),))
                P.add("V", lambda v: v.tensor_tensor(lo8[:, 3:4], lo8[:, 2:3],
                      lo8[:, 1:2], ALU.subtract), reads=(BT("lo8"),), writes=(BT("lo8"),))
                P.add("V", lambda v: v.scalar_tensor_tensor(lo8[:, 1:2], ge8[:, 1:2],
                      lo8[:, 3:4], lo8[:, 1:2], ALU.mult, ALU.add),
                      reads=(BT("ge8"), BT("lo8")), writes=(BT("lo8"),))
                P.add("V", lambda v: v.tensor_tensor(lo8[:, 3:4], lo8[:, 0:1],
                      lo8[:, 1:2], ALU.add), reads=(BT("lo8"),), writes=(BT("lo8"),))
                P.add("V", lambda v: v.tensor_scalar(lo8[:, 2:3], lo8[:, 3:4],
                      0.5, None, ALU.mult), reads=(BT("lo8"),), writes=(BT("lo8"),))
                P.add("T", lambda t: t.matmul(PSP[3][:, 0:1], gmt[:, :], lo8[:, 2:3],
                      start=True, stop=True),
                      reads=(BT("gmt"), BT("lo8")), writes=(BT("ps3"),))
            P.add("T", lambda t: t.transpose(PSP[2][0:1, 0:8], lo8[0:8, 0:1],
                  idn_f32[0:8, 0:8]), reads=(BT("lo8"), BT("idnf")), writes=(BT("ps2"),))
            P.add("V", lambda v: v.tensor_copy(tiny2[0:1, 0:8], PSP[2][0:1, 0:8]),
                  reads=(BT("ps2"),), writes=(BT("tiny2"),))
            P.add("T", lambda t: t.matmul(PSP[3][:, 0:8], onesr[0:1, :],
                  tiny2[0:1, 0:8], start=True, stop=True),
                  reads=(BT("tiny2"), BT("ones")), writes=(BT("ps3"),))
            P.add("V", lambda v: v.tensor_copy(thrB[:, :], PSP[3][:, 0:8]),
                  reads=(BT("ps3"),), writes=(BT("thrB"),))
            P.add("V", lambda v: v.memset(bpm[:, :], -1e30), writes=(bpm_r,))
            P.add("V", lambda v: v.memset(bem[:, :], 0.0), writes=(bem_r,))
            P.add("V", lambda v: v.memset(bpu[:, :], -1e30), writes=(bpu_r,))
            P.add("V", lambda v: v.memset(beu[:, :], 0.0), writes=(beu_r,))
            pav = probs_all.rearrange("p (t e) -> p t e", e=E)
            for e in range(E):
                pe = pav[:, :, e]
                P.add("V", lambda v, pe=pe, e=e: v.tensor_scalar(
                    sc1[:, 0:64], pe, thrB[:, e:e + 1], 1e30, ALU.is_ge, op1=ALU.mult),
                    reads=(probs_r, BT("thrB")), writes=(sc1_r,), osz=64)
                P.add("V", lambda v, pe=pe: v.scalar_tensor_tensor(
                    sc1[:, 0:64], sc1[:, 0:64], -1e30, pe, ALU.add, ALU.add),
                    reads=(sc1_r, probs_r), writes=(sc1_r,), osz=64)
                for (bp, bp_r, be, be_r, val, val_r) in (
                        (bpm, bpm_r, bem, bem_r, sc1[:, 0:64], sc1_r),
                        (bpu, bpu_r, beu, beu_r, pe, probs_r)):
                    P.add("V", lambda v, val=val, bp=bp: v.tensor_tensor(
                        sc2[:, 0:64], val, bp[:, 0:64], ALU.is_gt),
                        reads=(val_r, bp_r), writes=(sc2_r,), osz=64)
                    P.add("V", lambda v, val=val, bp=bp: v.tensor_tensor(
                        bp[:, 0:64], bp[:, 0:64], val, ALU.max),
                        reads=(bp_r, val_r), writes=(bp_r,), osz=64)
                    P.add("V", lambda v, be=be, e=e: v.tensor_scalar(
                        sc3[:, 0:64], be[:, 0:64], -1.0, float(e), ALU.mult, op1=ALU.add),
                        reads=(be_r,), writes=(sc3_r,), osz=64)
                    P.add("V", lambda v: v.tensor_tensor(sc3[:, 0:64], sc2[:, 0:64],
                          sc3[:, 0:64], ALU.mult), reads=(sc2_r, sc3_r),
                          writes=(sc3_r,), osz=64)
                    P.add("V", lambda v, be=be: v.tensor_tensor(be[:, 0:64], be[:, 0:64],
                          sc3[:, 0:64], ALU.add), reads=(be_r, sc3_r),
                          writes=(be_r,), osz=64)
            P.add("V", lambda v: v.tensor_scalar(sc1[:, 0:64], bpm[:, 0:64],
                  -1e29, None, ALU.is_gt), reads=(bpm_r,), writes=(sc1_r,), osz=64)
            for (dst, dst_r, a, a_r, b_, b_r) in (
                    (gate_all, gate_r, bpm, bpm_r, bpu, bpu_r),
                    (dest_all, dest_r, bem, bem_r, beu, beu_r)):
                P.add("V", lambda v, a=a, b_=b_: v.tensor_tensor(sc2[:, 0:64],
                      a[:, 0:64], b_[:, 0:64], ALU.subtract),
                      reads=(a_r, b_r), writes=(sc2_r,), osz=64)
                P.add("V", lambda v: v.tensor_tensor(sc2[:, 0:64], sc1[:, 0:64],
                      sc2[:, 0:64], ALU.mult), reads=(sc1_r, sc2_r),
                      writes=(sc2_r,), osz=64)
                P.add("V", lambda v, dst=dst, b_=b_: v.tensor_tensor(dst[:, 0:64],
                      b_[:, 0:64], sc2[:, 0:64], ALU.add),
                      reads=(b_r, sc2_r), writes=(dst_r,), osz=64)
            P.add("V", lambda v: v.memset(rnk[:, 0:64], 0.0), writes=(rnk_r,))
            for e in range(E):
                P.add("V", lambda v, e=e: v.tensor_scalar(meq[:, 0:64],
                      dest_all[:, 0:64], float(e), None, ALU.is_equal),
                      reads=(dest_r,), writes=(meq_r,), osz=64)
                P.add("T", lambda t: t.matmul(PSP[0][:, 0:64], utri[:, :],
                      meq[:, 0:64], start=True, stop=True),
                      reads=(meq_r, BT("utri")), writes=(BT("ps0"),), osz=64)
                P.add("T", lambda t: t.matmul(PSP[2][0:1, 0:64], cons[:, 16:17],
                      meq[:, 0:64], start=True, stop=True),
                      reads=(meq_r, BT("cons")), writes=(BT("ps2"),), osz=64)
                P.add("V", lambda v: v.tensor_copy(sc2[0:1, 0:64], PSP[2][0:1, 0:64]),
                      reads=(BT("ps2"),), writes=(sc2_r,), osz=64)
                P.add("T", lambda t: t.transpose(PSP[2][0:64, 0:1], sc2[0:1, 0:64],
                      idn_f32[0:1, 0:1]), reads=(sc2_r, BT("idnf")),
                      writes=(BT("ps2"),), osz=64)
                P.add("V", lambda v: v.tensor_copy(p8s[0:64, 0:1], PSP[2][0:64, 0:1]),
                      reads=(BT("ps2"),), writes=(BT("p8s"),), osz=64)
                P.add("T", lambda t: t.matmul(PSP[2][0:64, 1:2], utri64[:, :],
                      p8s[0:64, 0:1], start=True, stop=True),
                      reads=(BT("p8s"), BT("utri64")), writes=(BT("ps2"),), osz=64)
                P.add("V", lambda v: v.tensor_copy(p8s[0:64, 1:2], PSP[2][0:64, 1:2]),
                      reads=(BT("ps2"),), writes=(BT("p8s"),), osz=64)
                P.add("T", lambda t: t.transpose(PSP[2][0:1, 8:72], p8s[0:64, 1:2],
                      idn_f32[0:64, 0:64]), reads=(BT("p8s"), BT("idnf")),
                      writes=(BT("ps2"),), osz=64)
                P.add("V", lambda v: v.tensor_copy(sc2[0:1, 0:64], PSP[2][0:1, 8:72]),
                      reads=(BT("ps2"),), writes=(sc2_r,), osz=64)
                P.add("T", lambda t: t.matmul(PSP[0][:, 0:64], onesr[0:1, :],
                      sc2[0:1, 0:64], start=False, stop=True),
                      reads=(sc2_r, BT("ones")), writes=(BT("ps0"),), osz=64)
                P.add("V", lambda v: v.tensor_tensor(sc3[:, 0:64], meq[:, 0:64],
                      PSP[0][:, 0:64], ALU.mult), reads=(meq_r, BT("ps0")),
                      writes=(sc3_r,), osz=64)
                P.add("V", lambda v: v.tensor_tensor(rnk[:, 0:64], rnk[:, 0:64],
                      sc3[:, 0:64], ALU.add), reads=(rnk_r, sc3_r),
                      writes=(rnk_r,), osz=64)
            P.add("V", lambda v: v.scalar_tensor_tensor(dest_all[:, 0:64],
                  dest_all[:, 0:64], float(CEXP), rnk[:, 0:64], ALU.mult, ALU.add),
                  reads=(dest_r, rnk_r), writes=(dest_r,), osz=64)
            P.add("V", lambda v: v.tensor_scalar(sc1[:, 0:64], rnk[:, 0:64],
                  float(CEXP), None, ALU.is_ge), reads=(rnk_r,), writes=(sc1_r,), osz=64)
            P.add("V", lambda v: v.tensor_scalar(sc2[:, 0:64], sc1[:, 0:64],
                  -1.0, 1.0, ALU.mult, op1=ALU.add), reads=(sc1_r,), writes=(sc2_r,), osz=64)
            P.add("V", lambda v: v.tensor_tensor(dest_all[:, 0:64], dest_all[:, 0:64],
                  sc2[:, 0:64], ALU.mult), reads=(dest_r, sc2_r), writes=(dest_r,), osz=64)
            P.add("V", lambda v: v.scalar_tensor_tensor(dest_all[:, 0:64], sc1[:, 0:64],
                  DUMP, dest_all[:, 0:64], ALU.mult, ALU.add),
                  reads=(sc1_r, dest_r), writes=(dest_r,), osz=64)
            if phase == "m1":
                P.add("Y", lambda y: y.dma_start(
                    out=dbg[4:5, 96:160], in_=lnx_ext[0:1, 0:64]),
                    dma=True, deps=(P.ops[-1],), reads=(BT("lnx0"), BT("thrB")),
                    tag="dbgmid")
            for i in range(NT):
                dma("Y", xls[:, 0:D], lnx_ext[i * 128:(i + 1) * 128, 0:D],
                    deps=[o for o in bufs[f"lnx{i}"].rdeps()],
                    reads=(BT(f"lnx{i}"),), writes=(BT("xls"),), tag="msc_ld")
                P.add("V", lambda v, i=i: v.tensor_copy(
                    xls[:, D:D + 1], gate_all[:, i:i + 1]),
                    reads=(gate_r,), writes=(BT("xls"),), osz=64)
                di = dI[i % 2]
                P.add("V", lambda v, di=di, i=i: v.tensor_copy(
                    di[:, 0:1], dest_all[:, i:i + 1]),
                    reads=(dest_r,), writes=(BT(f"dI{i % 2}"),), osz=64)
                P.add("G", lambda g, di=di: g.indirect_dma_start(
                    xsort_d[:, :], IndirectOffsetOnAxis(di[:, 0:1], 0),
                    xls[:, :], None),
                    dma=True, reads=(BT("xls"), BT(f"dI{i % 2}"), BT("xs_z")),
                    writes=(BT("xsort"),), tag="mscat")
            xl = [AR(0, 2176, F32), AR(4 * 2048 + 2048, 2176, F32)]
            xeg = [AR(2 * 2048, 2048), AR(3 * 2048, 2048)]
            w1c = [AR(6 * 2048, 1024), AR(6 * 2048 + 1024, 1024)]
            w2c = [AR(7 * 2048, 1024), AR(7 * 2048 + 1024, 1024)]
            hb = [AR(8 * 2048, 256), AR(8 * 2048 + 256, 256)]
            ysr = [AR(9 * 2048, 1024), AR(9 * 2048 + 1024, 1024)]
            NGRP = CEXP // 256
            for e in range(E):
                for grp in range(NGRP):
                    xg, xg_r = xeg[grp % 2]
                    xgv = xg.rearrange("p (a b) -> p a b", a=DC)
                    gs, gs_r = gsort[grp % 2], BT(f"gsort{grp % 2}")
                    for t2 in range(2):
                        slot = e * CEXP + grp * 256 + t2 * 128
                        xla, xla_r = xl[t2]
                        dma("Y", xla[:, :], xsort_d[slot:slot + 128, :],
                            deps=[o for o in bufs["xsort"].rdeps()],
                            reads=(BT("xsort"),), writes=(xla_r,), tag="mxld")
                        P.add("V", lambda v, gs=gs, xla=xla, t2=t2: v.tensor_copy(
                            gs[:, t2:t2 + 1], xla[:, D:D + 1]),
                            reads=(xla_r,), writes=(gs_r,), osz=64)
                        for dc in range(DC):
                            pst, bpst = (psT, BT("ps2")) if dc % 2 == 0 else (psU, BT("ps3"))
                            P.add("T", lambda t, dc=dc, pst=pst, xla=xla: t.transpose(
                                pst[:, :], xla[:, dc * 128:(dc + 1) * 128], idn_f32[:, :]),
                                reads=(xla_r, BT("idnf")), writes=(bpst,), osz=16384)
                            P.add("V" if dc % 2 == 0 else "S",
                                  (lambda v, dc=dc, pst=pst, xgv=xgv, t2=t2: v.tensor_copy(
                                      xgv[:, dc, t2 * 128:(t2 + 1) * 128], pst[:, :]))
                                  if dc % 2 == 0 else
                                  (lambda s, dc=dc, pst=pst, xgv=xgv, t2=t2: s.copy(
                                      xgv[:, dc, t2 * 128:(t2 + 1) * 128], pst[:, :])),
                                  reads=(bpst,), writes=(xg_r,), osz=16384)
                    for fc in range(FC + 1):
                        if fc < FC:
                            wa, wa_r = w1c[fc % 2]
                            wav = wa.rearrange("p (a b) -> p a b", a=DC)
                            wb, wb_r = w2c[fc % 2]
                            kwc = e * FC + fc
                            P.add("G", lambda g, wa=wa, kwc=kwc: g.dma_start(
                                out=wa[:, :], in_=w1bf_d[kwc * 128:(kwc + 1) * 128, :]),
                                dma=True, reads=(BT("w1bf"),), writes=(wa_r,), tag="w1ld")
                            P.add("G", lambda g, wb=wb, kwc=kwc: g.dma_start(
                                out=wb[:, :], in_=w2bf_d[kwc * 128:(kwc + 1) * 128, :]),
                                dma=True, reads=(BT("w2bf"),), writes=(wb_r,), tag="w2ld")
                            ph = "ab"[fc % 2]
                            psl = PSP[0][:, (fc % 2) * 512:(fc % 2) * 512 + 256]
                            for dc in range(DC):
                                P.add("T", lambda t, dc=dc, wav=wav, xgv=xgv, psl=psl: t.matmul(
                                    psl, wav[:, dc, :], xgv[:, dc, :],
                                    start=(dc == 0), stop=(dc == DC - 1)),
                                    reads=(wa_r, xg_r), writes=(BT(f"ps0{ph}"),), osz=256)
                        if fc >= 1:
                            pwb, pwb_r = w2c[(fc - 1) % 2]
                            phb, phb_r = hb[(fc - 1) % 2]
                            for t2 in range(2):
                                for nh_ in range(2):
                                    P.add("T", lambda t, phb=phb, pwb=pwb, t2=t2, nh_=nh_, fc=fc: t.matmul(
                                        PSP[2 + t2][:, nh_ * 512:(nh_ + 1) * 512],
                                        phb[:, t2 * 128:(t2 + 1) * 128],
                                        pwb[:, nh_ * 512:(nh_ + 1) * 512],
                                        start=(fc - 1 == 0), stop=(fc - 1 == FC - 1)),
                                        reads=(phb_r, pwb_r), writes=(BT(f"ps{2 + t2}"),), osz=512)
                        if fc < FC:
                            hbb, hb_r = hb[fc % 2]
                            P.add("S", lambda s, hbb=hbb, psl=psl: s.activation(
                                hbb[:, :], psl, AF.Gelu),
                                reads=(BT(f"ps0{ph}"),), writes=(hb_r,), osz=256)
                    for t2 in range(2):
                        yy, yy_r = ysr[t2]
                        P.add("V", lambda v, yy=yy, gs=gs, t2=t2: v.tensor_scalar(
                            yy[:, :], PSP[2 + t2][:, :], gs[:, t2:t2 + 1], None, ALU.mult),
                            reads=(BT(f"ps{2 + t2}"), gs_r), writes=(yy_r,), osz=1024)
                        slot = e * CEXP + grp * 256 + t2 * 128
                        P.add("Y", lambda y, yy=yy, slot=slot: y.dma_start(
                            out=ysort_d[slot:slot + 128, :], in_=yy[:, :]),
                            dma=True, reads=(yy_r,), writes=(BT("ysort"),), tag="myst")
            for i in range(NT):
                sl, bsl = xt[i % 2], BT(f"xt{i % 2}")
                dma("Y", sl[:, :], xres[i * 128:(i + 1) * 128, :],
                    reads=(xres_b[i],), writes=(bsl,), tag="mrld")
                di = dI[i % 2]
                P.add("V", lambda v, di=di, i=i: v.tensor_copy(
                    di[:, 0:1], dest_all[:, i:i + 1]),
                    reads=(dest_r,), writes=(BT(f"dI{i % 2}"),), osz=64)
                yb, byb = ybf[i % 2], BT(f"ybf{i % 2}")
                P.add("G", lambda g, yb=yb, di=di: g.indirect_dma_start(
                    yb[:, :], None, ysort_d[:, :],
                    IndirectOffsetOnAxis(di[:, 0:1], 0)),
                    dma=True, reads=(BT(f"dI{i % 2}"), BT("ysort"), BT("ys_z")),
                    writes=(byb,), tag="mgat")
                xn, bxn = xnew[i % 2], BT(f"xnew{i % 2}")
                P.add("V", lambda v, yb=yb, sl=sl, xn=xn: v.scalar_tensor_tensor(
                    xn[:, :], yb[:, :], 0.5, sl[:, :], ALU.mult, ALU.add),
                    reads=(byb, bsl), writes=(bxn,), osz=D)
                dma("Y", xres[i * 128:(i + 1) * 128, :], xn[:, :],
                    reads=(bxn,), writes=(xres_b[i],), tag="mrst")

        def attn_phase():
            prep_wT(wT["q"], w["l1a_wq"], "wq")
            prep_wT(wT["k"], w["l1a_wk"], "wk")
            prep_wT(wT["v"], w["l1a_wv"], "wv")
            dma("Y", rtc[:, :], rt_c, writes=(BT("rtc"),))
            dma("Y", ropes[:, 0:S], rope_c[:, 0:S], writes=(ropes_r,))
            dma("Y", ropes[:, S:2 * S], rope_c[:, 2 * S:3 * S], writes=(ropes_r,))
            for ch in range(4):
                lodma = dma("Y", lnt[0][:, :],
                            rope_c[:, S + (ch % 2) * 1024 + (ch // 2) * 2 * S:
                                   S + (ch % 2) * 1024 + (ch // 2) * 2 * S + 1024],
                            writes=(BT("lnt0"),), tag="ropelo")
                P.add("V", lambda v, ch=ch: v.tensor_tensor(
                    ropes[:, ch * 1024:(ch + 1) * 1024],
                    ropes[:, ch * 1024:(ch + 1) * 1024], lnt[0][:, :], ALU.add),
                    reads=(ropes_r, BT("lnt0")), writes=(ropes_r,), osz=1024)
            for j, nm in ((0, "l1a_bq"), (1, "l1a_bk")):
                P.add("Y", lambda y, j=j, nm=nm: y.dma_start(
                    out=bqk[:, j * 8:(j + 1) * 8],
                    in_=bass.AP(w[nm].tensor, 0, [[1, 128], [128, 8]]),
                    allow_slow_non_contiguous=True),
                    dma=True, writes=(BT("bqk"),))
            for nm, dstt, bb in (("l1a_bv", bv128, "bv128"), ("l1a_bo", bo128, "bo128")):
                ldb = dma("G", ropt2[0][0:1, 0:D], w[nm], writes=(BT("ropt2_0"),))
                for hb in range(2):
                    P.add("T", lambda t, hb=hb: t.matmul(
                        psB[0][:, hb * 512:(hb + 1) * 512], onesr[0:1, :],
                        ropt2[0][0:1, hb * 512:hb * 512 + 512], start=True, stop=True),
                        reads=(BT("ropt2_0"), BT("ones")), writes=(BT("ps0"),), osz=512)
                P.add("V", lambda v, dstt=dstt: v.tensor_copy(dstt[:, :], psB[0][:, 0:D]),
                      reads=(BT("ps0"),), writes=(BT(bb),), osz=D)
            ln_stats_pass("attn", with_absmax=False)
            for i in range(NT):
                sl, bsl = xt[i % 2], BT(f"xt{i % 2}")
                lb, blb = ybf[i % 2], BT(f"ybf{i % 2}")
                dma("Y", sl[:, :], xres[i * 128:(i + 1) * 128, :],
                    reads=(xres_b[i],), writes=(bsl,), tag="attnld")
                P.add("S", lambda s, sl=sl, lb=lb, i=i: s.activation(
                    lb[:, :], sl[:, :], AF.Identity,
                    bias=stats[:, 2 * i + 1:2 * i + 2], scale=stats[:, 2 * i:2 * i + 1]),
                    reads=(bsl, BT(f"st{i}")), writes=(blb,), osz=D)
                dma("Y", lnbf[i * 128:(i + 1) * 128, :], lb[:, :],
                    reads=(blb,), writes=(BT(f"lnbf{i}"),), tag="attnst")
            NBLK = T // 1024
            for blk in range(NBLK):
                lt, blt = lnaT[0], lnaT_r
                lnbf_deps = [o for i in range(blk * 8, blk * 8 + 8)
                             for o in bufs[f"lnbf{i}"].rdeps()]
                for dc in range(DC):
                    P.add("Y", lambda y, blk=blk, dc=dc, lt=lt: y.dma_start(
                        out=lt[:, dc, :],
                        in_=lnbf[blk * 1024:(blk + 1) * 1024, dc * 128:(dc + 1) * 128],
                        transpose=True),
                        dma=True, writes=(blt,), deps=lnbf_deps, tag="lnaTld")
                cb = (blk % 2) * 1024
                for j, (nm, rowbase) in enumerate((("q", 0), ("k", D))):
                    for ic in range(DC):
                        ps, bps = psB[ic % 2], BT(f"ps{ic % 2}")
                        for dc in range(DC):
                            for nh_ in range(2):
                                P.add("T", lambda t, nm=nm, ic=ic, dc=dc, ps=ps, lt=lt, nh_=nh_: t.matmul(
                                    ps[:, nh_ * 512:(nh_ + 1) * 512],
                                    wT[nm][:, dc * D + ic * 128: dc * D + ic * 128 + 128],
                                    lt[:, dc, nh_ * 512:(nh_ + 1) * 512],
                                    start=(dc == 0), stop=(dc == DC - 1)),
                                    reads=(blt, BT(f"w{nm}_w")), writes=(bps,), osz=512)
                        qs, bqs = qstg[ic % 2], BT(f"qstg{ic % 2}")
                        P.add("V", lambda v, ps=ps, qs=qs, j=j, ic=ic: v.tensor_scalar(
                            qs[:, :], ps[:, :], bqk[:, j * 8 + ic:j * 8 + ic + 1],
                            None, ALU.add),
                            reads=(bps, BT("bqk")), writes=(bqs,), osz=1024)
                        for nh_ in range(2):
                            P.add("T", lambda t, qs=qs, nh_=nh_: t.matmul(
                                PSP[2][:, nh_ * 512:(nh_ + 1) * 512], rtc[:, :],
                                qs[:, nh_ * 512:(nh_ + 1) * 512], start=True, stop=True),
                                reads=(bqs, BT("rtc")), writes=(BT("ps2"),), osz=512)
                        t1, bt1 = ropt1[0], BT("ropt1_0")
                        t2, bt2 = ropt2[0], BT("ropt2_0")
                        P.add("V", lambda v, qs=qs, t1=t1, cb=cb: v.tensor_tensor(
                            t1[:, :], qs[:, :], ropes[:, cb:cb + 1024], ALU.mult),
                            reads=(bqs, ropes_r), writes=(bt1,), osz=1024)
                        P.add("V", lambda v, t2=t2, cb=cb: v.tensor_tensor(
                            t2[:, :], PSP[2][:, :], ropes[:, S + cb:S + cb + 1024],
                            ALU.mult), reads=(BT("ps2"), ropes_r), writes=(bt2,), osz=1024)
                        qr, bqr = qrt[ic % 2], BT(f"qrt{ic % 2}")
                        P.add("V", lambda v, qr=qr, t1=t1, t2=t2: v.tensor_tensor(
                            qr[:, :], t1[:, :], t2[:, :], ALU.add),
                            reads=(bt1, bt2), writes=(bqr,), osz=1024)
                        P.add("Y", lambda y, qr=qr, rowbase=rowbase, ic=ic, blk=blk: y.dma_start(
                            out=qkT[rowbase + ic * 128: rowbase + (ic + 1) * 128,
                                    blk * 1024:(blk + 1) * 1024], in_=qr[:, :]),
                            dma=True, reads=(bqr,), writes=(BT(f"qk{j}_{ic}_{blk}"),),
                            tag="qkst")
                for tt in range(8):
                    i = blk * 8 + tt
                    ps, bps = psB[tt % 2], BT(f"ps{tt % 2}")
                    for dc in range(DC):
                        for nh_ in range(2):
                            P.add("T", lambda t, dc=dc, tt=tt, ps=ps, lt=lt, nh_=nh_: t.matmul(
                                ps[:, nh_ * 512:(nh_ + 1) * 512],
                                lt[:, dc, tt * 128:(tt + 1) * 128],
                                wT["v"][:, dc * D + nh_ * 512: dc * D + nh_ * 512 + 512],
                                start=(dc == 0), stop=(dc == DC - 1)),
                                reads=(blt, BT("wv_w")), writes=(bps,), osz=512)
                    vb, bvb = qrt[tt % 2], BT(f"qrt{tt % 2}")
                    P.add("V", lambda v, ps=ps, vb=vb: v.tensor_tensor(
                        vb[:, :], ps[:, :], bv128[:, :], ALU.add),
                        reads=(bps, BT("bv128")), writes=(bvb,), osz=1024)
                    P.add("Y", lambda y, vb=vb, i=i: y.dma_start(
                        out=vtok[i * 128:(i + 1) * 128, :], in_=vb[:, :]),
                        dma=True, reads=(bvb,), writes=(BT(f"vtok{i}"),), tag="vst")
            prep_wT(wT["o"], w["l1a_wo"], "wo", extra=(BT("wq_w"),))
            for b in range(B):
                for h in range(NH):
                    sl = (b * NH + h) % 2
                    qt, bqt = qtb[0], BT("qtb0")
                    kt, bkt = ktb[0], BT("ktb0")
                    vt, bvt = vtb[0], BT("vtb0")
                    P.add("Y", lambda y, qt=qt, h=h, b=b: y.dma_start(
                        out=qt[:, :], in_=qkT[h * 64:(h + 1) * 64, b * S:(b + 1) * S]),
                        dma=True,
                        deps=[o for ic in (h // 2,) for blkk in range(b * 2, b * 2 + 2)
                              for o in bufs[f"qk0_{ic}_{blkk}"].rdeps()],
                        writes=(bqt,), tag="qld")
                    P.add("Y", lambda y, kt=kt, h=h, b=b: y.dma_start(
                        out=kt[:, :], in_=qkT[D + h * 64: D + h * 64 + 64, b * S:(b + 1) * S]),
                        dma=True,
                        deps=[o for ic in (h // 2,) for blkk in range(b * 2, b * 2 + 2)
                              for o in bufs[f"qk1_{ic}_{blkk}"].rdeps()],
                        writes=(bkt,), tag="kld")
                    P.add("V", lambda v, vt=vt: v.memset(vt[:, :, :], 1.0),
                          writes=(bvt,), osz=1040)
                    P.add("Y", lambda y, vt=vt, h=h, b=b: y.dma_start(
                        out=vt[:, :, 0:64],
                        in_=vtok[b * S:(b + 1) * S, h * 64:(h + 1) * 64]),
                        dma=True,
                        deps=[o for i in range(b * 16, b * 16 + 16)
                              for o in bufs[f"vtok{i}"].rdeps()],
                        reads=(bvt,), writes=(bvt,), tag="vld")
                    for ktile in range(16):
                        pt = ptile[0]
                        for qh in range(2):
                            for nh_ in range(2):
                                P.add("T", lambda t, kt=kt, qt=qt, ktile=ktile, qh=qh, nh_=nh_: t.matmul(
                                    PSP[qh][:, nh_ * 512:(nh_ + 1) * 512],
                                    kt[:, ktile * 128:(ktile + 1) * 128],
                                    qt[:, qh * 1024 + nh_ * 512: qh * 1024 + nh_ * 512 + 512],
                                    start=True, stop=True),
                                    reads=(bkt, bqt), writes=(BT(f"ps{qh}"),), osz=512)
                            P.add("S", lambda s, pt=pt, qh=qh: s.activation(
                                pt[:, qh * 1024:(qh + 1) * 1024], PSP[qh][:, :],
                                AF.Exp, scale=0.125),
                                reads=(BT(f"ps{qh}"),), writes=(BT(f"pt{qh}"),), osz=1024)
                        for qh in range(2):
                            for nh_ in range(2):
                                P.add("T", lambda t, vt=vt, pt=pt, ktile=ktile, qh=qh, nh_=nh_: t.matmul(
                                    PSP[2 + qh][0:65, nh_ * 512:(nh_ + 1) * 512],
                                    vt[:, ktile, :],
                                    pt[:, qh * 1024 + nh_ * 512: qh * 1024 + nh_ * 512 + 512],
                                    start=(ktile == 0), stop=(ktile == 15)),
                                    reads=(bvt, BT(f"pt{qh}")), writes=(BT(f"ps{2 + qh}"),), osz=512)
                    och, boch = otch[h // 2], BT(f"ARr{h // 2}")
                    hp = (h % 2) * 64
                    for qh in range(2):
                        P.add("V", lambda v, qh=qh: v.tensor_copy(
                            qstg[qh][0:64, :], PSP[2 + qh][0:64, :]),
                            reads=(BT(f"ps{2 + qh}"),), writes=(BT(f"qstg{qh}"),), osz=1024)
                        rc = ropt1[0] if qh == 0 else ropt2[0]
                        brc = BT("ropt1_0") if qh == 0 else BT("ropt2_0")
                        P.add("V", lambda v, qh=qh, rc=rc: v.reciprocal(
                            rc[0:1, :], PSP[2 + qh][64:65, :]),
                            reads=(BT(f"ps{2 + qh}"),), writes=(brc,), osz=1024)
                        for nh_ in range(2):
                            P.add("T", lambda t, qh=qh, rc=rc, nh_=nh_: t.matmul(
                                PSP[qh][0:64, nh_ * 512:(nh_ + 1) * 512], onesr[0:1, 0:64],
                                rc[0:1, nh_ * 512:(nh_ + 1) * 512], start=True, stop=True),
                                reads=(brc, BT("ones")), writes=(BT(f"ps{qh}"),), osz=512)
                        P.add("V", lambda v, och=och, hp=hp, qh=qh: v.tensor_tensor(
                            och[hp:hp + 64, qh * 1024:(qh + 1) * 1024],
                            qstg[qh][0:64, :],
                            PSP[qh][0:64, :], ALU.mult),
                            reads=(BT(f"qstg{qh}"), BT(f"ps{qh}")), writes=(boch,), osz=1024)
                for tt in range(16):
                    i = b * 16 + tt
                    ps, bps = psB[tt % 2], BT(f"ps{tt % 2}")
                    for ic in range(DC):
                        for nh_ in range(2):
                            P.add("T", lambda t, ic=ic, tt=tt, ps=ps, nh_=nh_: t.matmul(
                                ps[:, nh_ * 512:(nh_ + 1) * 512],
                                otch[ic][:, tt * 128:(tt + 1) * 128],
                                wT["o"][:, ic * D + nh_ * 512: ic * D + nh_ * 512 + 512],
                                start=(ic == 0), stop=(ic == DC - 1)),
                                reads=(BT(f"ARr{ic}"), BT("wo_w")), writes=(bps,), osz=512)
                    sl2, bsl2 = xt[tt % 2], BT(f"xt{tt % 2}")
                    dma("Y", sl2[:, :], xres[i * 128:(i + 1) * 128, :],
                        reads=(xres_b[i],), writes=(bsl2,), tag="oxld")
                    yb, byb = ybf[tt % 2], BT(f"ybf{tt % 2}")
                    P.add("V", lambda v, ps=ps, yb=yb: v.tensor_tensor(
                        yb[:, :], ps[:, :], bo128[:, :], ALU.add),
                        reads=(bps, BT("bo128")), writes=(byb,), osz=D)
                    xn, bxn = xnew[tt % 2], BT(f"xnew{tt % 2}")
                    P.add("V", lambda v, yb=yb, sl2=sl2, xn=xn: v.tensor_tensor(
                        xn[:, :], yb[:, :], sl2[:, :], ALU.add),
                        reads=(byb, bsl2), writes=(bxn,), osz=D)
                    dma("Y", xres[i * 128:(i + 1) * 128, :], xn[:, :],
                        reads=(bxn,), writes=(xres_b[i],), tag="oxst")

        if "ssm" in enable:
            ssm_setup()
        if "moe1" in enable or "moe2" in enable:
            moe_setup()
        if "moe1" in enable:
            moe_phase("m1")
        if "ssm" in enable:
            ssm_phase()
        if "moe2" in enable:
            moe_phase("m2")
        if "fp8a" in enable:
            fp8_phase("fa")
        if "attn" in enable:
            attn_phase()
        if "fp8b" in enable:
            fp8_phase("fb")

        if "moe1" in enable or "moe2" in enable:
            pav_d, pav_dr = AR(14 * 2048, 1024, F32)
            gate_d, gate_dr = AR(14 * 2048 + 1024, 128, F32)
            dest_d, dest_dr = AR(14 * 2048 + 1152, 128, F32)
            bpm_d, bpm_dr = AR(15 * 2048, 128, F32)
            bem_d, bem_dr = AR(15 * 2048 + 128, 128, F32)
            bpu_d, bpu_dr = AR(15 * 2048 + 256, 128, F32)
            beu_d, beu_dr = AR(15 * 2048 + 384, 128, F32)
            rnk_d, rnk_dr = AR(15 * 2048 + 832, 128, F32)
            P.add("V", lambda v: v.tensor_copy(sb["dbgt"][:, 0:8], pav_d[:, 0:8]),
                  reads=(pav_dr,), writes=(BT("dbgt"),))
            P.add("V", lambda v: v.tensor_copy(sb["dbgt"][:, 8:16], thrB[:, :]),
                  reads=(BT("thrB"),), writes=(BT("dbgt"),))
            P.add("V", lambda v: v.tensor_copy(sb["dbgt"][:, 16:17], gate_d[:, 0:1]),
                  reads=(gate_dr,), writes=(BT("dbgt"),))
            P.add("V", lambda v: v.tensor_copy(sb["dbgt"][:, 17:18], dest_d[:, 0:1]),
                  reads=(dest_dr,), writes=(BT("dbgt"),))
            P.add("V", lambda v: v.tensor_copy(sb["dbgt"][:, 18:19], rnk_d[:, 0:1]),
                  reads=(rnk_dr,), writes=(BT("dbgt"),))
            P.add("V", lambda v: v.tensor_copy(sb["dbgt"][:, 19:20], bpm_d[:, 0:1]),
                  reads=(bpm_dr,), writes=(BT("dbgt"),))
            P.add("V", lambda v: v.tensor_copy(sb["dbgt"][:, 20:21], bem_d[:, 0:1]),
                  reads=(bem_dr,), writes=(BT("dbgt"),))
            P.add("V", lambda v: v.tensor_copy(sb["dbgt"][:, 21:22], bpu_d[:, 0:1]),
                  reads=(bpu_dr,), writes=(BT("dbgt"),))
            P.add("V", lambda v: v.tensor_copy(sb["dbgt"][:, 22:23], beu_d[:, 0:1]),
                  reads=(beu_dr,), writes=(BT("dbgt"),))
        dma("Y", dbg[0:1, 64:96], scrow[0:1, :], reads=(BT("wsc"), BT("xsc")), tag="dbgrow")
        if "moe1" in enable or "moe2" in enable:
            for row, (src, deps_) in enumerate((
                    (xsort_d[0:1, 0:64], ("xsort",)),
                    (lnx_ext[0:1, 0:64], ("lnx0",)),
            )):
                dma("Y", sb["dbgt2"][row:row + 1, 0:64], src,
                    reads=tuple(BT(k) for k in deps_), writes=(BT("dbgt2"),),
                    tag=f"dbgs{row}")
            P.add("G", lambda g: g.dma_start(
                out=sb["dbgt2"][2:3, 0:64], in_=ysort_d[0:1, 0:64]),
                dma=True, reads=(BT("ysort"),), writes=(BT("dbgt2"),), tag="dbgys")
            dma("Y", dbg[0:4, 96:160], sb["dbgt2"][0:4, 0:64],
                reads=(BT("dbgt2"), BT("dbgt2pre")), tag="dbgs_out")
        P.add("V", lambda v: v.tensor_copy(sb["dbgt"][:, 32:33], amx[:, 0:1]),
              reads=(BT("amx"),), writes=(BT("dbgt"),))
        P.add("V", lambda v: v.tensor_copy(sb["dbgt"][:, 33:37], stats[:, 0:4]),
              reads=(BT("st0"), BT("st1")), writes=(BT("dbgt"),))
        P.add("V", lambda v: v.tensor_copy(sb["dbgt"][:, 37:38], k128[:, 0:1]),
              reads=(BT("k4b"),), writes=(BT("dbgt"),))
        P.add("V", lambda v: v.tensor_copy(sb["dbgt"][:, 38:39], s2_128[:, 0:1]),
              reads=(BT("s2b"),), writes=(BT("dbgt"),))
        dma("Y", dbg[:, 0:64], sb["dbgt"][:, :], reads=(BT("dbgt"),), tag="dbgout")
        for i in range(NT):
            dma("Y", out[i * 128:(i + 1) * 128, :], xres[i * 128:(i + 1) * 128, :],
                reads=(xres_b[i],), tag="outcp")

        P.emit(block, sems, dsems)
    nc.has_collectives = True
    return nc


def make_rope_consts():
    i64 = np.arange(0, 64, 2, dtype=np.float64) / 64.0
    freqs = 10000.0 ** i64
    t = np.arange(S, dtype=np.float64)
    ang = t[:, None] / freqs[None, :]
    cos = np.cos(ang); sin = np.sin(ang)
    def expand(a):
        a64 = np.repeat(a, 2, axis=1).T
        return np.tile(a64, (2, 1))
    ch = expand(cos); sh = expand(sin)
    chi = ch.astype(np.float32); clo = (ch - chi).astype(np.float32)
    shi = sh.astype(np.float32); slo = (sh - shi).astype(np.float32)
    rope = np.concatenate([chi, clo, shi, slo], axis=1).astype(np.float32)
    rt = np.zeros((128, 128), np.float32)
    for i in range(64):
        rt[2 * i + 1, 2 * i] = -1.0
        rt[2 * i, 2 * i + 1] = 1.0
    return rope, rt.astype(ml_dtypes.bfloat16)


def make_consts():
    c = np.zeros((128, 64), np.float32)
    for p in range(128):
        c[p, p // 16] = 1.0
    c[:, 16] = 1.0
    c[:, 17] = np.arange(128)
    return c


_cache = {}
_LAST_DBG = None
_ROPE = None
_ALL_DBG = None


def get_program(enable):
    key = tuple(enable)
    if key not in _cache:
        nc = build_program(enable)
        lower_extended_insts(nc)
        _cache[key] = nc
    return _cache[key]


def _get_rope():
    global _ROPE
    if _ROPE is None:
        _ROPE = make_rope_consts()
    return _ROPE


def kernel(L0_attn_x, L0_attn_log_dt, L0_attn_log_A_real, L0_attn_A_imag,
           L0_attn_C, L0_attn_D, L0_ffn_Wg, L0_ffn_W1, L0_ffn_W2,
           L1_attn_wq, L1_attn_bq, L1_attn_wk, L1_attn_bk, L1_attn_wv,
           L1_attn_bv, L1_attn_wo, L1_attn_bo, L1_ffn_weight, L1_ffn_bias,
           _enable=("moe1", "ssm", "moe2", "fp8a", "attn", "fp8b")):
    _get_rope()
    nc = get_program(_enable)
    xf = np.ascontiguousarray(L0_attn_x.reshape(B_FULL * S, D))
    ident_bf = np.eye(128, dtype=ml_dtypes.bfloat16)
    ident_f32 = np.eye(128, dtype=np.float32)
    consts = make_consts()
    base = {
        "l0a_logdt": np.ascontiguousarray(L0_attn_log_dt.reshape(1, D), np.float32),
        "l0a_logA": np.ascontiguousarray(L0_attn_log_A_real, np.float32),
        "l0a_Aim": np.ascontiguousarray(L0_attn_A_imag, np.float32),
        "l0a_C": np.ascontiguousarray(L0_attn_C.reshape(D, 64), np.float32),
        "l0a_D": np.ascontiguousarray(L0_attn_D.reshape(1, D), np.float32),
        "l0f_Wg": np.ascontiguousarray(L0_ffn_Wg, np.float32),
        "l0f_W1": np.ascontiguousarray(L0_ffn_W1, np.float32),
        "l0f_W2": np.ascontiguousarray(L0_ffn_W2, np.float32),
        "l1a_wq": np.ascontiguousarray(L1_attn_wq, np.float32),
        "l1a_bq": np.ascontiguousarray(L1_attn_bq.reshape(1, D), np.float32),
        "l1a_wk": np.ascontiguousarray(L1_attn_wk, np.float32),
        "l1a_bk": np.ascontiguousarray(L1_attn_bk.reshape(1, D), np.float32),
        "l1a_wv": np.ascontiguousarray(L1_attn_wv, np.float32),
        "l1a_bv": np.ascontiguousarray(L1_attn_bv.reshape(1, D), np.float32),
        "l1a_wo": np.ascontiguousarray(L1_attn_wo, np.float32),
        "l1a_bo": np.ascontiguousarray(L1_attn_bo.reshape(1, D), np.float32),
        "l1f_w": np.ascontiguousarray(L1_ffn_weight, np.float32),
        "l1f_b": np.ascontiguousarray(L1_ffn_bias.reshape(1, D), np.float32),
        "ident_bf": ident_bf, "ident_f32": ident_f32, "consts": consts,
        "ones_row": np.ones((1, 128), np.float32),
        "rope_c": _ROPE[0], "rt_c": _ROPE[1],
        "gmat_t": np.repeat(np.eye(8, dtype=np.float32), 16, axis=1),
        "utri_c": (np.triu(np.ones((128, 128)), 1)).astype(np.float32),
        "utri64_c": (np.triu(np.ones((64, 64)), 1)).astype(np.float32),
    }
    in_maps = []
    for c in range(NCORES):
        m = dict(base)
        m["x_in"] = np.ascontiguousarray(xf[c * T:(c + 1) * T])
        in_maps.append(m)
    res = run_bass_kernel_spmd(nc, in_maps, core_ids=list(range(NCORES)))
    global _LAST_DBG, _ALL_DBG
    _LAST_DBG = res.results[0].get("dbg")
    _ALL_DBG = [r.get("dbg") for r in res.results]
    outs = [r["out"] for r in res.results]
    return np.concatenate(outs, 0).reshape(B_FULL, S, D)


# version 06
# speedup vs baseline: 1.0050x; 1.0050x over previous
import numpy as np
import ml_dtypes
from contextlib import ExitStack

import concourse.bass as bass
import concourse.mybir as mybir
from concourse.bass_utils import run_bass_kernel_spmd
from concourse.library_overlay import lower_extended_insts
from concourse.bass import IndirectOffsetOnAxis

F32 = mybir.dt.float32
BF16 = mybir.dt.bfloat16
FP8 = mybir.dt.float8e4
I32 = mybir.dt.int32
AF = mybir.ActivationFunctionType
ALU = mybir.AluOpType
AX = mybir.AxisListType

NCORES = 8
B_FULL, S, D = 32, 2048, 1024
B = B_FULL // NCORES
T = B * S
NT = T // 128
DC = D // 128
E = 8
CEXP = 1536
F_MOE = 2048
FC = F_MOE // 128
NH, DK = 16, 64
LN_EPS = 1e-5
BISECT_ITERS = 40

ENGINES = ("T", "V", "S", "G", "Y")


NDMASEM = 8


class Op:
    __slots__ = ("eng", "fn", "deps", "dma", "idx", "cum", "need_inc", "tag",
                 "coll", "osz", "epos", "slot")

    def __init__(self, eng, fn, deps, dma, idx, tag="", coll=False, osz=0):
        self.eng, self.fn, self.deps, self.dma, self.idx = eng, fn, list(deps), dma, idx
        self.cum, self.need_inc, self.tag, self.coll = 0, False, tag, coll
        self.osz, self.epos, self.slot = osz, 0, 0

    @property
    def asyn(self):
        return self.dma or self.coll

    @property
    def incval(self):
        return 16 if self.dma else 1


class Buf:
    def __init__(self):
        self.writers = []
        self.readers = []

    def rdeps(self):
        return list(self.writers)

    def wdeps(self):
        return self.writers + self.readers

    def did_read(self, op):
        self.readers.append(op)

    def did_write(self, op):
        self.writers.append(op)
        if len(self.writers) > 96:
            self.writers = self.writers[-96:]
        self.readers = []


class Prog:
    def __init__(self):
        self.ops = []

    @staticmethod
    def _flat(bs):
        out = []
        for b in bs:
            if isinstance(b, (tuple, list)):
                out.extend(b)
            else:
                out.append(b)
        return out

    def add(self, eng, fn, deps=(), dma=False, tag="", reads=(), writes=(),
            coll=False, osz=0):
        reads = self._flat(reads)
        writes = self._flat(writes)
        dd = list(deps)
        for b in reads:
            dd += b.rdeps()
        for b in writes:
            dd += b.wdeps()
        seen, deps2 = set(), []
        for d in dd:
            if d is not None and id(d) not in seen:
                seen.add(id(d))
                deps2.append(d)
        op = Op(eng, fn, deps2, dma, len(self.ops), tag, coll, osz)
        self.ops.append(op)
        for b in reads:
            b.did_read(op)
        for b in writes:
            b.did_write(op)
        return op

    def emit(self, block, sems, dsems):
        for op in self.ops:
            if op.asyn:
                op.need_inc = True
            for d in op.deps:
                if d.eng != op.eng or d.asyn:
                    d.need_inc = True
        per_eng = {e: [o for o in self.ops if o.eng == e] for e in ENGINES}
        for e in ENGINES:
            for i, o in enumerate(per_eng[e]):
                o.epos = i
        cnt = {e: 0 for e in ENGINES}
        dcnt = {e: [0] * NDMASEM for e in ENGINES}
        dseq = {e: 0 for e in ENGINES}
        for op in self.ops:
            if op.dma:
                op.slot = dseq[op.eng] % NDMASEM
                dseq[op.eng] += 1
                dcnt[op.eng][op.slot] += 16
                op.cum = dcnt[op.eng][op.slot]
            else:
                if op.need_inc:
                    cnt[op.eng] += op.incval
                op.cum = cnt[op.eng]
        for e in ENGINES:
            nd = [o for o in per_eng[e] if not o.dma]
            if nd and not nd[-1].need_inc:
                nd[-1].need_inc = True
                cnt[e] += nd[-1].incval
                nd[-1].cum = cnt[e]
        final = dict(cnt)

        def emit_stream(eng_obj, ename):
            waited = {e: 0 for e in ENGINES}
            dwaited = {e: [0] * NDMASEM for e in ENGINES}
            for op in per_eng[ename]:
                need_drain = False
                for d in op.deps:
                    if d.dma:
                        if d.cum > dwaited[d.eng][d.slot]:
                            eng_obj.wait_ge(dsems[d.eng][d.slot], d.cum)
                            dwaited[d.eng][d.slot] = d.cum
                    elif (d.eng != ename or d.asyn) and d.cum > waited[d.eng]:
                        eng_obj.wait_ge(sems[d.eng], d.cum)
                        waited[d.eng] = d.cum
                    if (d.eng == ename and not d.asyn
                            and op.epos - d.epos <= 2 and d.osz <= 256):
                        need_drain = True
                if need_drain and ename in ("V", "S", "G"):
                    eng_obj.drain()
                inst = op.fn(eng_obj)
                if op.need_inc:
                    sem = dsems[ename][op.slot] if op.dma else sems[ename]
                    if inst is None:
                        eng_obj.sem_inc(sem, op.incval)
                    else:
                        inst.then_inc(sem, op.incval)
            if ename == "Y":
                for e in ENGINES:
                    if final.get(e, 0) > waited.get(e, 0):
                        eng_obj.wait_ge(sems[e], final[e])
                    for sl in range(NDMASEM):
                        if dcnt[e][sl] > dwaited[e][sl]:
                            eng_obj.wait_ge(dsems[e][sl], dcnt[e][sl])

        @block.tensor
        def _(t):
            emit_stream(t, "T")

        @block.vector
        def _(v):
            emit_stream(v, "V")

        @block.scalar
        def _(s):
            emit_stream(s, "S")

        @block.gpsimd
        def _(g):
            emit_stream(g, "G")

        @block.sync
        def _(y):
            emit_stream(y, "Y")


def build_program(enable):
    nc = bass.Bass("TRN2", target_bir_lowering=False)
    P = Prog()

    x_in = nc.dram_tensor("x_in", [T, D], F32, kind="ExternalInput").ap()
    out = nc.dram_tensor("out", [T, D], F32, kind="ExternalOutput").ap()
    w = {}
    for name, shape in [
        ("l0a_logdt", [1, D]), ("l0a_logA", [D, 32]), ("l0a_Aim", [D, 32]),
        ("l0a_C", [D, 64]), ("l0a_D", [1, D]),
        ("l0f_Wg", [D, E]), ("l0f_W1", [E, D, F_MOE]), ("l0f_W2", [E, F_MOE, D]),
        ("l1a_wq", [D, D]), ("l1a_bq", [1, D]), ("l1a_wk", [D, D]),
        ("l1a_bk", [1, D]), ("l1a_wv", [D, D]), ("l1a_bv", [1, D]),
        ("l1a_wo", [D, D]), ("l1a_bo", [1, D]),
        ("l1f_w", [D, D]), ("l1f_b", [1, D]),
    ]:
        w[name] = nc.dram_tensor(name, shape, F32, kind="ExternalInput").ap()
    ident_bf = nc.dram_tensor("ident_bf", [128, 128], BF16, kind="ExternalInput").ap()
    ident_f32 = nc.dram_tensor("ident_f32", [128, 128], F32, kind="ExternalInput").ap()
    consts = nc.dram_tensor("consts", [128, 64], F32, kind="ExternalInput").ap()
    ones_row = nc.dram_tensor("ones_row", [1, 128], F32, kind="ExternalInput").ap()
    rt_c = nc.dram_tensor("rt_c", [128, 128], BF16, kind="ExternalInput").ap()
    gmat_t = nc.dram_tensor("gmat_t", [8, 128], F32, kind="ExternalInput").ap()
    utri_c = nc.dram_tensor("utri_c", [128, 128], F32, kind="ExternalInput").ap()
    utri64_c = nc.dram_tensor("utri64_c", [64, 64], F32, kind="ExternalInput").ap()
    rope_c = nc.dram_tensor("rope_c", [128, 4 * S], F32, kind="ExternalInput").ap()

    dbg = nc.dram_tensor("dbg", [128, 160], F32, kind="ExternalOutput").ap()
    xres = nc.dram_tensor("xres", [T, D], F32, kind="Internal").ap()
    scr_f32 = nc.dram_tensor("scr_f32", [T, D], F32, kind="Internal").ap()
    lnbf = nc.dram_tensor("lnbf", [T, D], BF16, kind="Internal").ap()
    lnx_ext = nc.dram_tensor("lnx_ext", [T, D + 64], F32, kind="Internal").ap()
    xsort_d = nc.dram_tensor("xsort_d", [E * CEXP + 128, D + 64], F32, kind="Internal").ap()
    ysort_d = nc.dram_tensor("ysort_d", [E * CEXP + 1, D], BF16, kind="Internal").ap()
    vd_d = nc.dram_tensor("vd_d", [D, 129, 64], F32, kind="Internal").ap()
    wtab_d = nc.dram_tensor("wtab_d", [D, 64, 132], F32, kind="Internal").ap()
    kpad_d = nc.dram_tensor("kpad_d", [D, 256], F32, kind="Internal").ap()
    ktoep_d = nc.dram_tensor("ktoep_d", [D, 128, 128], BF16, kind="Internal").ap()
    vw_d = nc.dram_tensor("vw_d", [D, 128, 64], BF16, kind="Internal").ap()
    lnT_d = nc.dram_tensor("lnT_d", [D, T], BF16, kind="Internal").ap()
    ycm_d = nc.dram_tensor("ycm_d", [D, T], BF16, kind="Internal").ap()
    w1bf_d = nc.dram_tensor("w1bf_d", [E * F_MOE, D], BF16, kind="Internal").ap()
    w2bf_d = nc.dram_tensor("w2bf_d", [E * F_MOE, D], BF16, kind="Internal").ap()
    prbT = nc.dram_tensor("prbT", [1, E * T], F32, kind="Internal").ap()
    prbA = nc.dram_tensor("prbA", [NCORES, E * T], F32, kind="Internal").ap()
    qkT = nc.dram_tensor("qkT", [2 * D, T], BF16, kind="Internal").ap()
    vtok = nc.dram_tensor("vtok", [T, D], BF16, kind="Internal").ap()

    with ExitStack() as ES:
        En = ES.enter_context
        sb = {}

        def S_(name, shape, dt):
            sb[name] = En(nc.sbuf_tensor(name, shape, dt))
            return sb[name]

        bufs = {}

        def BT(name):
            if name not in bufs:
                bufs[name] = Buf()
            return bufs[name]

        ARN = S_("ARN", [128, 16 * 2048], BF16)

        def AR(c0, ncols, dt=BF16):
            ap = ARN[:, c0:c0 + ncols]
            if dt == F32:
                ap = ap.bitcast(F32)
            elif dt == FP8:
                ap = ap.bitcast(FP8)
            regs = tuple(BT(f"ARr{r}") for r in
                         range(c0 // 2048, (c0 + ncols + 2047) // 2048))
            return ap, regs

        xt = [S_(f"xt{i}", [128, D], F32) for i in range(2)]
        xnew = [S_(f"xnew{i}", [128, D], F32) for i in range(2)]
        lnt = [S_("lnt0", [128, D], F32)] * 2
        ybf = [S_(f"ybf{i}", [128, D], BF16) for i in range(2)]
        xq8 = [S_(f"xq8_{i}", [128, DC, 128], FP8) for i in range(2)]
        tiny = S_("tiny", [128, 16], F32)
        tiny2 = S_("tiny2", [128, 16], F32)
        stats = S_("stats", [128, 2 * NT], F32)
        amx = S_("amx", [128, 1], F32)
        scrow = S_("scrow", [1, 32], F32)
        k128 = S_("k128", [128, 1], F32)
        s2_128 = S_("s2_128", [128, 1], F32)
        bias128 = S_("bias128", [128, D], F32)
        idn_f32 = S_("idn_f32", [128, 128], F32)
        idn_bfs = S_("idn_bfs", [128, 128], BF16)
        cons = S_("cons", [128, 64], F32)
        wrow = S_("wrow", [128, D], F32)
        onesr = S_("onesr", [1, 128], F32)
        wqt8, wqt8_r = AR(12 * 2048, 4096, FP8)
        S_("dbgt", [128, 64], F32)
        S_("dbgt2", [4, 64], F32)
        S_("vtmp2", [128, 16, 129], F32)
        wg_sb = S_("wg_sb", [128, DC, E], F32)
        gmt = S_("gmt", [8, 128], F32)
        utri = S_("utri", [128, 128], F32)
        utri64 = S_("utri64", [64, 64], F32)
        p8s = S_("p8s", [128, 128], F32)
        lo8 = S_("lo8", [8, 4], F32)
        ge8 = S_("ge8", [8, 4], F32)
        thrB = S_("thrB", [128, 8], F32)
        mid128 = S_("mid128", [128, 1], F32)
        dI = [S_(f"dI{i}", [128, 1], I32) for i in range(2)]
        xls = S_("xls", [128, D + 64], F32)
        gsort = [S_(f"gsort{i}", [128, 2], F32) for i in range(2)]

        wT = {n: S_(f"wT_{n}", [128, DC * D], BF16) for n in ("q", "k", "v")}
        wT["o"] = wT["q"]
        bqk = S_("bqk", [128, 16], F32)
        bv128 = S_("bv128", [128, D], F32)
        bo128 = S_("bo128", [128, D], F32)
        rtc = S_("rtc", [128, 128], BF16)
        ropes, ropes_r = AR(8 * 2048, 8192, F32)
        _lnaT_ap, lnaT_r = AR(0, 8192)
        lnaT = [_lnaT_ap.rearrange("p (a b) -> p a b", a=DC)] * 2
        qstg = [S_(f"qstg{i}", [128, 1024], BF16) for i in range(2)]
        ropt1 = [S_("ropt1_0", [128, 1024], F32)] * 2
        ropt2 = [S_("ropt2_0", [128, 1024], F32)] * 2
        qrt = [S_(f"qrt{i}", [128, 1024], BF16) for i in range(2)]
        qtb = [S_(f"qtb{i}", [64, S], BF16) for i in range(2)]
        ktb = [S_(f"ktb{i}", [64, S], BF16) for i in range(2)]
        vtb = [AR(28672, 1040)[0].rearrange("p (h c) -> p h c", h=16),
               AR(29712, 1040)[0].rearrange("p (h c) -> p h c", h=16)]
        ptile = [S_("ptile0", [128, S], BF16)] * 2
        otch = [AR(i * 2048, 2048)[0] for i in range(DC)]
        PSP = [En(nc.psum_tensor(f"ps{i}", [128, 1024], F32)) for i in range(4)]
        psB = [PSP[0][:, :], PSP[1][:, :]]
        psT = PSP[2][:, 0:128]
        psU = PSP[3][:, 0:128]

        sems = {e: En(nc.semaphore(f"sem{e}")) for e in ENGINES}
        dsems = {e: [En(nc.semaphore(f"dsem{e}{i}")) for i in range(NDMASEM)]
                 for e in "YGS"}
        dsems["V"] = dsems["Y"]; dsems["T"] = dsems["Y"]
        block = En(nc.Block())

        xres_b = [BT(f"xres{i}") for i in range(NT)]

        def dma(eng, dst, src, deps=(), reads=(), writes=(), tag="dma"):
            return P.add(eng, lambda E_, dst=dst, src=src: E_.dma_start(out=dst, in_=src),
                         deps, dma=True, tag=tag, reads=reads, writes=writes)

        op_idnf = dma("Y", idn_f32[:, :], ident_f32, writes=(BT("idnf"),))
        dma("Y", idn_bfs[:, :], ident_bf, writes=(BT("idnbf"),))
        op_cons = dma("Y", cons[:, :], consts, writes=(BT("cons"),))
        dma("Y", onesr[:, :], ones_row, writes=(BT("ones"),))
        for i in range(NT):
            dma("Y", xres[i * 128:(i + 1) * 128, :], x_in[i * 128:(i + 1) * 128, :],
                writes=(xres_b[i],), tag="xinit")

        def prep_wT(dst, src_ap, tag, extra=()):
            for oc in range(DC):
                ld = dma("G", wrow[:, :], src_ap[oc * 128:(oc + 1) * 128, :],
                         writes=(BT("wrow"),), tag=f"{tag}ld")
                for dc in range(DC):
                    pst = psT if dc % 2 == 0 else psU
                    bpst = BT("ps2") if dc % 2 == 0 else BT("ps3")
                    P.add("T", lambda t, dc=dc, pst=pst: t.transpose(
                        pst[:, :], wrow[:, dc * 128:(dc + 1) * 128], idn_f32[:, :]),
                        reads=(BT("wrow"), BT("idnf")), writes=(bpst,), osz=16384)
                    eng = "V" if dc % 2 == 0 else "S"
                    if eng == "V":
                        P.add("V", lambda v, dc=dc, oc=oc, pst=pst: v.tensor_copy(
                            dst[:, dc * D + oc * 128: dc * D + oc * 128 + 128], pst[:, :]),
                            reads=(bpst,), writes=(BT(f"{tag}_w"),) + extra, osz=16384)
                    else:
                        P.add("S", lambda s, dc=dc, oc=oc, pst=pst: s.copy(
                            dst[:, dc * D + oc * 128: dc * D + oc * 128 + 128], pst[:, :]),
                            reads=(bpst,), writes=(BT(f"{tag}_w"),) + extra, osz=16384)

        if "fp8a" in enable or "fp8b" in enable:
            P.add("V", lambda v: v.memset(amx[:, :], 0.0), writes=(BT("amx"),))
            for oc in range(DC):
                ld = dma("G", wrow[:, :], w["l1f_w"][oc * 128:(oc + 1) * 128, :],
                         writes=(BT("wrow"),), tag="wld")
                P.add("V", lambda v: v.tensor_reduce(
                    tiny[:, 0:1], wrow[:, :], AX.X, ALU.max, apply_absolute_value=True),
                    reads=(BT("wrow"),), writes=(BT("tiny"),))
                P.add("V", lambda v: v.tensor_tensor(
                    amx[:, 0:1], amx[:, 0:1], tiny[:, 0:1], ALU.max),
                    reads=(BT("tiny"), BT("amx")), writes=(BT("amx"),))
            P.add("T", lambda t: t.transpose(psT[0:1, :], amx[:, 0:1], idn_f32[:, :]),
                  reads=(BT("amx"), BT("idnf")), writes=(BT("ps2"),))
            P.add("V", lambda v: v.reduce_max(scrow[0:1, 0:1], psT[0:1, 0:128], AX.X),
                  reads=(BT("ps2"),), writes=(BT("wsc"),))
            P.add("T", lambda t: t.matmul(psT[:, 1:2], onesr[0:1, :],
                  scrow[0:1, 0:1], start=True, stop=True),
                  reads=(BT("wsc"), BT("ones")), writes=(BT("ps2"),))
            wsc128 = S_("wsc128", [128, 4], F32)
            P.add("V", lambda v: v.tensor_scalar(wsc128[:, 0:1], psT[:, 1:2],
                  1e-12, None, ALU.max), reads=(BT("ps2"),), writes=(BT("wsc128"),))
            P.add("V", lambda v: v.reciprocal(wsc128[:, 1:2], wsc128[:, 0:1]),
                  reads=(BT("wsc128"),), writes=(BT("wsc128"),))
            P.add("V", lambda v: v.tensor_scalar(wsc128[:, 1:2], wsc128[:, 1:2],
                  448.0, 448.0, ALU.mult, op1=ALU.min),
                  reads=(BT("wsc128"),), writes=(BT("wsc128"),))
            P.add("V", lambda v: v.tensor_scalar(wsc128[:, 2:3], wsc128[:, 1:2],
                  0.5, None, ALU.mult), reads=(BT("wsc128"),), writes=(BT("wsc128"),))
            P.add("V", lambda v: v.reciprocal(wsc128[:, 3:4], wsc128[:, 1:2]),
                  reads=(BT("wsc128"),), writes=(BT("wsc128"),))
            for oc in range(DC):
                ld2 = dma("G", wrow[:, :], w["l1f_w"][oc * 128:(oc + 1) * 128, :],
                          writes=(BT("wrow"),), tag="wld2")
                for dc in range(DC):
                    pst = psT if dc % 2 == 0 else psU
                    bpst = BT("ps2") if dc % 2 == 0 else BT("ps3")
                    P.add("T", lambda t, dc=dc, pst=pst: t.transpose(
                        pst[:, :], wrow[:, dc * 128:(dc + 1) * 128], idn_f32[:, :]),
                        reads=(BT("wrow"), BT("idnf")), writes=(bpst,), osz=16384)
                    P.add("V" if dc % 2 == 0 else "S",
                          (lambda v, dc=dc, oc=oc, pst=pst: v.tensor_scalar(
                              wqt8[:, dc * D + oc * 128: dc * D + oc * 128 + 128],
                              pst[:, :], wsc128[:, 2:3], None, ALU.mult))
                          if dc % 2 == 0 else
                          (lambda s, dc=dc, oc=oc, pst=pst: s.activation(
                              wqt8[:, dc * D + oc * 128: dc * D + oc * 128 + 128],
                              pst[:, :], AF.Identity, scale=wsc128[:, 2:3])),
                          reads=(bpst, BT("wsc128")), writes=(wqt8_r,), osz=16384)
            dma("G", ropt2[0][0:1, 0:D], w["l1f_b"], writes=(BT("ropt2_0"),))
            for hb in range(2):
                P.add("T", lambda t, hb=hb: t.matmul(psB[0][:, hb * 512:(hb + 1) * 512],
                      onesr[0:1, :], ropt2[0][0:1, hb * 512:hb * 512 + 512],
                      start=True, stop=True),
                      reads=(BT("ropt2_0"), BT("ones")), writes=(BT("ps0"),))
            P.add("V", lambda v: v.tensor_copy(bias128[:, :], psB[0][:, 0:D]),
                  reads=(BT("ps0"),), writes=(BT("bias128"),))

        def ln_stats_pass(phase, with_absmax=False):
            if with_absmax:
                P.add("V", lambda v: v.memset(amx[:, :], 0.0), writes=(BT("amx"),))
            for i in range(NT):
                sl = xt[i % 2]
                bsl = BT(f"xt{i % 2}")
                ld = dma("Y", sl[:, :], xres[i * 128:(i + 1) * 128, :],
                         reads=(xres_b[i],), writes=(bsl,), tag=f"{phase}lnld")
                P.add("V", lambda v, sl=sl: v.reduce_sum(tiny[:, 0:1], sl[:, :], AX.X),
                      reads=(bsl,), writes=(BT("tiny"),))
                P.add("V", lambda v, sl=sl: v.tensor_tensor(
                    lnt[0][:, :], sl[:, :], sl[:, :], ALU.mult),
                    reads=(bsl,), writes=(BT("lnt0"),))
                P.add("V", lambda v: v.reduce_sum(tiny[:, 1:2], lnt[0][:, :], AX.X),
                      reads=(BT("lnt0"),), writes=(BT("tiny"),))
                P.add("V", lambda v: v.tensor_scalar(tiny[:, 2:3], tiny[:, 0:1],
                      1.0 / D, None, ALU.mult), reads=(BT("tiny"),), writes=(BT("tiny"),))
                P.add("V", lambda v: v.tensor_scalar(tiny[:, 3:4], tiny[:, 1:2],
                      1.0 / D, None, ALU.mult), reads=(BT("tiny"),), writes=(BT("tiny"),))
                P.add("V", lambda v: v.tensor_tensor(tiny[:, 4:5], tiny[:, 2:3],
                      tiny[:, 2:3], ALU.mult), reads=(BT("tiny"),), writes=(BT("tiny"),))
                P.add("V", lambda v: v.tensor_tensor(tiny[:, 5:6], tiny[:, 3:4],
                      tiny[:, 4:5], ALU.subtract), reads=(BT("tiny"),), writes=(BT("tiny"),))
                P.add("V", lambda v: v.tensor_scalar(tiny[:, 6:7], tiny[:, 5:6],
                      LN_EPS, None, ALU.add), reads=(BT("tiny"),), writes=(BT("tiny"),))
                sq = P.add("S", lambda s: s.activation(tiny2[:, 0:1], tiny[:, 6:7], AF.Sqrt),
                      reads=(BT("tiny"),), writes=(BT("tiny2"),))
                P.add("V", lambda v, i=i: v.reciprocal(stats[:, 2 * i:2 * i + 1], tiny2[:, 0:1]),
                      reads=(BT("tiny2"),), writes=(BT(f"st{i}"),))
                P.add("V", lambda v, i=i: v.tensor_scalar(
                    stats[:, 2 * i + 1:2 * i + 2], tiny[:, 2:3],
                    -1.0, stats[:, 2 * i:2 * i + 1], ALU.mult, op1=ALU.mult),
                    reads=(BT("tiny"), BT(f"st{i}")), writes=(BT(f"st{i}"),))
                if with_absmax:
                    nm = P.add("S", lambda s, sl=sl, i=i: s.activation(
                        xnew[1][:, :], sl[:, :], AF.Identity,
                        bias=stats[:, 2 * i + 1:2 * i + 2],
                        scale=stats[:, 2 * i:2 * i + 1]),
                        reads=(bsl, BT(f"st{i}")), writes=(BT("xnew1"),))
                    P.add("V", lambda v: v.tensor_reduce(
                        tiny[:, 7:8], xnew[1][:, :], AX.X, ALU.max,
                        apply_absolute_value=True),
                        reads=(BT("xnew1"),), writes=(BT("tiny"),))
                    P.add("V", lambda v: v.tensor_tensor(
                        amx[:, 0:1], amx[:, 0:1], tiny[:, 7:8], ALU.max),
                        reads=(BT("tiny"), BT("amx")), writes=(BT("amx"),))

        def fp8_phase(phase):
            amax_loc = nc.dram_tensor(f"amax_loc_{phase}", [1, 8], F32, kind="Internal").ap()
            amax_all = nc.dram_tensor(f"amax_all_{phase}", [1, 8], F32, kind="Internal").ap()
            ln_stats_pass(phase, with_absmax=True)
            P.add("T", lambda t: t.transpose(psT[0:1, :], amx[:, 0:1], idn_f32[:, :]),
                  reads=(BT("amx"), BT("idnf")), writes=(BT("ps2"),))
            P.add("V", lambda v: v.reduce_max(scrow[0:1, 8:9], psT[0:1, 0:128], AX.X),
                  reads=(BT("ps2"),), writes=(BT("xsc"),))
            dma("G", amax_loc[0:1, 0:1], scrow[0:1, 8:9], reads=(BT("xsc"),), writes=(BT("amaxd"),))
            cc = P.add("G", lambda g: g.collective_compute(
                "AllReduce", ALU.max, [list(range(NCORES))], [amax_loc], [amax_all]),
                reads=(BT("amaxd"),), writes=(BT("amaxd"),), coll=True)
            dma("G", scrow[0:1, 8:9], amax_all[0:1, 0:1], reads=(BT("amaxd"),), writes=(BT("xsc"),))
            P.add("T", lambda t: t.matmul(psT[:, 1:2], onesr[0:1, :],
                  scrow[0:1, 8:9], start=True, stop=True),
                  reads=(BT("xsc"), BT("ones")), writes=(BT("ps2"),))
            xsc128 = tiny2[:, 8:10]
            P.add("V", lambda v: v.tensor_scalar(xsc128[:, 0:1], psT[:, 1:2],
                  1e-12, None, ALU.max), reads=(BT("ps2"),), writes=(BT("xsc128"),))
            P.add("V", lambda v: v.reciprocal(xsc128[:, 0:1], xsc128[:, 0:1]),
                  reads=(BT("xsc128"),), writes=(BT("xsc128"),))
            P.add("V", lambda v: v.tensor_scalar(xsc128[:, 0:1], xsc128[:, 0:1],
                  448.0, 448.0, ALU.mult, op1=ALU.min),
                  reads=(BT("xsc128"),), writes=(BT("xsc128"),))
            P.add("V", lambda v: v.tensor_scalar(s2_128[:, 0:1], xsc128[:, 0:1],
                  0.5, None, ALU.mult), reads=(BT("xsc128"),), writes=(BT("s2b"),))
            P.add("V", lambda v: v.reciprocal(xsc128[:, 1:2], xsc128[:, 0:1]),
                  reads=(BT("xsc128"),), writes=(BT("xsc128"),))
            P.add("V", lambda v: v.tensor_tensor(k128[:, 0:1], xsc128[:, 1:2],
                  wsc128[:, 3:4], ALU.mult),
                  reads=(BT("xsc128"), BT("wsc128")), writes=(BT("k4b"),))
            P.add("V", lambda v: v.tensor_scalar(k128[:, 0:1], k128[:, 0:1],
                  4.0, None, ALU.mult), reads=(BT("k4b"),), writes=(BT("k4b"),))
            for i in range(NT):
                sl, bsl = xt[i % 2], BT(f"xt{i % 2}")
                lsl, blsl = lnt[0], BT("lnt0")
                q8, bq8 = xq8[i % 2], BT(f"xq8_{i % 2}")
                ps, bps = psB[i % 2], BT(f"ps{i % 2}")
                yb, byb = ybf[i % 2], BT(f"ybf{i % 2}")
                xn, bxn = xnew[i % 2], BT(f"xnew{i % 2}")
                ld = dma("Y", sl[:, :], xres[i * 128:(i + 1) * 128, :],
                         reads=(xres_b[i],), writes=(bsl,), tag="fp8ld")
                P.add("S", lambda s, sl=sl, lsl=lsl, i=i: s.activation(
                    lsl[:, :], sl[:, :], AF.Identity,
                    bias=stats[:, 2 * i + 1:2 * i + 2], scale=stats[:, 2 * i:2 * i + 1]),
                    reads=(bsl, BT(f"st{i}")), writes=(blsl,))
                for dc in range(DC):
                    pst = psT if dc % 2 == 0 else psU
                    bpst = BT("ps2") if dc % 2 == 0 else BT("ps3")
                    P.add("T", lambda t, lsl=lsl, dc=dc, pst=pst: t.transpose(
                        pst[:, :], lsl[:, dc * 128:(dc + 1) * 128], idn_f32[:, :]),
                        reads=(blsl, BT("idnf")), writes=(bpst,))
                    if dc % 2 == 0:
                        P.add("V", lambda v, q8=q8, dc=dc, pst=pst: v.tensor_scalar(
                            q8[:, dc, :], pst[:, :], s2_128[:, 0:1], None, ALU.mult),
                            reads=(bpst, BT("s2b")), writes=(bq8,))
                    else:
                        P.add("S", lambda s, q8=q8, dc=dc, pst=pst: s.activation(
                            q8[:, dc, :], pst[:, :], AF.Copy, scale=s2_128[:, 0:1]),
                            reads=(bpst, BT("s2b")), writes=(bq8,))
                for dc in range(DC):
                    for hb in range(2):
                        P.add("T", lambda t, q8=q8, dc=dc, ps=ps, hb=hb: t.matmul(
                            ps[:, hb * 512:(hb + 1) * 512], q8[:, dc, :],
                            wqt8[:, dc * D + hb * 512: dc * D + (hb + 1) * 512],
                            start=(dc == 0), stop=(dc == DC - 1)),
                            reads=(bq8, wqt8_r), writes=(bps,))
                P.add("V", lambda v, ps=ps, yb=yb: v.scalar_tensor_tensor(
                    yb[:, :], ps[:, :], k128[:, 0:1], bias128[:, :],
                    ALU.mult, ALU.add),
                    reads=(bps, BT("k4b"), BT("bias128")), writes=(byb,))
                P.add("V", lambda v, sl=sl, yb=yb, xn=xn: v.scalar_tensor_tensor(
                    xn[:, :], yb[:, :], 0.5, sl[:, :], ALU.mult, ALU.add),
                    reads=(byb, bsl), writes=(bxn,))
                dma("Y", xres[i * 128:(i + 1) * 128, :], xn[:, :],
                    reads=(bxn,), writes=(xres_b[i],), tag="fp8st")



        def ssm_setup():
            PI_HI = float(np.float32(np.pi))
            TPI_HI = float(np.float32(2 * np.pi))
            TPI_LO = float(np.float32(2 * np.pi - np.float64(np.float32(2 * np.pi))))
            ldt128 = S_("ldt128", [128, 8], F32)
            P.add("Y", lambda y: y.dma_start(
                out=ldt128[:, :], in_=bass.AP(w["l0a_logdt"].tensor, 0, [[1, 128], [128, 8]]),
                allow_slow_non_contiguous=True), dma=True, writes=(BT("ldt"),))
            drow = wrow[0:1, :]
            dma("Y", drow[:, :], w["l0a_D"], reads=(BT("wrow"),), writes=(BT("wrow"),))
            dmat = bv128
            for hb in range(2):
                P.add("T", lambda t, hb=hb: t.matmul(
                    PSP[0][:, hb * 512:(hb + 1) * 512], onesr[0:1, :],
                    drow[0:1, hb * 512:hb * 512 + 512], start=True, stop=True),
                    reads=(BT("wrow"), BT("ones")), writes=(BT("ps0"),), osz=512)
            P.add("V", lambda v: v.tensor_copy(dmat[:, :], PSP[0][:, 0:D]),
                  reads=(BT("ps0"),), writes=(BT("dmat"), BT("bv128")), osz=D)
            dt8 = S_("dt8", [128, 8], F32)
            P.add("S", lambda s: s.activation(dt8[:, :], ldt128[:, :], AF.Exp),
                  reads=(BT("ldt"),), writes=(BT("dt8"),))
            smv, smv_r = AR(14 * 2048, 1024, F32)
            la = smv[:, 0:32]
            ai = smv[:, 32:64]
            ct = smv[:, 64:128]
            cw = smv[:, 128:192]
            pw2 = smv[:, 192:256]
            t32 = [smv[:, 256 + 32 * i:288 + 32 * i] for i in range(8)]
            tb = [BT(f"t{i}") for i in range(8)]
            _kkv = sb["ptile0"][:, 0:528].bitcast(F32)
            kknl = _kkv[:, 0:132]
            kk2 = _kkv[:, 132:264]
            vand, vand_r = AR(0, 16512, F32)
            vandv = vand.rearrange("p (l r) -> p l r", l=129)
            wblk, wblk_r = AR(16512, 4224, F32)
            wbv = wblk.rearrange("p (r l) -> p r l", r=16)
            vtmpv = wblk[:, 0:2048].rearrange("p (a b) -> p a b", a=64)

            def TT(dst, a, b, op, rd, wr, osz=64):
                P.add("V", lambda v, dst=dst, a=a, b=b, op=op: v.tensor_tensor(
                    dst, a, b, op), reads=rd, writes=wr, osz=osz)

            def TS(dst, a, s1, s2, op, op1, rd, wr, osz=64):
                if op1 is None:
                    P.add("V", lambda v, dst=dst, a=a: v.tensor_scalar(
                        dst, a, s1, None, op), reads=rd, writes=wr, osz=osz)
                else:
                    P.add("V", lambda v, dst=dst, a=a: v.tensor_scalar(
                        dst, a, s1, s2, op, op1=op1), reads=rd, writes=wr, osz=osz)

            def STT(dst, a, sc, b, op0, op1, rd, wr, osz=64):
                P.add("V", lambda v, dst=dst, a=a, sc=sc, b=b: v.scalar_tensor_tensor(
                    dst, a, sc, b, op0, op1), reads=rd, writes=wr, osz=osz)

            P.add("V", lambda v: v.memset(kknl[:, :], 0.0), writes=(BT("kknl"),))
            P.add("V", lambda v: v.memset(kk2[:, :], 0.0), writes=(BT("kk2"),))
            for ht in range(8):
                P.add("Y", lambda y, ht=ht: y.dma_start(
                    out=kpad_d[ht * 128:(ht + 1) * 128, 0:128], in_=kknl[:, 0:128]),
                    dma=True, reads=(BT("kknl"),), writes=(BT("kpad"),), tag="kz")
                P.add("Y", lambda y, ht=ht: y.dma_start(
                    out=kpad_d[ht * 128:(ht + 1) * 128, 128:256], in_=kk2[:, 0:128]),
                    dma=True, reads=(BT("kk2"),), writes=(BT("kpad"),), tag="kz2")
            for ht in range(8):
                r0 = ht * 128
                dma("Y", la[:, :], w["l0a_logA"][r0:r0 + 128, :], writes=(BT("la"),))
                dma("Y", ai[:, :], w["l0a_Aim"][r0:r0 + 128, :], writes=(BT("ai"),))
                dma("Y", ct[:, :], w["l0a_C"][r0:r0 + 128, :], writes=(BT("ct"),))
                P.add("S", lambda s: s.activation(t32[0][:, :], la[:, :], AF.Exp),
                      reads=(BT("la"),), writes=(tb[0],))
                P.add("V", lambda v, ht=ht: v.tensor_scalar(t32[1][:, :], t32[0][:, :],
                      dt8[:, ht:ht + 1], -1.0, ALU.mult, op1=ALU.mult),
                      reads=(tb[0], BT("dt8")), writes=(tb[1],))
                P.add("V", lambda v, ht=ht: v.tensor_scalar(t32[2][:, :], ai[:, :],
                      dt8[:, ht:ht + 1], None, ALU.mult),
                      reads=(BT("ai"), BT("dt8")), writes=(tb[2],))
                for mlt in (16.0, 8.0, 4.0, 2.0, 1.0, 1.0):
                    TS(t32[3][:, :], t32[2][:, :], mlt * PI_HI, None, ALU.is_gt, None,
                       (tb[2],), (tb[3],))
                    STT(t32[2][:, :], t32[3][:, :], -mlt * TPI_HI, t32[2][:, :],
                        ALU.mult, ALU.add, (tb[3], tb[2]), (tb[2],))
                    STT(t32[2][:, :], t32[3][:, :], -mlt * TPI_LO, t32[2][:, :],
                        ALU.mult, ALU.add, (tb[3], tb[2]), (tb[2],))
                    TS(t32[3][:, :], t32[2][:, :], -mlt * PI_HI, None, ALU.is_lt, None,
                       (tb[2],), (tb[3],))
                    STT(t32[2][:, :], t32[3][:, :], mlt * TPI_HI, t32[2][:, :],
                        ALU.mult, ALU.add, (tb[3], tb[2]), (tb[2],))
                    STT(t32[2][:, :], t32[3][:, :], mlt * TPI_LO, t32[2][:, :],
                        ALU.mult, ALU.add, (tb[3], tb[2]), (tb[2],))
                P.add("S", lambda s: s.activation(t32[4][:, :], t32[1][:, :], AF.Exp),
                      reads=(tb[1],), writes=(tb[4],))
                P.add("S", lambda s: s.activation(t32[5][:, :], t32[2][:, :], AF.Sin),
                      reads=(tb[2],), writes=(tb[5],))
                P.add("S", lambda s: s.activation(t32[6][:, :], t32[2][:, :], AF.Abs),
                      reads=(tb[2],), writes=(tb[6],))
                TS(t32[6][:, :], t32[6][:, :], -1.0, PI_HI / 2, ALU.mult, ALU.add,
                   (tb[6],), (tb[6],))
                P.add("S", lambda s: s.activation(t32[6][:, :], t32[6][:, :], AF.Sin),
                      reads=(tb[6],), writes=(tb[6],))
                TT(pw2[:, 0:32], t32[4][:, :], t32[6][:, :], ALU.mult,
                   (tb[4], tb[6]), (BT("pw2"),))
                TT(pw2[:, 32:64], t32[4][:, :], t32[5][:, :], ALU.mult,
                   (tb[4], tb[5]), (BT("pw2"),))
                TT(t32[7][:, :], t32[0][:, :], t32[0][:, :], ALU.mult, (tb[0],), (tb[7],))
                STT(t32[7][:, :], ai[:, :], 1.0, t32[7][:, :], ALU.mult, ALU.mult,
                    (BT("ai"), tb[7]), (tb[7],))
                TT(t32[7][:, :], t32[0][:, :], t32[0][:, :], ALU.mult, (tb[0],), (tb[7],))
                TT(t32[3][:, :], ai[:, :], ai[:, :], ALU.mult, (BT("ai"),), (tb[3],))
                TT(t32[7][:, :], t32[7][:, :], t32[3][:, :], ALU.add, (tb[7], tb[3]), (tb[7],))
                P.add("V", lambda v: v.reciprocal(t32[7][:, :], t32[7][:, :]),
                      reads=(tb[7],), writes=(tb[7],))
                TS(t32[1][:, :], pw2[:, 0:32], 1.0, None, ALU.subtract, None,
                   (BT("pw2"),), (tb[1],))
                ccr = ct.rearrange("p (n two) -> p n two", two=2)
                TT(cw[:, 0:32], ccr[:, :, 0], t32[1][:, :], ALU.mult,
                   (BT("ct"), tb[1]), (BT("cw"),))
                STT(t32[3][:, :], ccr[:, :, 1], -1.0, pw2[:, 32:64], ALU.mult, ALU.mult,
                    (BT("ct"), BT("pw2")), (tb[3],))
                TT(cw[:, 0:32], cw[:, 0:32], t32[3][:, :], ALU.add,
                   (BT("cw"), tb[3]), (BT("cw"),))
                TT(t32[3][:, :], ccr[:, :, 0], pw2[:, 32:64], ALU.mult,
                   (BT("ct"), BT("pw2")), (tb[3],))
                TT(cw[:, 32:64], ccr[:, :, 1], t32[1][:, :], ALU.mult,
                   (BT("ct"), tb[1]), (BT("cw"),))
                TT(cw[:, 32:64], cw[:, 32:64], t32[3][:, :], ALU.add,
                   (BT("cw"), tb[3]), (BT("cw"),))
                STT(t32[3][:, :], cw[:, 0:32], -1.0, t32[0][:, :], ALU.mult, ALU.mult,
                    (BT("cw"), tb[0]), (tb[3],))
                TT(t32[2][:, :], cw[:, 32:64], ai[:, :], ALU.mult,
                   (BT("cw"), BT("ai")), (tb[2],))
                TT(t32[2][:, :], t32[2][:, :], t32[3][:, :], ALU.add, (tb[2], tb[3]), (tb[2],))
                STT(t32[3][:, :], cw[:, 0:32], -1.0, ai[:, :], ALU.mult, ALU.mult,
                    (BT("cw"), BT("ai")), (tb[3],))
                STT(t32[5][:, :], cw[:, 32:64], -1.0, t32[0][:, :], ALU.mult, ALU.mult,
                    (BT("cw"), tb[0]), (tb[5],))
                TT(t32[3][:, :], t32[3][:, :], t32[5][:, :], ALU.add, (tb[3], tb[5]), (tb[3],))
                TT(cw[:, 0:32], t32[2][:, :], t32[7][:, :], ALU.mult, (tb[2], tb[7]), (BT("cw"),))
                TT(cw[:, 32:64], t32[3][:, :], t32[7][:, :], ALU.mult, (tb[3], tb[7]), (BT("cw"),))
                P.add("V", lambda v: v.memset(vandv[:, 0, 0:32], 1.0), writes=(vand_r,))
                P.add("V", lambda v: v.memset(vandv[:, 0, 32:64], 0.0), writes=(vand_r,))
                vt = vtmpv
                Lc = 1
                while Lc < 129:
                    nn = min(Lc, 129 - Lc)
                    pwre = pw2[:, 0:32].unsqueeze(1).to_broadcast([128, nn, 32])
                    pwim = pw2[:, 32:64].unsqueeze(1).to_broadcast([128, nn, 32])
                    osn = nn * 32
                    TT(vandv[:, Lc:Lc + nn, 0:32], vandv[:, 0:nn, 0:32], pwre,
                       ALU.mult, (vand_r, BT("pw2")), (vand_r,), osn)
                    TT(vt[:, 0:nn, 0:32], vandv[:, 0:nn, 32:64], pwim,
                       ALU.mult, (vand_r, BT("pw2")), (wblk_r,), osn)
                    TT(vandv[:, Lc:Lc + nn, 0:32], vandv[:, Lc:Lc + nn, 0:32],
                       vt[:, 0:nn, 0:32], ALU.subtract, (vand_r, wblk_r), (vand_r,), osn)
                    TT(vandv[:, Lc:Lc + nn, 32:64], vandv[:, 0:nn, 0:32], pwim,
                       ALU.mult, (vand_r, BT("pw2")), (vand_r,), osn)
                    TT(vt[:, 0:nn, 0:32], vandv[:, 0:nn, 32:64], pwre,
                       ALU.mult, (vand_r, BT("pw2")), (wblk_r,), osn)
                    TT(vandv[:, Lc:Lc + nn, 32:64], vandv[:, Lc:Lc + nn, 32:64],
                       vt[:, 0:nn, 0:32], ALU.add, (vand_r, wblk_r), (vand_r,), osn)
                    if 2 * Lc < 129:
                        TT(vt[:, 0, 0:32], pw2[:, 0:32], pw2[:, 0:32], ALU.mult,
                           (BT("pw2"),), (wblk_r,))
                        TT(vt[:, 0, 32:64] if False else vt[:, 1, 0:32],
                           pw2[:, 32:64], pw2[:, 32:64], ALU.mult,
                           (BT("pw2"),), (wblk_r,))
                        TT(vt[:, 2, 0:32], pw2[:, 0:32], pw2[:, 32:64], ALU.mult,
                           (BT("pw2"),), (wblk_r,))
                        TT(pw2[:, 32:64], vt[:, 2, 0:32], vt[:, 2, 0:32], ALU.add,
                           (wblk_r,), (BT("pw2"),))
                        TT(pw2[:, 0:32], vt[:, 0, 0:32], vt[:, 1, 0:32], ALU.subtract,
                           (wblk_r,), (BT("pw2"),))
                    Lc *= 2
                P.add("Y", lambda y, ht=ht: y.dma_start(
                    out=vd_d[ht * 128:(ht + 1) * 128, :, :], in_=vandv[:, :, :]),
                    dma=True, reads=(vand_r,), writes=(BT("vd"),), tag="vdst")
                for half in range(2):
                    for q2 in range(2):
                        vre = vandv[:, :, q2 * 16:(q2 + 1) * 16].transpose([0, 2, 1])
                        vim = vandv[:, :, 32 + q2 * 16:32 + (q2 + 1) * 16].transpose([0, 2, 1])
                        cre = cw[:, q2 * 16:(q2 + 1) * 16].unsqueeze(2).to_broadcast(
                            [128, 16, 129])
                        cim = cw[:, 32 + q2 * 16:32 + (q2 + 1) * 16].unsqueeze(2).to_broadcast(
                            [128, 16, 129])
                        if half == 0:
                            TT(wbv[:, :, 0:129], vre, cre, ALU.mult,
                               (vand_r, BT("cw")), (wblk_r,), 2064)
                            TT(sb["vtmp2"][:, 0:16, 0:129], vim, cim, ALU.mult,
                               (vand_r, BT("cw")), (BT("vtmp2"),), 2064)
                            TT(wbv[:, :, 0:129], wbv[:, :, 0:129],
                               sb["vtmp2"][:, 0:16, 0:129],
                               ALU.subtract, (wblk_r, BT("vtmp2")), (wblk_r,), 2064)
                            TS(wbv[:, :, 0:129], wbv[:, :, 0:129], 2.0, None, ALU.mult,
                               None, (wblk_r,), (wblk_r,), 2064)
                            if q2 == 0:
                                P.add("V", lambda v: v.tensor_reduce(
                                    kknl[:, 0:129], wbv[:, :, 0:129].transpose([0, 2, 1]),
                                    AX.X, ALU.add), reads=(wblk_r,),
                                    writes=(BT("kknl"),), osz=129)
                            else:
                                P.add("V", lambda v: v.tensor_reduce(
                                    kk2[:, 0:129], wbv[:, :, 0:129].transpose([0, 2, 1]),
                                    AX.X, ALU.add), reads=(wblk_r,),
                                    writes=(BT("kk2"),), osz=129)
                                TT(kknl[:, 0:129], kknl[:, 0:129], kk2[:, 0:129],
                                   ALU.add, (BT("kknl"), BT("kk2")), (BT("kknl"),), 129)
                        else:
                            TT(wbv[:, :, 0:129], vim, cre, ALU.mult,
                               (vand_r, BT("cw")), (wblk_r,), 2064)
                            TT(sb["vtmp2"][:, 0:16, 0:129], vre, cim, ALU.mult,
                               (vand_r, BT("cw")), (BT("vtmp2"),), 2064)
                            TT(wbv[:, :, 0:129], wbv[:, :, 0:129],
                               sb["vtmp2"][:, 0:16, 0:129],
                               ALU.add, (wblk_r, BT("vtmp2")), (wblk_r,), 2064)
                            TS(wbv[:, :, 0:129], wbv[:, :, 0:129], -2.0, None, ALU.mult,
                               None, (wblk_r,), (wblk_r,), 2064)
                        P.add("Y", lambda y, ht=ht, half=half, q2=q2: y.dma_start(
                            out=wtab_d[ht * 128:(ht + 1) * 128,
                                       half * 32 + q2 * 16:half * 32 + (q2 + 1) * 16,
                                       0:129],
                            in_=wbv[:, :, 0:129]),
                            dma=True, reads=(wblk_r,), writes=(BT("wtab"),), tag="wtst")
                P.add("Y", lambda y, ht=ht: y.dma_start(
                    out=kpad_d[ht * 128:(ht + 1) * 128, 127:255],
                    in_=kknl[:, 0:128]),
                    dma=True, reads=(BT("kknl"),), writes=(BT("kpad"),), tag="kst")
            for j in range(128):
                P.add("G", lambda g, j=j: g.dma_start(
                    out=bass.AP(ktoep_d.tensor, j * 128, [[16384, 1024], [1, 128]]),
                    in_=bass.AP(kpad_d.tensor, 127 - j, [[256, 1024], [1, 128]])),
                    dma=True, reads=(BT("kpad"),), writes=(BT("ktoep"),), tag="tbuild")
            for cc in range(16):
                P.add("G", lambda g, cc=cc: g.dma_start(
                    out=vw_d[cc * 64:(cc + 1) * 64, :, :],
                    in_=bass.AP(vd_d.tensor, cc * 64 * 8256 + 127 * 64,
                                [[8256, 64], [-64, 128], [1, 64]])),
                    dma=True, reads=(BT("vd"),), writes=(BT("vw"),), tag="vwbuild")

        def ssm_phase():
            ln_stats_pass("ssm", with_absmax=False)
            for i in range(NT):
                sl, bsl = xt[i % 2], BT(f"xt{i % 2}")
                lb, blb = ybf[i % 2], BT(f"ybf{i % 2}")
                dma("Y", sl[:, :], xres[i * 128:(i + 1) * 128, :],
                    reads=(xres_b[i],), writes=(bsl,), tag="ssmld")
                P.add("S", lambda s, sl=sl, lb=lb, i=i: s.activation(
                    lb[:, :], sl[:, :], AF.Identity,
                    bias=stats[:, 2 * i + 1:2 * i + 2], scale=stats[:, 2 * i:2 * i + 1]),
                    reads=(bsl, BT(f"st{i}")), writes=(blb,), osz=D)
                dma("Y", lnbf[i * 128:(i + 1) * 128, :], lb[:, :],
                    reads=(blb,), writes=(BT(f"lnbf{i}"),), tag="ssmst")
            stg = [AR(0, 8192), AR(8192, 8192)]
            for dc in range(DC):
                sg, sg_r = stg[dc % 2]
                for tb in range(8):
                    rd = tuple(BT(f"lnbf{8 * tb + k}") for k in range(8))
                    P.add("Y", lambda y, dc=dc, tb=tb, sg=sg: y.dma_start(
                        out=sg[:, tb * 1024:(tb + 1) * 1024],
                        in_=lnbf[tb * 1024:(tb + 1) * 1024, dc * 128:(dc + 1) * 128],
                        transpose=True),
                        dma=True, reads=rd, writes=sg_r, tag="stgT")
                P.add("G", lambda g, dc=dc, sg=sg: g.dma_start(
                    out=lnT_d[dc * 128:(dc + 1) * 128, :], in_=sg[:, :]),
                    dma=True, reads=sg_r, writes=(BT("lnT"),), tag="stgS")
            QRe, QRe_r = AR(0, 8192, F32)
            QIm, QIm_r = AR(8192, 8192, F32)
            xjgs = [AR(16384, 1024), AR(17408, 1024)]
            tblks = [AR(18432, 2048), AR(28672, 2048)]
            wjblk_f, wjblk_r = AR(20480, 2048)
            wjblk = wjblk_f[0:64, :]
            ytmp, ytmp_r = AR(22528, 2048, F32)
            ocm, ocm_r = AR(30720, 1024)
            lamqR = p8s[:, 0:64]
            lamqI = p8s[:, 64:128]
            BLQ = BT("p8s")
            sst = [ropt1[0][:, 0:512], ropt1[0][:, 512:1024]]
            BSS = BT("ropt1_0")
            sscr = ropt2[0][:, 0:512]
            BSC = BT("ropt2_0")
            crhs = qstg[1]
            BCR = BT("qstg1")
            vwb = qstg[0]
            BVW = BT("qstg0")
            xjl = [qrt[0], qrt[1]]
            QRv = QRe.rearrange("p (gc b c) -> p gc b c", gc=64, b=4)
            QIv = QIm.rearrange("p (gc b c) -> p gc b c", gc=64, b=4)
            lamRb = lamqR[:, :].unsqueeze(2).to_broadcast([128, 64, 4])
            lamIb = lamqI[:, :].unsqueeze(2).to_broadcast([128, 64, 4])

            def xj_pass(grp, xjg, dst_r):
                for cp in range(8):
                    xl, bxl = xjl[cp % 2], BT(f"qrt{cp % 2}")
                    P.add("Y", lambda y, grp=grp, cp=cp, xl=xl: y.dma_start(
                        out=xl[:, 0:128],
                        in_=bass.AP(lnT_d.tensor, (grp * 16 + cp * 2) * T,
                                    [[T, 2], [128, 64], [1, 128]])),
                        dma=True, reads=(BT("lnT"),), writes=(bxl,), tag="xjld")
                    pst, bpst = (psT, BT("ps2")) if cp % 2 == 0 else (psU, BT("ps3"))
                    P.add("T", lambda t, xl=xl, pst=pst: t.matmul(
                        pst[:, :], xl[:, 0:128], idn_bfs[:, :], start=True, stop=True),
                        reads=(bxl, BT("idnbf")), writes=(bpst,), osz=16384)
                    P.add("S", lambda s, cp=cp, pst=pst, xjg=xjg: s.copy(
                        xjg[:, cp * 128:(cp + 1) * 128], pst[:, :]),
                        reads=(bpst,), writes=dst_r, osz=16384)

            for q in range(4):
                for gl in range(16):
                    grp = q * 16 + gl
                    gq, glo = gl >> 2, gl & 3
                    P.add("G", lambda g, grp=grp: g.dma_start(
                        out=vwb[:, :],
                        in_=bass.AP(vw_d.tensor, grp * 16 * 8192,
                                    [[64, 128], [8192, 16], [1, 64]])),
                        dma=True, reads=(BT("vw"),), writes=(BVW,), tag="vwld")
                    xjg, xjg_r = xjgs[gl % 2]
                    xj_pass(grp, xjg, (xjg_r,))
                    for ch in range(16):
                        P.add("T", lambda t, ch=ch, xjg=xjg: t.matmul(
                            PSP[0][0:64, ch * 64:(ch + 1) * 64],
                            vwb[:, ch * 64:(ch + 1) * 64],
                            xjg[:, ch * 64:(ch + 1) * 64],
                            start=True, stop=True),
                            reads=(BVW, xjg_r), writes=(BT("ps0"),), osz=64)
                    P.add("V", lambda v, gq=gq, glo=glo: v.tensor_copy(
                        QRe[gq * 32:(gq + 1) * 32, glo * 1024:(glo + 1) * 1024],
                        PSP[0][0:32, 0:1024]),
                        reads=(BT("ps0"),), writes=(QRe_r,), osz=1024)
                    P.add("V", lambda v, gq=gq, glo=glo: v.tensor_copy(
                        QIm[gq * 32:(gq + 1) * 32, glo * 1024:(glo + 1) * 1024],
                        PSP[0][32:64, 0:1024]),
                        reads=(BT("ps0"),), writes=(QIm_r,), osz=1024)
                for half in range(2):
                    lq = wrow[:, 0:64]
                    P.add("Y", lambda y, q=q, half=half, lq=lq: y.dma_start(
                        out=lq,
                        in_=bass.AP(vd_d.tensor,
                                    (q * 256 + half * 128) * 8256 + 8192,
                                    [[8256, 128], [1, 64]])),
                        dma=True, reads=(BT("vd"), BT("wrow")), writes=(BT("wrow"),),
                        tag="lamld")
                    P.add("T", lambda t, lq=lq: t.transpose(
                        psT[0:64, 0:128], lq, idn_f32[:, :]),
                        reads=(BT("wrow"), BT("idnf")), writes=(BT("ps2"),), osz=8192)
                    for q1 in range(2):
                        gQ = half * 2 + q1
                        P.add("V", lambda v, gQ=gQ, q1=q1: v.tensor_copy(
                            lamqR[gQ * 32:(gQ + 1) * 32, 0:64],
                            psT[0:32, q1 * 64:(q1 + 1) * 64]),
                            reads=(BT("ps2"),), writes=(BLQ,), osz=64)
                        P.add("V", lambda v, gQ=gQ, q1=q1: v.tensor_copy(
                            lamqI[gQ * 32:(gQ + 1) * 32, 0:64],
                            psT[32:64, q1 * 64:(q1 + 1) * 64]),
                            reads=(BT("ps2"),), writes=(BLQ,), osz=64)
                P.add("V", lambda v: v.memset(sst[0][:, :], 0.0), writes=(BSS,))
                for c in range(16):
                    cur, nxt = sst[c % 2], sst[(c + 1) % 2]
                    bcur, bnxt = BSS, BSS
                    cRe = cur[:, 0:256].rearrange("p (gc b) -> p gc b", gc=64)
                    cIm = cur[:, 256:512].rearrange("p (gc b) -> p gc b", gc=64)
                    nRe = nxt[:, 0:256].rearrange("p (gc b) -> p gc b", gc=64)
                    nIm = nxt[:, 256:512].rearrange("p (gc b) -> p gc b", gc=64)
                    m1 = sscr[:, 0:256].rearrange("p (gc b) -> p gc b", gc=64)
                    m2 = sscr[:, 256:512].rearrange("p (gc b) -> p gc b", gc=64)
                    uR = QRv[:, :, :, c]
                    uI = QIv[:, :, :, c]
                    P.add("V", lambda v, nRe=nRe, cRe=cRe: v.tensor_tensor(
                        nRe, cRe, lamRb, ALU.mult),
                        reads=(bcur, BLQ), writes=(bnxt,), osz=64)
                    P.add("V", lambda v, nIm=nIm, cIm=cIm: v.tensor_tensor(
                        nIm, cIm, lamRb, ALU.mult),
                        reads=(bcur, BLQ), writes=(bnxt,), osz=64)
                    P.add("V", lambda v, m1=m1, cIm=cIm: v.tensor_tensor(
                        m1, cIm, lamIb, ALU.mult),
                        reads=(bcur, BLQ), writes=(BSC,), osz=64)
                    P.add("V", lambda v, m2=m2, cRe=cRe: v.tensor_tensor(
                        m2, cRe, lamIb, ALU.mult),
                        reads=(bcur, BLQ), writes=(BSC,), osz=64)
                    P.add("V", lambda v, nRe=nRe, m1=m1: v.tensor_tensor(
                        nRe, nRe, m1, ALU.subtract),
                        reads=(bnxt, BSC), writes=(bnxt,), osz=64)
                    P.add("V", lambda v, nIm=nIm, m2=m2: v.tensor_tensor(
                        nIm, nIm, m2, ALU.add),
                        reads=(bnxt, BSC), writes=(bnxt,), osz=64)
                    P.add("V", lambda v, nRe=nRe, uR=uR: v.tensor_tensor(
                        nRe, nRe, uR, ALU.add),
                        reads=(bnxt, QRe_r), writes=(bnxt,), osz=64)
                    P.add("V", lambda v, nIm=nIm, uI=uI: v.tensor_tensor(
                        nIm, nIm, uI, ALU.add),
                        reads=(bnxt, QIm_r), writes=(bnxt,), osz=64)
                    P.add("V", lambda v, uR=uR, cRe=cRe: v.tensor_copy(uR, cRe),
                          reads=(bcur,), writes=QRe_r, osz=64)
                    P.add("V", lambda v, uI=uI, cIm=cIm: v.tensor_copy(uI, cIm),
                          reads=(bcur,), writes=QIm_r, osz=64)
                for gl in range(16):
                    grp = q * 16 + gl
                    gq, glo = gl >> 2, gl & 3
                    tblk, tblk_r = tblks[gl % 2]
                    xjg, xjg_r = xjgs[gl % 2]
                    P.add("Y", lambda y, grp=grp, tblk=tblk: y.dma_start(
                        out=tblk[:, :],
                        in_=bass.AP(ktoep_d.tensor, grp * 16 * 16384,
                                    [[128, 128], [16384, 16], [1, 128]])),
                        dma=True, reads=(BT("ktoep"),), writes=(tblk_r,), tag="tbld")
                    P.add("G", lambda g, grp=grp: g.dma_start(
                        out=wjblk[:, :],
                        in_=bass.AP(wtab_d.tensor, grp * 16 * 8448 + 1,
                                    [[132, 64], [8448, 16], [1, 128]])),
                        dma=True, reads=(BT("wtab"),), writes=(wjblk_r,), tag="wjld")
                    P.add("V", lambda v, gq=gq, glo=glo: v.tensor_copy(
                        crhs[0:32, :],
                        QRe[gq * 32:(gq + 1) * 32, glo * 1024:(glo + 1) * 1024]),
                        reads=(QRe_r,), writes=(BCR,), osz=1024)
                    P.add("V", lambda v, gq=gq, glo=glo: v.tensor_copy(
                        crhs[32:64, :],
                        QIm[gq * 32:(gq + 1) * 32, glo * 1024:(glo + 1) * 1024]),
                        reads=(QIm_r,), writes=(BCR,), osz=1024)
                    xj_pass(grp, xjg, (xjg_r,))
                    for ch in range(16):
                        P.add("T", lambda t, ch=ch, tblk=tblk, xjg=xjg: t.matmul(
                            PSP[1][:, ch * 64:(ch + 1) * 64],
                            tblk[:, ch * 128:(ch + 1) * 128],
                            xjg[:, ch * 64:(ch + 1) * 64],
                            start=True, stop=False),
                            reads=(tblk_r, xjg_r), writes=(BT("ps1"),), osz=64)
                        P.add("T", lambda t, ch=ch: t.matmul(
                            PSP[1][:, ch * 64:(ch + 1) * 64],
                            wjblk[:, ch * 128:(ch + 1) * 128],
                            crhs[0:64, ch * 64:(ch + 1) * 64],
                            start=False, stop=True),
                            reads=(wjblk_r, BCR), writes=(BT("ps1"),), osz=64)
                    P.add("V", lambda v: v.tensor_copy(ytmp[:, :], PSP[1][:, 0:1024]),
                          reads=(BT("ps1"),), writes=(ytmp_r,), osz=1024)
                    for k in range(8):
                        pst, bpst = (psT, BT("ps2")) if k % 2 == 0 else (psU, BT("ps3"))
                        P.add("T", lambda t, k=k, pst=pst: t.transpose(
                            pst[:, :], ytmp[:, k * 128:(k + 1) * 128], idn_f32[:, :]),
                            reads=(ytmp_r, BT("idnf")), writes=(bpst,), osz=16384)
                        P.add("S", lambda s, k=k, pst=pst: s.copy(
                            ocm[:, k * 128:(k + 1) * 128], pst[:, :]),
                            reads=(bpst,), writes=(ocm_r,), osz=16384)
                    P.add("Y", lambda y, grp=grp: y.dma_start(
                        out=bass.AP(ycm_d.tensor, grp * 16 * T,
                                    [[T, 2], [128, 64], [2 * T, 8], [1, 128]]),
                        in_=ocm[:, :].rearrange("p (k j) -> p k j", k=8)),
                        dma=True, reads=(ocm_r,), writes=(BT("ycm"),), tag="ycst")
            for i in range(NT):
                sl, bsl = xt[i % 2], BT(f"xt{i % 2}")
                lb, blb = ybf[i % 2], BT(f"ybf{i % 2}")
                yt, byt = qrt[i % 2], BT(f"qrt{i % 2}")
                xn, bxn = xnew[i % 2], BT(f"xnew{i % 2}")
                dma("Y", sl[:, :], xres[i * 128:(i + 1) * 128, :],
                    reads=(xres_b[i],), writes=(bsl,), tag="ssmr1")
                P.add("Y", lambda y, i=i, yt=yt: y.dma_start(
                    out=yt[:, :], in_=ycm_d[:, i * 128:(i + 1) * 128], transpose=True),
                    dma=True, reads=(BT("ycm"),), writes=(byt,), tag="ssmr2")
                dma("G", lb[:, :], lnbf[i * 128:(i + 1) * 128, :],
                    reads=(BT(f"lnbf{i}"),), writes=(blb,), tag="ssmr3")
                P.add("V", lambda v, xn=xn, sl=sl, yt=yt: v.tensor_tensor(
                    xn[:, :], sl[:, :], yt[:, :], ALU.add),
                    reads=(bsl, byt), writes=(bxn,), osz=D)
                P.add("V", lambda v, lb=lb: v.tensor_tensor(
                    lnt[0][:, :], lb[:, :], bv128[:, :], ALU.mult),
                    reads=(blb, BT("dmat"), BT("bv128")), writes=(BT("lnt0"),), osz=D)
                P.add("V", lambda v, xn=xn: v.tensor_tensor(
                    xn[:, :], xn[:, :], lnt[0][:, :], ALU.add),
                    reads=(bxn, BT("lnt0")), writes=(bxn,), osz=D)
                dma("Y", xres[i * 128:(i + 1) * 128, :], xn[:, :],
                    reads=(bxn,), writes=(xres_b[i],), tag="ssmr4")

        DUMP = float(E * CEXP)

        def moe_setup():
            w1p = [AR(11 * 2048, 1024), AR(11 * 2048 + 1024, 1024)]
            w2p = [AR(15 * 2048, 1024), AR(15 * 2048 + 1024, 1024)]
            for e in range(E):
                for fc in range(FC):
                    k = e * FC + fc
                    wa, wa_r = w1p[k % 2]
                    wb, wb_r = w2p[k % 2]
                    P.add("G", lambda g, wa=wa, e=e, fc=fc: g.dma_start(
                        out=wa[:, :],
                        in_=bass.AP(w["l0f_W1"].tensor, e * D * F_MOE + fc * 128,
                                    [[F_MOE, 128], [128 * F_MOE, 8], [1, 128]])),
                        dma=True, reads=(wa_r,), writes=(wa_r,), tag="w1p")
                    P.add("G", lambda g, wb=wb, e=e, fc=fc: g.dma_start(
                        out=wb[:, :],
                        in_=bass.AP(w["l0f_W2"].tensor, e * F_MOE * D + fc * 128 * D,
                                    [[D, 128], [1, D]])),
                        dma=True, reads=(wb_r,), writes=(wb_r,), tag="w2p")
                    P.add("Y", lambda y, wa=wa, k=k: y.dma_start(
                        out=w1bf_d[k * 128:(k + 1) * 128, :], in_=wa[:, :]),
                        dma=True, reads=(wa_r,), writes=(BT("w1bf"),), tag="w1ps")
                    P.add("Y", lambda y, wb=wb, k=k: y.dma_start(
                        out=w2bf_d[k * 128:(k + 1) * 128, :], in_=wb[:, :]),
                        dma=True, reads=(wb_r,), writes=(BT("w2bf"),), tag="w2ps")
            dma("Y", wg_sb[:, :, :],
                bass.AP(w["l0f_Wg"].tensor, 0, [[8, 128], [1024, 8], [1, 8]]),
                writes=(BT("wg"),), tag="wgld")
            dma("Y", gmt[:, :], gmat_t, writes=(BT("gmt"),))
            dma("Y", utri[:, :], utri_c, writes=(BT("utri"),))
            dma("Y", utri64[:, :], utri64_c, writes=(BT("utri64"),))
            zt = xnew[0]
            zop = P.add("V", lambda v: v.memset(zt[:, :], 0.0), writes=(BT("xnew0"),))
            for r in range(0, E * CEXP + 128, 128):
                n = min(128, E * CEXP + 128 - r)
                P.add("G", lambda g, r=r, n=n: g.dma_start(
                    out=xsort_d[r:r + n, 0:D], in_=zt[0:n, :]),
                    dma=True, reads=(BT("xnew0"),), writes=(BT("xs_z"),), tag="xsz")
                P.add("G", lambda g, r=r, n=n: g.dma_start(
                    out=xsort_d[r:r + n, D:D + 64], in_=zt[0:n, 0:64]),
                    dma=True, reads=(BT("xnew0"),), writes=(BT("xs_z"),), tag="xsz2")
            P.add("V", lambda v: v.memset(ybf[0][0:1, :], 0.0), writes=(BT("ybf0"),))
            P.add("G", lambda g: g.dma_start(out=ysort_d[E * CEXP:E * CEXP + 1, :],
                  in_=ybf[0][0:1, :]), dma=True, reads=(BT("ybf0"),),
                  writes=(BT("ys_z"),), tag="ysz")

        def moe_phase(phase):
            pbis, pbis_r = AR(0, 8192, F32)
            bisj, bisj_r = AR(4 * 2048, 4096)
            mxT = [AR(6 * 2048, 2048, F32), AR(7 * 2048, 2048, F32)]
            probs_all, probs_r = AR(14 * 2048, 1024, F32)
            gate_all, gate_r = AR(14 * 2048 + 1024, 128, F32)
            dest_all, dest_r = AR(14 * 2048 + 1152, 128, F32)
            bpm, bpm_r = AR(15 * 2048, 128, F32)
            bem, bem_r = AR(15 * 2048 + 128, 128, F32)
            bpu, bpu_r = AR(15 * 2048 + 256, 128, F32)
            beu, beu_r = AR(15 * 2048 + 384, 128, F32)
            meq, meq_r = AR(15 * 2048 + 512, 128, F32)
            sc1, sc1_r = AR(15 * 2048 + 576, 128, F32)
            sc2, sc2_r = AR(15 * 2048 + 704, 128, F32)
            rnk, rnk_r = AR(15 * 2048 + 832, 128, F32)
            sc3, sc3_r = AR(15 * 2048 + 960, 128, F32)
            ln_stats_pass(phase, with_absmax=False)
            for i in range(NT):
                sl, bsl = xt[i % 2], BT(f"xt{i % 2}")
                dma("Y", sl[:, :], xres[i * 128:(i + 1) * 128, :],
                    reads=(xres_b[i],), writes=(bsl,), tag="mrout")
                P.add("S", lambda s, sl=sl, i=i: s.activation(
                    lnt[0][:, :], sl[:, :], AF.Identity,
                    bias=stats[:, 2 * i + 1:2 * i + 2], scale=stats[:, 2 * i:2 * i + 1]),
                    reads=(bsl, BT(f"st{i}")), writes=(BT("lnt0"),), osz=D)
                dma("Y", lnx_ext[i * 128:(i + 1) * 128, 0:D], lnt[0][:, :],
                    reads=(BT("lnt0"),), writes=(BT(f"lnx{i}"),), tag="mlnst")

                mx, mx_r = mxT[i % 2]
                mxv = mx.rearrange("p (a b) -> p a b", a=DC)
                for dc in range(DC):
                    pst, bpst = (psT, BT("ps2")) if dc % 2 == 0 else (psU, BT("ps3"))
                    P.add("T", lambda t, dc=dc, pst=pst: t.transpose(
                        pst[:, :], lnt[0][:, dc * 128:(dc + 1) * 128], idn_f32[:, :]),
                        reads=(BT("lnt0"), BT("idnf")), writes=(bpst,), osz=16384)
                    P.add("V" if dc % 2 == 0 else "S",
                          (lambda v, dc=dc, pst=pst, mxv=mxv: v.tensor_copy(
                              mxv[:, dc, :], pst[:, :])) if dc % 2 == 0 else
                          (lambda s, dc=dc, pst=pst, mxv=mxv: s.copy(
                              mxv[:, dc, :], pst[:, :])),
                          reads=(bpst,), writes=(mx_r,), osz=16384)
                for dc in range(DC):
                    P.add("T", lambda t, dc=dc, mxv=mxv: t.matmul(
                        PSP[1][:, 0:8], mxv[:, dc, :], wg_sb[:, dc, :],
                        start=(dc == 0), stop=(dc == DC - 1)),
                        reads=(mx_r, BT("wg")), writes=(BT("ps1"),), osz=8)
                P.add("V", lambda v: v.reduce_max(tiny[:, 9:10], PSP[1][:, 0:8], AX.X),
                      reads=(BT("ps1"),), writes=(BT("tiny"),))
                P.add("V", lambda v: v.tensor_scalar(tiny[:, 10:11], tiny[:, 9:10],
                      -1.0, None, ALU.mult), reads=(BT("tiny"),), writes=(BT("tiny"),))
                P.add("S", lambda s, i=i: s.activation(
                    probs_all[:, i * 8:(i + 1) * 8], PSP[1][:, 0:8], AF.Exp,
                    bias=tiny[:, 10:11]),
                    reads=(BT("ps1"), BT("tiny")), writes=(probs_r,), osz=8)
                P.add("V", lambda v, i=i: v.reduce_sum(tiny[:, 11:12],
                      probs_all[:, i * 8:(i + 1) * 8], AX.X),
                      reads=(probs_r,), writes=(BT("tiny"),))
                P.add("V", lambda v: v.reciprocal(tiny[:, 12:13], tiny[:, 11:12]),
                      reads=(BT("tiny"),), writes=(BT("tiny"),))
                P.add("V", lambda v, i=i: v.tensor_scalar(
                    probs_all[:, i * 8:(i + 1) * 8], probs_all[:, i * 8:(i + 1) * 8],
                    tiny[:, 12:13], None, ALU.mult),
                    reads=(probs_r, BT("tiny")), writes=(probs_r,), osz=8)
                if i % 16 == 15:
                    i0 = i - 15
                    P.add("T", lambda t, i0=i0: t.transpose(
                        PSP[2][:, 0:128], probs_all[:, i0 * 8:(i0 + 16) * 8],
                        idn_f32[:, :]),
                        reads=(probs_r, BT("idnf")), writes=(BT("ps2"),), osz=16384)
                    P.add("V", lambda v: v.tensor_copy(p8s[:, :], PSP[2][:, 0:128]),
                          reads=(BT("ps2"),), writes=(BT("p8s"),), osz=16384)
                    P.add("G", lambda g, i0=i0: g.dma_start(
                        out=bass.AP(prbT.tensor, i0 * 128,
                                    [[128, 16], [T, 8], [1, 128]]),
                        in_=p8s[:, :]),
                        dma=True, reads=(BT("p8s"),), writes=(BT("prbT"),), tag="prbst")
            if phase == "m1":
                dma("Y", sb["dbgt2"][3:4, 0:64], lnx_ext[0:1, 0:64],
                    reads=(BT("lnx0"),), writes=(BT("dbgt2pre"),), tag="dbgpre")
            cc = P.add("G", lambda g: g.collective_compute(
                "AllGather", ALU.bypass, [list(range(NCORES))],
                [prbT.opt()], [prbA.opt()]),
                reads=(BT("prbT"),), writes=(BT("prbA"),), coll=True)
            for e in range(E):
                P.add("Y", lambda y, e=e: y.dma_start(
                    out=pbis[e * 16:(e + 1) * 16, :],
                    in_=bass.AP(prbA.tensor, e * T,
                                [[E * T, 8], [T // 2, 2], [1, T // 2]])),
                    dma=True, reads=(BT("prbA"),), writes=(pbis_r,), tag="pbld")
            P.add("V", lambda v: v.memset(mid128[:, :], 0.5), writes=(BT("mid128"),))
            P.add("V", lambda v: v.memset(lo8[:, 0:1], 0.0), writes=(BT("lo8"),))
            P.add("V", lambda v: v.memset(lo8[:, 1:2], 1.0), writes=(BT("lo8"),))
            P.add("V", lambda v: v.memset(lo8[:, 2:3], 0.5), writes=(BT("lo8"),))
            for it in range(BISECT_ITERS):
                msrc = mid128[:, 0:1] if it == 0 else PSP[3][:, 0:1]
                mb = BT("mid128") if it == 0 else BT("ps3")
                P.add("V", lambda v, msrc=msrc: v.tensor_scalar(
                    bisj[:, :], pbis[:, :], msrc, 0.0, ALU.is_ge, op1=ALU.add,
                    accum_out=tiny[:, 8:9]),
                    reads=(pbis_r, mb), writes=(bisj_r, BT("tiny")), osz=4096)
                P.add("T", lambda t: t.matmul(PSP[1][0:8, 0:1], cons[:, 0:8],
                      tiny[:, 8:9], start=True, stop=True),
                      reads=(BT("tiny"), BT("cons")), writes=(BT("ps1"),))
                P.add("V", lambda v: v.tensor_scalar(ge8[:, 0:1], PSP[1][0:8, 0:1],
                      float(E * CEXP * 0 + 8192), None, ALU.is_ge),
                      reads=(BT("ps1"),), writes=(BT("ge8"),))
                P.add("V", lambda v: v.tensor_tensor(lo8[:, 3:4], lo8[:, 2:3],
                      lo8[:, 0:1], ALU.subtract), reads=(BT("lo8"),), writes=(BT("lo8"),))
                P.add("V", lambda v: v.scalar_tensor_tensor(lo8[:, 0:1], ge8[:, 0:1],
                      lo8[:, 3:4], lo8[:, 0:1], ALU.mult, ALU.add),
                      reads=(BT("ge8"), BT("lo8")), writes=(BT("lo8"),))
                P.add("V", lambda v: v.tensor_scalar(ge8[:, 1:2], ge8[:, 0:1],
                      -1.0, 1.0, ALU.mult, op1=ALU.add),
                      reads=(BT("ge8"),), writes=(BT("ge8"),))
                P.add("V", lambda v: v.tensor_tensor(lo8[:, 3:4], lo8[:, 2:3],
                      lo8[:, 1:2], ALU.subtract), reads=(BT("lo8"),), writes=(BT("lo8"),))
                P.add("V", lambda v: v.scalar_tensor_tensor(lo8[:, 1:2], ge8[:, 1:2],
                      lo8[:, 3:4], lo8[:, 1:2], ALU.mult, ALU.add),
                      reads=(BT("ge8"), BT("lo8")), writes=(BT("lo8"),))
                P.add("V", lambda v: v.tensor_tensor(lo8[:, 3:4], lo8[:, 0:1],
                      lo8[:, 1:2], ALU.add), reads=(BT("lo8"),), writes=(BT("lo8"),))
                P.add("V", lambda v: v.tensor_scalar(lo8[:, 2:3], lo8[:, 3:4],
                      0.5, None, ALU.mult), reads=(BT("lo8"),), writes=(BT("lo8"),))
                P.add("T", lambda t: t.matmul(PSP[3][:, 0:1], gmt[:, :], lo8[:, 2:3],
                      start=True, stop=True),
                      reads=(BT("gmt"), BT("lo8")), writes=(BT("ps3"),))
            P.add("T", lambda t: t.transpose(PSP[2][0:1, 0:8], lo8[0:8, 0:1],
                  idn_f32[0:8, 0:8]), reads=(BT("lo8"), BT("idnf")), writes=(BT("ps2"),))
            P.add("V", lambda v: v.tensor_copy(tiny2[0:1, 0:8], PSP[2][0:1, 0:8]),
                  reads=(BT("ps2"),), writes=(BT("tiny2"),))
            P.add("T", lambda t: t.matmul(PSP[3][:, 0:8], onesr[0:1, :],
                  tiny2[0:1, 0:8], start=True, stop=True),
                  reads=(BT("tiny2"), BT("ones")), writes=(BT("ps3"),))
            P.add("V", lambda v: v.tensor_copy(thrB[:, :], PSP[3][:, 0:8]),
                  reads=(BT("ps3"),), writes=(BT("thrB"),))
            P.add("V", lambda v: v.memset(bpm[:, :], -1e30), writes=(bpm_r,))
            P.add("V", lambda v: v.memset(bem[:, :], 0.0), writes=(bem_r,))
            P.add("V", lambda v: v.memset(bpu[:, :], -1e30), writes=(bpu_r,))
            P.add("V", lambda v: v.memset(beu[:, :], 0.0), writes=(beu_r,))
            pav = probs_all.rearrange("p (t e) -> p t e", e=E)
            for e in range(E):
                pe = pav[:, :, e]
                P.add("V", lambda v, pe=pe, e=e: v.tensor_scalar(
                    sc1[:, 0:64], pe, thrB[:, e:e + 1], 1e30, ALU.is_ge, op1=ALU.mult),
                    reads=(probs_r, BT("thrB")), writes=(sc1_r,), osz=64)
                P.add("V", lambda v, pe=pe: v.scalar_tensor_tensor(
                    sc1[:, 0:64], sc1[:, 0:64], -1e30, pe, ALU.add, ALU.add),
                    reads=(sc1_r, probs_r), writes=(sc1_r,), osz=64)
                for (bp, bp_r, be, be_r, val, val_r) in (
                        (bpm, bpm_r, bem, bem_r, sc1[:, 0:64], sc1_r),
                        (bpu, bpu_r, beu, beu_r, pe, probs_r)):
                    P.add("V", lambda v, val=val, bp=bp: v.tensor_tensor(
                        sc2[:, 0:64], val, bp[:, 0:64], ALU.is_gt),
                        reads=(val_r, bp_r), writes=(sc2_r,), osz=64)
                    P.add("V", lambda v, val=val, bp=bp: v.tensor_tensor(
                        bp[:, 0:64], bp[:, 0:64], val, ALU.max),
                        reads=(bp_r, val_r), writes=(bp_r,), osz=64)
                    P.add("V", lambda v, be=be, e=e: v.tensor_scalar(
                        sc3[:, 0:64], be[:, 0:64], -1.0, float(e), ALU.mult, op1=ALU.add),
                        reads=(be_r,), writes=(sc3_r,), osz=64)
                    P.add("V", lambda v: v.tensor_tensor(sc3[:, 0:64], sc2[:, 0:64],
                          sc3[:, 0:64], ALU.mult), reads=(sc2_r, sc3_r),
                          writes=(sc3_r,), osz=64)
                    P.add("V", lambda v, be=be: v.tensor_tensor(be[:, 0:64], be[:, 0:64],
                          sc3[:, 0:64], ALU.add), reads=(be_r, sc3_r),
                          writes=(be_r,), osz=64)
            P.add("V", lambda v: v.tensor_scalar(sc1[:, 0:64], bpm[:, 0:64],
                  -1e29, None, ALU.is_gt), reads=(bpm_r,), writes=(sc1_r,), osz=64)
            for (dst, dst_r, a, a_r, b_, b_r) in (
                    (gate_all, gate_r, bpm, bpm_r, bpu, bpu_r),
                    (dest_all, dest_r, bem, bem_r, beu, beu_r)):
                P.add("V", lambda v, a=a, b_=b_: v.tensor_tensor(sc2[:, 0:64],
                      a[:, 0:64], b_[:, 0:64], ALU.subtract),
                      reads=(a_r, b_r), writes=(sc2_r,), osz=64)
                P.add("V", lambda v: v.tensor_tensor(sc2[:, 0:64], sc1[:, 0:64],
                      sc2[:, 0:64], ALU.mult), reads=(sc1_r, sc2_r),
                      writes=(sc2_r,), osz=64)
                P.add("V", lambda v, dst=dst, b_=b_: v.tensor_tensor(dst[:, 0:64],
                      b_[:, 0:64], sc2[:, 0:64], ALU.add),
                      reads=(b_r, sc2_r), writes=(dst_r,), osz=64)
            P.add("V", lambda v: v.memset(rnk[:, 0:64], 0.0), writes=(rnk_r,))
            for e in range(E):
                P.add("V", lambda v, e=e: v.tensor_scalar(meq[:, 0:64],
                      dest_all[:, 0:64], float(e), None, ALU.is_equal),
                      reads=(dest_r,), writes=(meq_r,), osz=64)
                P.add("T", lambda t: t.matmul(PSP[0][:, 0:64], utri[:, :],
                      meq[:, 0:64], start=True, stop=True),
                      reads=(meq_r, BT("utri")), writes=(BT("ps0"),), osz=64)
                P.add("T", lambda t: t.matmul(PSP[2][0:1, 0:64], cons[:, 16:17],
                      meq[:, 0:64], start=True, stop=True),
                      reads=(meq_r, BT("cons")), writes=(BT("ps2"),), osz=64)
                P.add("V", lambda v: v.tensor_copy(sc2[0:1, 0:64], PSP[2][0:1, 0:64]),
                      reads=(BT("ps2"),), writes=(sc2_r,), osz=64)
                P.add("T", lambda t: t.transpose(PSP[2][0:64, 0:1], sc2[0:1, 0:64],
                      idn_f32[0:1, 0:1]), reads=(sc2_r, BT("idnf")),
                      writes=(BT("ps2"),), osz=64)
                P.add("V", lambda v: v.tensor_copy(p8s[0:64, 0:1], PSP[2][0:64, 0:1]),
                      reads=(BT("ps2"),), writes=(BT("p8s"),), osz=64)
                P.add("T", lambda t: t.matmul(PSP[2][0:64, 1:2], utri64[:, :],
                      p8s[0:64, 0:1], start=True, stop=True),
                      reads=(BT("p8s"), BT("utri64")), writes=(BT("ps2"),), osz=64)
                P.add("V", lambda v: v.tensor_copy(p8s[0:64, 1:2], PSP[2][0:64, 1:2]),
                      reads=(BT("ps2"),), writes=(BT("p8s"),), osz=64)
                P.add("T", lambda t: t.transpose(PSP[2][0:1, 8:72], p8s[0:64, 1:2],
                      idn_f32[0:64, 0:64]), reads=(BT("p8s"), BT("idnf")),
                      writes=(BT("ps2"),), osz=64)
                P.add("V", lambda v: v.tensor_copy(sc2[0:1, 0:64], PSP[2][0:1, 8:72]),
                      reads=(BT("ps2"),), writes=(sc2_r,), osz=64)
                P.add("T", lambda t: t.matmul(PSP[0][:, 0:64], onesr[0:1, :],
                      sc2[0:1, 0:64], start=False, stop=True),
                      reads=(sc2_r, BT("ones")), writes=(BT("ps0"),), osz=64)
                P.add("V", lambda v: v.tensor_tensor(sc3[:, 0:64], meq[:, 0:64],
                      PSP[0][:, 0:64], ALU.mult), reads=(meq_r, BT("ps0")),
                      writes=(sc3_r,), osz=64)
                P.add("V", lambda v: v.tensor_tensor(rnk[:, 0:64], rnk[:, 0:64],
                      sc3[:, 0:64], ALU.add), reads=(rnk_r, sc3_r),
                      writes=(rnk_r,), osz=64)
            P.add("V", lambda v: v.scalar_tensor_tensor(dest_all[:, 0:64],
                  dest_all[:, 0:64], float(CEXP), rnk[:, 0:64], ALU.mult, ALU.add),
                  reads=(dest_r, rnk_r), writes=(dest_r,), osz=64)
            P.add("V", lambda v: v.tensor_scalar(sc1[:, 0:64], rnk[:, 0:64],
                  float(CEXP), None, ALU.is_ge), reads=(rnk_r,), writes=(sc1_r,), osz=64)
            P.add("V", lambda v: v.tensor_scalar(sc2[:, 0:64], sc1[:, 0:64],
                  -1.0, 1.0, ALU.mult, op1=ALU.add), reads=(sc1_r,), writes=(sc2_r,), osz=64)
            P.add("V", lambda v: v.tensor_tensor(dest_all[:, 0:64], dest_all[:, 0:64],
                  sc2[:, 0:64], ALU.mult), reads=(dest_r, sc2_r), writes=(dest_r,), osz=64)
            P.add("V", lambda v: v.scalar_tensor_tensor(dest_all[:, 0:64], sc1[:, 0:64],
                  DUMP, dest_all[:, 0:64], ALU.mult, ALU.add),
                  reads=(sc1_r, dest_r), writes=(dest_r,), osz=64)
            if phase == "m1":
                P.add("Y", lambda y: y.dma_start(
                    out=dbg[4:5, 96:160], in_=lnx_ext[0:1, 0:64]),
                    dma=True, deps=(P.ops[-1],), reads=(BT("lnx0"), BT("thrB")),
                    tag="dbgmid")
            for i in range(NT):
                dma("Y", xls[:, 0:D], lnx_ext[i * 128:(i + 1) * 128, 0:D],
                    deps=[o for o in bufs[f"lnx{i}"].rdeps()],
                    reads=(BT(f"lnx{i}"),), writes=(BT("xls"),), tag="msc_ld")
                P.add("V", lambda v, i=i: v.tensor_copy(
                    xls[:, D:D + 1], gate_all[:, i:i + 1]),
                    reads=(gate_r,), writes=(BT("xls"),), osz=64)
                di = dI[i % 2]
                P.add("V", lambda v, di=di, i=i: v.tensor_copy(
                    di[:, 0:1], dest_all[:, i:i + 1]),
                    reads=(dest_r,), writes=(BT(f"dI{i % 2}"),), osz=64)
                P.add("G", lambda g, di=di: g.indirect_dma_start(
                    xsort_d[:, :], IndirectOffsetOnAxis(di[:, 0:1], 0),
                    xls[:, :], None),
                    dma=True, reads=(BT("xls"), BT(f"dI{i % 2}"), BT("xs_z")),
                    writes=(BT("xsort"),), tag="mscat")
            xl = [AR(0, 2176, F32), AR(4 * 2048 + 2048, 2176, F32)]
            xeg = [AR(2 * 2048, 2048), AR(3 * 2048, 2048)]
            w1c = [AR(6 * 2048, 1024), AR(6 * 2048 + 1024, 1024)]
            w2c = [AR(7 * 2048, 1024), AR(7 * 2048 + 1024, 1024)]
            hb = [AR(8 * 2048, 256), AR(8 * 2048 + 256, 256)]
            ysr = [AR(9 * 2048, 1024), AR(9 * 2048 + 1024, 1024)]
            NGRP = CEXP // 256
            for e in range(E):
                for grp in range(NGRP):
                    xg, xg_r = xeg[grp % 2]
                    xgv = xg.rearrange("p (a b) -> p a b", a=DC)
                    gs, gs_r = gsort[grp % 2], BT(f"gsort{grp % 2}")
                    for t2 in range(2):
                        slot = e * CEXP + grp * 256 + t2 * 128
                        xla, xla_r = xl[t2]
                        dma("Y", xla[:, :], xsort_d[slot:slot + 128, :],
                            deps=[o for o in bufs["xsort"].rdeps()],
                            reads=(BT("xsort"),), writes=(xla_r,), tag="mxld")
                        P.add("V", lambda v, gs=gs, xla=xla, t2=t2: v.tensor_copy(
                            gs[:, t2:t2 + 1], xla[:, D:D + 1]),
                            reads=(xla_r,), writes=(gs_r,), osz=64)
                        for dc in range(DC):
                            pst, bpst = (psT, BT("ps2")) if dc % 2 == 0 else (psU, BT("ps3"))
                            P.add("T", lambda t, dc=dc, pst=pst, xla=xla: t.transpose(
                                pst[:, :], xla[:, dc * 128:(dc + 1) * 128], idn_f32[:, :]),
                                reads=(xla_r, BT("idnf")), writes=(bpst,), osz=16384)
                            P.add("V" if dc % 2 == 0 else "S",
                                  (lambda v, dc=dc, pst=pst, xgv=xgv, t2=t2: v.tensor_copy(
                                      xgv[:, dc, t2 * 128:(t2 + 1) * 128], pst[:, :]))
                                  if dc % 2 == 0 else
                                  (lambda s, dc=dc, pst=pst, xgv=xgv, t2=t2: s.copy(
                                      xgv[:, dc, t2 * 128:(t2 + 1) * 128], pst[:, :])),
                                  reads=(bpst,), writes=(xg_r,), osz=16384)
                    for fc in range(FC + 1):
                        if fc < FC:
                            wa, wa_r = w1c[fc % 2]
                            wav = wa.rearrange("p (a b) -> p a b", a=DC)
                            wb, wb_r = w2c[fc % 2]
                            kwc = e * FC + fc
                            P.add("G", lambda g, wa=wa, kwc=kwc: g.dma_start(
                                out=wa[:, :], in_=w1bf_d[kwc * 128:(kwc + 1) * 128, :]),
                                dma=True, reads=(BT("w1bf"),), writes=(wa_r,), tag="w1ld")
                            P.add("G", lambda g, wb=wb, kwc=kwc: g.dma_start(
                                out=wb[:, :], in_=w2bf_d[kwc * 128:(kwc + 1) * 128, :]),
                                dma=True, reads=(BT("w2bf"),), writes=(wb_r,), tag="w2ld")
                            ph = "ab"[fc % 2]
                            psl = PSP[0][:, (fc % 2) * 512:(fc % 2) * 512 + 256]
                            for dc in range(DC):
                                P.add("T", lambda t, dc=dc, wav=wav, xgv=xgv, psl=psl: t.matmul(
                                    psl, wav[:, dc, :], xgv[:, dc, :],
                                    start=(dc == 0), stop=(dc == DC - 1)),
                                    reads=(wa_r, xg_r), writes=(BT(f"ps0{ph}"),), osz=256)
                        if fc >= 1:
                            pwb, pwb_r = w2c[(fc - 1) % 2]
                            phb, phb_r = hb[(fc - 1) % 2]
                            for t2 in range(2):
                                for nh_ in range(2):
                                    P.add("T", lambda t, phb=phb, pwb=pwb, t2=t2, nh_=nh_, fc=fc: t.matmul(
                                        PSP[2 + t2][:, nh_ * 512:(nh_ + 1) * 512],
                                        phb[:, t2 * 128:(t2 + 1) * 128],
                                        pwb[:, nh_ * 512:(nh_ + 1) * 512],
                                        start=(fc - 1 == 0), stop=(fc - 1 == FC - 1)),
                                        reads=(phb_r, pwb_r), writes=(BT(f"ps{2 + t2}"),), osz=512)
                        if fc < FC:
                            hbb, hb_r = hb[fc % 2]
                            P.add("S", lambda s, hbb=hbb, psl=psl: s.activation(
                                hbb[:, :], psl, AF.Gelu),
                                reads=(BT(f"ps0{ph}"),), writes=(hb_r,), osz=256)
                    for t2 in range(2):
                        yy, yy_r = ysr[t2]
                        P.add("V", lambda v, yy=yy, gs=gs, t2=t2: v.tensor_scalar(
                            yy[:, :], PSP[2 + t2][:, :], gs[:, t2:t2 + 1], None, ALU.mult),
                            reads=(BT(f"ps{2 + t2}"), gs_r), writes=(yy_r,), osz=1024)
                        slot = e * CEXP + grp * 256 + t2 * 128
                        P.add("Y", lambda y, yy=yy, slot=slot: y.dma_start(
                            out=ysort_d[slot:slot + 128, :], in_=yy[:, :]),
                            dma=True, reads=(yy_r,), writes=(BT("ysort"),), tag="myst")
            for i in range(NT):
                sl, bsl = xt[i % 2], BT(f"xt{i % 2}")
                dma("Y", sl[:, :], xres[i * 128:(i + 1) * 128, :],
                    reads=(xres_b[i],), writes=(bsl,), tag="mrld")
                di = dI[i % 2]
                P.add("V", lambda v, di=di, i=i: v.tensor_copy(
                    di[:, 0:1], dest_all[:, i:i + 1]),
                    reads=(dest_r,), writes=(BT(f"dI{i % 2}"),), osz=64)
                yb, byb = ybf[i % 2], BT(f"ybf{i % 2}")
                P.add("G", lambda g, yb=yb, di=di: g.indirect_dma_start(
                    yb[:, :], None, ysort_d[:, :],
                    IndirectOffsetOnAxis(di[:, 0:1], 0)),
                    dma=True, reads=(BT(f"dI{i % 2}"), BT("ysort"), BT("ys_z")),
                    writes=(byb,), tag="mgat")
                xn, bxn = xnew[i % 2], BT(f"xnew{i % 2}")
                P.add("V", lambda v, yb=yb, sl=sl, xn=xn: v.scalar_tensor_tensor(
                    xn[:, :], yb[:, :], 0.5, sl[:, :], ALU.mult, ALU.add),
                    reads=(byb, bsl), writes=(bxn,), osz=D)
                dma("Y", xres[i * 128:(i + 1) * 128, :], xn[:, :],
                    reads=(bxn,), writes=(xres_b[i],), tag="mrst")

        def attn_phase():
            prep_wT(wT["q"], w["l1a_wq"], "wq")
            prep_wT(wT["k"], w["l1a_wk"], "wk")
            prep_wT(wT["v"], w["l1a_wv"], "wv")
            dma("Y", rtc[:, :], rt_c, writes=(BT("rtc"),))
            dma("Y", ropes[:, 0:S], rope_c[:, 0:S], writes=(ropes_r,))
            dma("Y", ropes[:, S:2 * S], rope_c[:, 2 * S:3 * S], writes=(ropes_r,))
            for ch in range(4):
                lodma = dma("Y", lnt[0][:, :],
                            rope_c[:, S + (ch % 2) * 1024 + (ch // 2) * 2 * S:
                                   S + (ch % 2) * 1024 + (ch // 2) * 2 * S + 1024],
                            writes=(BT("lnt0"),), tag="ropelo")
                P.add("V", lambda v, ch=ch: v.tensor_tensor(
                    ropes[:, ch * 1024:(ch + 1) * 1024],
                    ropes[:, ch * 1024:(ch + 1) * 1024], lnt[0][:, :], ALU.add),
                    reads=(ropes_r, BT("lnt0")), writes=(ropes_r,), osz=1024)
            for j, nm in ((0, "l1a_bq"), (1, "l1a_bk")):
                P.add("Y", lambda y, j=j, nm=nm: y.dma_start(
                    out=bqk[:, j * 8:(j + 1) * 8],
                    in_=bass.AP(w[nm].tensor, 0, [[1, 128], [128, 8]]),
                    allow_slow_non_contiguous=True),
                    dma=True, writes=(BT("bqk"),))
            for nm, dstt, bb in (("l1a_bv", bv128, "bv128"), ("l1a_bo", bo128, "bo128")):
                ldb = dma("G", ropt2[0][0:1, 0:D], w[nm], writes=(BT("ropt2_0"),))
                for hb in range(2):
                    P.add("T", lambda t, hb=hb: t.matmul(
                        psB[0][:, hb * 512:(hb + 1) * 512], onesr[0:1, :],
                        ropt2[0][0:1, hb * 512:hb * 512 + 512], start=True, stop=True),
                        reads=(BT("ropt2_0"), BT("ones")), writes=(BT("ps0"),), osz=512)
                P.add("V", lambda v, dstt=dstt: v.tensor_copy(dstt[:, :], psB[0][:, 0:D]),
                      reads=(BT("ps0"),), writes=(BT(bb),), osz=D)
            ln_stats_pass("attn", with_absmax=False)
            for i in range(NT):
                sl, bsl = xt[i % 2], BT(f"xt{i % 2}")
                lb, blb = ybf[i % 2], BT(f"ybf{i % 2}")
                dma("Y", sl[:, :], xres[i * 128:(i + 1) * 128, :],
                    reads=(xres_b[i],), writes=(bsl,), tag="attnld")
                P.add("S", lambda s, sl=sl, lb=lb, i=i: s.activation(
                    lb[:, :], sl[:, :], AF.Identity,
                    bias=stats[:, 2 * i + 1:2 * i + 2], scale=stats[:, 2 * i:2 * i + 1]),
                    reads=(bsl, BT(f"st{i}")), writes=(blb,), osz=D)
                dma("Y", lnbf[i * 128:(i + 1) * 128, :], lb[:, :],
                    reads=(blb,), writes=(BT(f"lnbf{i}"),), tag="attnst")
            NBLK = T // 1024
            for blk in range(NBLK):
                lt, blt = lnaT[0], lnaT_r
                lnbf_deps = [o for i in range(blk * 8, blk * 8 + 8)
                             for o in bufs[f"lnbf{i}"].rdeps()]
                for dc in range(DC):
                    P.add("Y", lambda y, blk=blk, dc=dc, lt=lt: y.dma_start(
                        out=lt[:, dc, :],
                        in_=lnbf[blk * 1024:(blk + 1) * 1024, dc * 128:(dc + 1) * 128],
                        transpose=True),
                        dma=True, writes=(blt,), deps=lnbf_deps, tag="lnaTld")
                cb = (blk % 2) * 1024
                for j, (nm, rowbase) in enumerate((("q", 0), ("k", D))):
                    for ic in range(DC):
                        ps, bps = psB[ic % 2], BT(f"ps{ic % 2}")
                        for dc in range(DC):
                            for nh_ in range(2):
                                P.add("T", lambda t, nm=nm, ic=ic, dc=dc, ps=ps, lt=lt, nh_=nh_: t.matmul(
                                    ps[:, nh_ * 512:(nh_ + 1) * 512],
                                    wT[nm][:, dc * D + ic * 128: dc * D + ic * 128 + 128],
                                    lt[:, dc, nh_ * 512:(nh_ + 1) * 512],
                                    start=(dc == 0), stop=(dc == DC - 1)),
                                    reads=(blt, BT(f"w{nm}_w")), writes=(bps,), osz=512)
                        qs, bqs = qstg[ic % 2], BT(f"qstg{ic % 2}")
                        P.add("V", lambda v, ps=ps, qs=qs, j=j, ic=ic: v.tensor_scalar(
                            qs[:, :], ps[:, :], bqk[:, j * 8 + ic:j * 8 + ic + 1],
                            None, ALU.add),
                            reads=(bps, BT("bqk")), writes=(bqs,), osz=1024)
                        for nh_ in range(2):
                            P.add("T", lambda t, qs=qs, nh_=nh_: t.matmul(
                                PSP[2][:, nh_ * 512:(nh_ + 1) * 512], rtc[:, :],
                                qs[:, nh_ * 512:(nh_ + 1) * 512], start=True, stop=True),
                                reads=(bqs, BT("rtc")), writes=(BT("ps2"),), osz=512)
                        t1, bt1 = ropt1[0], BT("ropt1_0")
                        t2, bt2 = ropt2[0], BT("ropt2_0")
                        P.add("V", lambda v, qs=qs, t1=t1, cb=cb: v.tensor_tensor(
                            t1[:, :], qs[:, :], ropes[:, cb:cb + 1024], ALU.mult),
                            reads=(bqs, ropes_r), writes=(bt1,), osz=1024)
                        P.add("V", lambda v, t2=t2, cb=cb: v.tensor_tensor(
                            t2[:, :], PSP[2][:, :], ropes[:, S + cb:S + cb + 1024],
                            ALU.mult), reads=(BT("ps2"), ropes_r), writes=(bt2,), osz=1024)
                        qr, bqr = qrt[ic % 2], BT(f"qrt{ic % 2}")
                        P.add("V", lambda v, qr=qr, t1=t1, t2=t2: v.tensor_tensor(
                            qr[:, :], t1[:, :], t2[:, :], ALU.add),
                            reads=(bt1, bt2), writes=(bqr,), osz=1024)
                        P.add("Y", lambda y, qr=qr, rowbase=rowbase, ic=ic, blk=blk: y.dma_start(
                            out=qkT[rowbase + ic * 128: rowbase + (ic + 1) * 128,
                                    blk * 1024:(blk + 1) * 1024], in_=qr[:, :]),
                            dma=True, reads=(bqr,), writes=(BT(f"qk{j}_{ic}_{blk}"),),
                            tag="qkst")
                for tt in range(8):
                    i = blk * 8 + tt
                    ps, bps = psB[tt % 2], BT(f"ps{tt % 2}")
                    for dc in range(DC):
                        for nh_ in range(2):
                            P.add("T", lambda t, dc=dc, tt=tt, ps=ps, lt=lt, nh_=nh_: t.matmul(
                                ps[:, nh_ * 512:(nh_ + 1) * 512],
                                lt[:, dc, tt * 128:(tt + 1) * 128],
                                wT["v"][:, dc * D + nh_ * 512: dc * D + nh_ * 512 + 512],
                                start=(dc == 0), stop=(dc == DC - 1)),
                                reads=(blt, BT("wv_w")), writes=(bps,), osz=512)
                    vb, bvb = qrt[tt % 2], BT(f"qrt{tt % 2}")
                    P.add("V", lambda v, ps=ps, vb=vb: v.tensor_tensor(
                        vb[:, :], ps[:, :], bv128[:, :], ALU.add),
                        reads=(bps, BT("bv128")), writes=(bvb,), osz=1024)
                    P.add("Y", lambda y, vb=vb, i=i: y.dma_start(
                        out=vtok[i * 128:(i + 1) * 128, :], in_=vb[:, :]),
                        dma=True, reads=(bvb,), writes=(BT(f"vtok{i}"),), tag="vst")
            prep_wT(wT["o"], w["l1a_wo"], "wo", extra=(BT("wq_w"),))
            for b in range(B):
                for h in range(NH):
                    sl = (b * NH + h) % 2
                    qt, bqt = qtb[sl], BT(f"qtb{sl}")
                    kt, bkt = ktb[sl], BT(f"ktb{sl}")
                    vt, bvt = vtb[sl], BT(f"vtb{sl}")
                    P.add("Y", lambda y, qt=qt, h=h, b=b: y.dma_start(
                        out=qt[:, :], in_=qkT[h * 64:(h + 1) * 64, b * S:(b + 1) * S]),
                        dma=True,
                        deps=[o for ic in (h // 2,) for blkk in range(b * 2, b * 2 + 2)
                              for o in bufs[f"qk0_{ic}_{blkk}"].rdeps()],
                        writes=(bqt,), tag="qld")
                    P.add("Y", lambda y, kt=kt, h=h, b=b: y.dma_start(
                        out=kt[:, :], in_=qkT[D + h * 64: D + h * 64 + 64, b * S:(b + 1) * S]),
                        dma=True,
                        deps=[o for ic in (h // 2,) for blkk in range(b * 2, b * 2 + 2)
                              for o in bufs[f"qk1_{ic}_{blkk}"].rdeps()],
                        writes=(bkt,), tag="kld")
                    P.add("V", lambda v, vt=vt: v.memset(vt[:, :, :], 1.0),
                          writes=(bvt,), osz=1040)
                    P.add("Y", lambda y, vt=vt, h=h, b=b: y.dma_start(
                        out=vt[:, :, 0:64],
                        in_=vtok[b * S:(b + 1) * S, h * 64:(h + 1) * 64]),
                        dma=True,
                        deps=[o for i in range(b * 16, b * 16 + 16)
                              for o in bufs[f"vtok{i}"].rdeps()],
                        reads=(bvt,), writes=(bvt,), tag="vld")
                    for ktile in range(16):
                        pt = ptile[0]
                        for qh in range(2):
                            for nh_ in range(2):
                                P.add("T", lambda t, kt=kt, qt=qt, ktile=ktile, qh=qh, nh_=nh_: t.matmul(
                                    PSP[qh][:, nh_ * 512:(nh_ + 1) * 512],
                                    kt[:, ktile * 128:(ktile + 1) * 128],
                                    qt[:, qh * 1024 + nh_ * 512: qh * 1024 + nh_ * 512 + 512],
                                    start=True, stop=True),
                                    reads=(bkt, bqt), writes=(BT(f"ps{qh}"),), osz=512)
                            P.add("S", lambda s, pt=pt, qh=qh: s.activation(
                                pt[:, qh * 1024:(qh + 1) * 1024], PSP[qh][:, :],
                                AF.Exp, scale=0.125),
                                reads=(BT(f"ps{qh}"),), writes=(BT(f"pt{qh}"),), osz=1024)
                        for qh in range(2):
                            for nh_ in range(2):
                                P.add("T", lambda t, vt=vt, pt=pt, ktile=ktile, qh=qh, nh_=nh_: t.matmul(
                                    PSP[2 + qh][0:65, nh_ * 512:(nh_ + 1) * 512],
                                    vt[:, ktile, :],
                                    pt[:, qh * 1024 + nh_ * 512: qh * 1024 + nh_ * 512 + 512],
                                    start=(ktile == 0), stop=(ktile == 15)),
                                    reads=(bvt, BT(f"pt{qh}")), writes=(BT(f"ps{2 + qh}"),), osz=512)
                    och, boch = otch[h // 2], BT(f"ARr{h // 2}")
                    hp = (h % 2) * 64
                    for qh in range(2):
                        P.add("V", lambda v, qh=qh: v.tensor_copy(
                            qstg[qh][0:64, :], PSP[2 + qh][0:64, :]),
                            reads=(BT(f"ps{2 + qh}"),), writes=(BT(f"qstg{qh}"),), osz=1024)
                        rc = ropt1[0] if qh == 0 else ropt2[0]
                        brc = BT("ropt1_0") if qh == 0 else BT("ropt2_0")
                        P.add("V", lambda v, qh=qh, rc=rc: v.reciprocal(
                            rc[0:1, :], PSP[2 + qh][64:65, :]),
                            reads=(BT(f"ps{2 + qh}"),), writes=(brc,), osz=1024)
                        for nh_ in range(2):
                            P.add("T", lambda t, qh=qh, rc=rc, nh_=nh_: t.matmul(
                                PSP[qh][0:64, nh_ * 512:(nh_ + 1) * 512], onesr[0:1, 0:64],
                                rc[0:1, nh_ * 512:(nh_ + 1) * 512], start=True, stop=True),
                                reads=(brc, BT("ones")), writes=(BT(f"ps{qh}"),), osz=512)
                        P.add("V", lambda v, och=och, hp=hp, qh=qh: v.tensor_tensor(
                            och[hp:hp + 64, qh * 1024:(qh + 1) * 1024],
                            qstg[qh][0:64, :],
                            PSP[qh][0:64, :], ALU.mult),
                            reads=(BT(f"qstg{qh}"), BT(f"ps{qh}")), writes=(boch,), osz=1024)
                for tt in range(16):
                    i = b * 16 + tt
                    ps, bps = psB[tt % 2], BT(f"ps{tt % 2}")
                    for ic in range(DC):
                        for nh_ in range(2):
                            P.add("T", lambda t, ic=ic, tt=tt, ps=ps, nh_=nh_: t.matmul(
                                ps[:, nh_ * 512:(nh_ + 1) * 512],
                                otch[ic][:, tt * 128:(tt + 1) * 128],
                                wT["o"][:, ic * D + nh_ * 512: ic * D + nh_ * 512 + 512],
                                start=(ic == 0), stop=(ic == DC - 1)),
                                reads=(BT(f"ARr{ic}"), BT("wo_w")), writes=(bps,), osz=512)
                    sl2, bsl2 = xt[tt % 2], BT(f"xt{tt % 2}")
                    dma("Y", sl2[:, :], xres[i * 128:(i + 1) * 128, :],
                        reads=(xres_b[i],), writes=(bsl2,), tag="oxld")
                    yb, byb = ybf[tt % 2], BT(f"ybf{tt % 2}")
                    P.add("V", lambda v, ps=ps, yb=yb: v.tensor_tensor(
                        yb[:, :], ps[:, :], bo128[:, :], ALU.add),
                        reads=(bps, BT("bo128")), writes=(byb,), osz=D)
                    xn, bxn = xnew[tt % 2], BT(f"xnew{tt % 2}")
                    P.add("V", lambda v, yb=yb, sl2=sl2, xn=xn: v.tensor_tensor(
                        xn[:, :], yb[:, :], sl2[:, :], ALU.add),
                        reads=(byb, bsl2), writes=(bxn,), osz=D)
                    dma("Y", xres[i * 128:(i + 1) * 128, :], xn[:, :],
                        reads=(bxn,), writes=(xres_b[i],), tag="oxst")

        if "ssm" in enable:
            ssm_setup()
        if "moe1" in enable or "moe2" in enable:
            moe_setup()
        if "moe1" in enable:
            moe_phase("m1")
        if "ssm" in enable:
            ssm_phase()
        if "moe2" in enable:
            moe_phase("m2")
        if "fp8a" in enable:
            fp8_phase("fa")
        if "attn" in enable:
            attn_phase()
        if "fp8b" in enable:
            fp8_phase("fb")

        if "moe1" in enable or "moe2" in enable:
            pav_d, pav_dr = AR(14 * 2048, 1024, F32)
            gate_d, gate_dr = AR(14 * 2048 + 1024, 128, F32)
            dest_d, dest_dr = AR(14 * 2048 + 1152, 128, F32)
            bpm_d, bpm_dr = AR(15 * 2048, 128, F32)
            bem_d, bem_dr = AR(15 * 2048 + 128, 128, F32)
            bpu_d, bpu_dr = AR(15 * 2048 + 256, 128, F32)
            beu_d, beu_dr = AR(15 * 2048 + 384, 128, F32)
            rnk_d, rnk_dr = AR(15 * 2048 + 832, 128, F32)
            P.add("V", lambda v: v.tensor_copy(sb["dbgt"][:, 0:8], pav_d[:, 0:8]),
                  reads=(pav_dr,), writes=(BT("dbgt"),))
            P.add("V", lambda v: v.tensor_copy(sb["dbgt"][:, 8:16], thrB[:, :]),
                  reads=(BT("thrB"),), writes=(BT("dbgt"),))
            P.add("V", lambda v: v.tensor_copy(sb["dbgt"][:, 16:17], gate_d[:, 0:1]),
                  reads=(gate_dr,), writes=(BT("dbgt"),))
            P.add("V", lambda v: v.tensor_copy(sb["dbgt"][:, 17:18], dest_d[:, 0:1]),
                  reads=(dest_dr,), writes=(BT("dbgt"),))
            P.add("V", lambda v: v.tensor_copy(sb["dbgt"][:, 18:19], rnk_d[:, 0:1]),
                  reads=(rnk_dr,), writes=(BT("dbgt"),))
            P.add("V", lambda v: v.tensor_copy(sb["dbgt"][:, 19:20], bpm_d[:, 0:1]),
                  reads=(bpm_dr,), writes=(BT("dbgt"),))
            P.add("V", lambda v: v.tensor_copy(sb["dbgt"][:, 20:21], bem_d[:, 0:1]),
                  reads=(bem_dr,), writes=(BT("dbgt"),))
            P.add("V", lambda v: v.tensor_copy(sb["dbgt"][:, 21:22], bpu_d[:, 0:1]),
                  reads=(bpu_dr,), writes=(BT("dbgt"),))
            P.add("V", lambda v: v.tensor_copy(sb["dbgt"][:, 22:23], beu_d[:, 0:1]),
                  reads=(beu_dr,), writes=(BT("dbgt"),))
        dma("Y", dbg[0:1, 64:96], scrow[0:1, :], reads=(BT("wsc"), BT("xsc")), tag="dbgrow")
        if "moe1" in enable or "moe2" in enable:
            for row, (src, deps_) in enumerate((
                    (xsort_d[0:1, 0:64], ("xsort",)),
                    (lnx_ext[0:1, 0:64], ("lnx0",)),
            )):
                dma("Y", sb["dbgt2"][row:row + 1, 0:64], src,
                    reads=tuple(BT(k) for k in deps_), writes=(BT("dbgt2"),),
                    tag=f"dbgs{row}")
            P.add("G", lambda g: g.dma_start(
                out=sb["dbgt2"][2:3, 0:64], in_=ysort_d[0:1, 0:64]),
                dma=True, reads=(BT("ysort"),), writes=(BT("dbgt2"),), tag="dbgys")
            dma("Y", dbg[0:4, 96:160], sb["dbgt2"][0:4, 0:64],
                reads=(BT("dbgt2"), BT("dbgt2pre")), tag="dbgs_out")
        P.add("V", lambda v: v.tensor_copy(sb["dbgt"][:, 32:33], amx[:, 0:1]),
              reads=(BT("amx"),), writes=(BT("dbgt"),))
        P.add("V", lambda v: v.tensor_copy(sb["dbgt"][:, 33:37], stats[:, 0:4]),
              reads=(BT("st0"), BT("st1")), writes=(BT("dbgt"),))
        P.add("V", lambda v: v.tensor_copy(sb["dbgt"][:, 37:38], k128[:, 0:1]),
              reads=(BT("k4b"),), writes=(BT("dbgt"),))
        P.add("V", lambda v: v.tensor_copy(sb["dbgt"][:, 38:39], s2_128[:, 0:1]),
              reads=(BT("s2b"),), writes=(BT("dbgt"),))
        dma("Y", dbg[:, 0:64], sb["dbgt"][:, :], reads=(BT("dbgt"),), tag="dbgout")
        for i in range(NT):
            dma("Y", out[i * 128:(i + 1) * 128, :], xres[i * 128:(i + 1) * 128, :],
                reads=(xres_b[i],), tag="outcp")

        P.emit(block, sems, dsems)
    nc.has_collectives = True
    return nc


def make_rope_consts():
    i64 = np.arange(0, 64, 2, dtype=np.float64) / 64.0
    freqs = 10000.0 ** i64
    t = np.arange(S, dtype=np.float64)
    ang = t[:, None] / freqs[None, :]
    cos = np.cos(ang); sin = np.sin(ang)
    def expand(a):
        a64 = np.repeat(a, 2, axis=1).T
        return np.tile(a64, (2, 1))
    ch = expand(cos); sh = expand(sin)
    chi = ch.astype(np.float32); clo = (ch - chi).astype(np.float32)
    shi = sh.astype(np.float32); slo = (sh - shi).astype(np.float32)
    rope = np.concatenate([chi, clo, shi, slo], axis=1).astype(np.float32)
    rt = np.zeros((128, 128), np.float32)
    for i in range(64):
        rt[2 * i + 1, 2 * i] = -1.0
        rt[2 * i, 2 * i + 1] = 1.0
    return rope, rt.astype(ml_dtypes.bfloat16)


def make_consts():
    c = np.zeros((128, 64), np.float32)
    for p in range(128):
        c[p, p // 16] = 1.0
    c[:, 16] = 1.0
    c[:, 17] = np.arange(128)
    return c


_cache = {}
_LAST_DBG = None
_ROPE = None
_ALL_DBG = None


def get_program(enable):
    key = tuple(enable)
    if key not in _cache:
        nc = build_program(enable)
        lower_extended_insts(nc)
        _cache[key] = nc
    return _cache[key]


def _get_rope():
    global _ROPE
    if _ROPE is None:
        _ROPE = make_rope_consts()
    return _ROPE


def kernel(L0_attn_x, L0_attn_log_dt, L0_attn_log_A_real, L0_attn_A_imag,
           L0_attn_C, L0_attn_D, L0_ffn_Wg, L0_ffn_W1, L0_ffn_W2,
           L1_attn_wq, L1_attn_bq, L1_attn_wk, L1_attn_bk, L1_attn_wv,
           L1_attn_bv, L1_attn_wo, L1_attn_bo, L1_ffn_weight, L1_ffn_bias,
           _enable=("moe1", "ssm", "moe2", "fp8a", "attn", "fp8b")):
    _get_rope()
    nc = get_program(_enable)
    xf = np.ascontiguousarray(L0_attn_x.reshape(B_FULL * S, D))
    ident_bf = np.eye(128, dtype=ml_dtypes.bfloat16)
    ident_f32 = np.eye(128, dtype=np.float32)
    consts = make_consts()
    base = {
        "l0a_logdt": np.ascontiguousarray(L0_attn_log_dt.reshape(1, D), np.float32),
        "l0a_logA": np.ascontiguousarray(L0_attn_log_A_real, np.float32),
        "l0a_Aim": np.ascontiguousarray(L0_attn_A_imag, np.float32),
        "l0a_C": np.ascontiguousarray(L0_attn_C.reshape(D, 64), np.float32),
        "l0a_D": np.ascontiguousarray(L0_attn_D.reshape(1, D), np.float32),
        "l0f_Wg": np.ascontiguousarray(L0_ffn_Wg, np.float32),
        "l0f_W1": np.ascontiguousarray(L0_ffn_W1, np.float32),
        "l0f_W2": np.ascontiguousarray(L0_ffn_W2, np.float32),
        "l1a_wq": np.ascontiguousarray(L1_attn_wq, np.float32),
        "l1a_bq": np.ascontiguousarray(L1_attn_bq.reshape(1, D), np.float32),
        "l1a_wk": np.ascontiguousarray(L1_attn_wk, np.float32),
        "l1a_bk": np.ascontiguousarray(L1_attn_bk.reshape(1, D), np.float32),
        "l1a_wv": np.ascontiguousarray(L1_attn_wv, np.float32),
        "l1a_bv": np.ascontiguousarray(L1_attn_bv.reshape(1, D), np.float32),
        "l1a_wo": np.ascontiguousarray(L1_attn_wo, np.float32),
        "l1a_bo": np.ascontiguousarray(L1_attn_bo.reshape(1, D), np.float32),
        "l1f_w": np.ascontiguousarray(L1_ffn_weight, np.float32),
        "l1f_b": np.ascontiguousarray(L1_ffn_bias.reshape(1, D), np.float32),
        "ident_bf": ident_bf, "ident_f32": ident_f32, "consts": consts,
        "ones_row": np.ones((1, 128), np.float32),
        "rope_c": _ROPE[0], "rt_c": _ROPE[1],
        "gmat_t": np.repeat(np.eye(8, dtype=np.float32), 16, axis=1),
        "utri_c": (np.triu(np.ones((128, 128)), 1)).astype(np.float32),
        "utri64_c": (np.triu(np.ones((64, 64)), 1)).astype(np.float32),
    }
    in_maps = []
    for c in range(NCORES):
        m = dict(base)
        m["x_in"] = np.ascontiguousarray(xf[c * T:(c + 1) * T])
        in_maps.append(m)
    res = run_bass_kernel_spmd(nc, in_maps, core_ids=list(range(NCORES)))
    global _LAST_DBG, _ALL_DBG
    _LAST_DBG = res.results[0].get("dbg")
    _ALL_DBG = [r.get("dbg") for r in res.results]
    outs = [r["out"] for r in res.results]
    return np.concatenate(outs, 0).reshape(B_FULL, S, D)


# version 07 (final)
# speedup vs baseline: 1.0110x; 1.0060x over previous
import numpy as np
import ml_dtypes
from contextlib import ExitStack

import concourse.bass as bass
import concourse.mybir as mybir
from concourse.bass_utils import run_bass_kernel_spmd
from concourse.library_overlay import lower_extended_insts
from concourse.bass import IndirectOffsetOnAxis

F32 = mybir.dt.float32
BF16 = mybir.dt.bfloat16
FP8 = mybir.dt.float8e4
I32 = mybir.dt.int32
AF = mybir.ActivationFunctionType
ALU = mybir.AluOpType
AX = mybir.AxisListType

NCORES = 8
B_FULL, S, D = 32, 2048, 1024
B = B_FULL // NCORES
T = B * S
NT = T // 128
DC = D // 128
E = 8
CEXP = 1536
F_MOE = 2048
FC = F_MOE // 128
NH, DK = 16, 64
LN_EPS = 1e-5
BISECT_ITERS = 40

ENGINES = ("T", "V", "S", "G", "Y")


NDMASEM = 8


class Op:
    __slots__ = ("eng", "fn", "deps", "dma", "idx", "cum", "need_inc", "tag",
                 "coll", "osz", "epos", "slot")

    def __init__(self, eng, fn, deps, dma, idx, tag="", coll=False, osz=0):
        self.eng, self.fn, self.deps, self.dma, self.idx = eng, fn, list(deps), dma, idx
        self.cum, self.need_inc, self.tag, self.coll = 0, False, tag, coll
        self.osz, self.epos, self.slot = osz, 0, 0

    @property
    def asyn(self):
        return self.dma or self.coll

    @property
    def incval(self):
        return 16 if self.dma else 1


class Buf:
    def __init__(self):
        self.writers = []
        self.readers = []

    def rdeps(self):
        return list(self.writers)

    def wdeps(self):
        return self.writers + self.readers

    def did_read(self, op):
        self.readers.append(op)

    def did_write(self, op):
        self.writers.append(op)
        if len(self.writers) > 96:
            self.writers = self.writers[-96:]
        self.readers = []


class Prog:
    def __init__(self):
        self.ops = []

    @staticmethod
    def _flat(bs):
        out = []
        for b in bs:
            if isinstance(b, (tuple, list)):
                out.extend(b)
            else:
                out.append(b)
        return out

    def add(self, eng, fn, deps=(), dma=False, tag="", reads=(), writes=(),
            coll=False, osz=0):
        reads = self._flat(reads)
        writes = self._flat(writes)
        dd = list(deps)
        for b in reads:
            dd += b.rdeps()
        for b in writes:
            dd += b.wdeps()
        seen, deps2 = set(), []
        for d in dd:
            if d is not None and id(d) not in seen:
                seen.add(id(d))
                deps2.append(d)
        op = Op(eng, fn, deps2, dma, len(self.ops), tag, coll, osz)
        self.ops.append(op)
        for b in reads:
            b.did_read(op)
        for b in writes:
            b.did_write(op)
        return op

    def emit(self, block, sems, dsems):
        for op in self.ops:
            if op.asyn:
                op.need_inc = True
            for d in op.deps:
                if d.eng != op.eng or d.asyn:
                    d.need_inc = True
        per_eng = {e: [o for o in self.ops if o.eng == e] for e in ENGINES}
        for e in ENGINES:
            for i, o in enumerate(per_eng[e]):
                o.epos = i
        cnt = {e: 0 for e in ENGINES}
        dcnt = {e: [0] * NDMASEM for e in ENGINES}
        dseq = {e: 0 for e in ENGINES}
        for op in self.ops:
            if op.dma:
                op.slot = dseq[op.eng] % NDMASEM
                dseq[op.eng] += 1
                dcnt[op.eng][op.slot] += 16
                op.cum = dcnt[op.eng][op.slot]
            else:
                if op.need_inc:
                    cnt[op.eng] += op.incval
                op.cum = cnt[op.eng]
        for e in ENGINES:
            nd = [o for o in per_eng[e] if not o.dma]
            if nd and not nd[-1].need_inc:
                nd[-1].need_inc = True
                cnt[e] += nd[-1].incval
                nd[-1].cum = cnt[e]
        final = dict(cnt)

        def emit_stream(eng_obj, ename):
            waited = {e: 0 for e in ENGINES}
            dwaited = {e: [0] * NDMASEM for e in ENGINES}
            for op in per_eng[ename]:
                need_drain = False
                for d in op.deps:
                    if d.dma:
                        if d.cum > dwaited[d.eng][d.slot]:
                            eng_obj.wait_ge(dsems[d.eng][d.slot], d.cum)
                            dwaited[d.eng][d.slot] = d.cum
                    elif (d.eng != ename or d.asyn) and d.cum > waited[d.eng]:
                        eng_obj.wait_ge(sems[d.eng], d.cum)
                        waited[d.eng] = d.cum
                    if (d.eng == ename and not d.asyn
                            and op.epos - d.epos <= 2 and d.osz <= 256):
                        need_drain = True
                if need_drain and ename in ("V", "S", "G"):
                    eng_obj.drain()
                inst = op.fn(eng_obj)
                if op.need_inc:
                    sem = dsems[ename][op.slot] if op.dma else sems[ename]
                    if inst is None:
                        eng_obj.sem_inc(sem, op.incval)
                    else:
                        inst.then_inc(sem, op.incval)
            if ename == "Y":
                for e in ENGINES:
                    if final.get(e, 0) > waited.get(e, 0):
                        eng_obj.wait_ge(sems[e], final[e])
                    for sl in range(NDMASEM):
                        if dcnt[e][sl] > dwaited[e][sl]:
                            eng_obj.wait_ge(dsems[e][sl], dcnt[e][sl])

        @block.tensor
        def _(t):
            emit_stream(t, "T")

        @block.vector
        def _(v):
            emit_stream(v, "V")

        @block.scalar
        def _(s):
            emit_stream(s, "S")

        @block.gpsimd
        def _(g):
            emit_stream(g, "G")

        @block.sync
        def _(y):
            emit_stream(y, "Y")


def build_program(enable):
    nc = bass.Bass("TRN2", target_bir_lowering=False)
    P = Prog()

    x_in = nc.dram_tensor("x_in", [T, D], F32, kind="ExternalInput").ap()
    out = nc.dram_tensor("out", [T, D], F32, kind="ExternalOutput").ap()
    w = {}
    for name, shape in [
        ("l0a_logdt", [1, D]), ("l0a_logA", [D, 32]), ("l0a_Aim", [D, 32]),
        ("l0a_C", [D, 64]), ("l0a_D", [1, D]),
        ("l0f_Wg", [D, E]), ("l0f_W1", [E, D, F_MOE]), ("l0f_W2", [E, F_MOE, D]),
        ("l1a_wq", [D, D]), ("l1a_bq", [1, D]), ("l1a_wk", [D, D]),
        ("l1a_bk", [1, D]), ("l1a_wv", [D, D]), ("l1a_bv", [1, D]),
        ("l1a_wo", [D, D]), ("l1a_bo", [1, D]),
        ("l1f_w", [D, D]), ("l1f_b", [1, D]),
    ]:
        w[name] = nc.dram_tensor(name, shape, F32, kind="ExternalInput").ap()
    ident_bf = nc.dram_tensor("ident_bf", [128, 128], BF16, kind="ExternalInput").ap()
    ident_f32 = nc.dram_tensor("ident_f32", [128, 128], F32, kind="ExternalInput").ap()
    consts = nc.dram_tensor("consts", [128, 64], F32, kind="ExternalInput").ap()
    ones_row = nc.dram_tensor("ones_row", [1, 128], F32, kind="ExternalInput").ap()
    rt_c = nc.dram_tensor("rt_c", [128, 128], BF16, kind="ExternalInput").ap()
    gmat_t = nc.dram_tensor("gmat_t", [8, 128], F32, kind="ExternalInput").ap()
    utri_c = nc.dram_tensor("utri_c", [128, 128], F32, kind="ExternalInput").ap()
    utri64_c = nc.dram_tensor("utri64_c", [64, 64], F32, kind="ExternalInput").ap()
    rope_c = nc.dram_tensor("rope_c", [128, 4 * S], F32, kind="ExternalInput").ap()

    dbg = nc.dram_tensor("dbg", [128, 160], F32, kind="ExternalOutput").ap()
    xres = nc.dram_tensor("xres", [T, D], F32, kind="Internal").ap()
    scr_f32 = nc.dram_tensor("scr_f32", [T, D], F32, kind="Internal").ap()
    lnbf = nc.dram_tensor("lnbf", [T, D], BF16, kind="Internal").ap()
    lnx_ext = nc.dram_tensor("lnx_ext", [T, D + 64], F32, kind="Internal").ap()
    xsort_d = nc.dram_tensor("xsort_d", [E * CEXP + 128, D + 64], F32, kind="Internal").ap()
    ysort_d = nc.dram_tensor("ysort_d", [E * CEXP + 1, D], BF16, kind="Internal").ap()
    vd_d = nc.dram_tensor("vd_d", [D, 129, 64], F32, kind="Internal").ap()
    wtab_d = nc.dram_tensor("wtab_d", [D, 64, 132], F32, kind="Internal").ap()
    kpad_d = nc.dram_tensor("kpad_d", [D, 256], F32, kind="Internal").ap()
    ktoep_d = nc.dram_tensor("ktoep_d", [D, 128, 128], BF16, kind="Internal").ap()
    vw_d = nc.dram_tensor("vw_d", [D, 128, 64], BF16, kind="Internal").ap()
    lnT_d = nc.dram_tensor("lnT_d", [D, T], BF16, kind="Internal").ap()
    ycm_d = nc.dram_tensor("ycm_d", [D, T], BF16, kind="Internal").ap()
    w1bf_d = nc.dram_tensor("w1bf_d", [E * F_MOE, D], BF16, kind="Internal").ap()
    w2bf_d = nc.dram_tensor("w2bf_d", [E * F_MOE, D], BF16, kind="Internal").ap()
    prbT = nc.dram_tensor("prbT", [1, E * T], F32, kind="Internal").ap()
    prbA = nc.dram_tensor("prbA", [NCORES, E * T], F32, kind="Internal").ap()
    qkT = nc.dram_tensor("qkT", [2 * D, T], BF16, kind="Internal").ap()
    vtok = nc.dram_tensor("vtok", [T, D], BF16, kind="Internal").ap()

    with ExitStack() as ES:
        En = ES.enter_context
        sb = {}

        def S_(name, shape, dt):
            sb[name] = En(nc.sbuf_tensor(name, shape, dt))
            return sb[name]

        bufs = {}

        def BT(name):
            if name not in bufs:
                bufs[name] = Buf()
            return bufs[name]

        ARN = S_("ARN", [128, 16 * 2048], BF16)

        def AR(c0, ncols, dt=BF16):
            ap = ARN[:, c0:c0 + ncols]
            if dt == F32:
                ap = ap.bitcast(F32)
            elif dt == FP8:
                ap = ap.bitcast(FP8)
            regs = tuple(BT(f"ARr{r}") for r in
                         range(c0 // 2048, (c0 + ncols + 2047) // 2048))
            return ap, regs

        xt = [S_(f"xt{i}", [128, D], F32) for i in range(2)]
        xnew = [S_(f"xnew{i}", [128, D], F32) for i in range(2)]
        lnt = [S_("lnt0", [128, D], F32)] * 2
        ybf = [S_(f"ybf{i}", [128, D], BF16) for i in range(2)]
        xq8 = [S_(f"xq8_{i}", [128, DC, 128], FP8) for i in range(2)]
        tiny = S_("tiny", [128, 16], F32)
        tiny2 = S_("tiny2", [128, 16], F32)
        stats = S_("stats", [128, 2 * NT], F32)
        amx = S_("amx", [128, 1], F32)
        scrow = S_("scrow", [1, 32], F32)
        k128 = S_("k128", [128, 1], F32)
        s2_128 = S_("s2_128", [128, 1], F32)
        bias128 = S_("bias128", [128, D], F32)
        idn_f32 = S_("idn_f32", [128, 128], F32)
        idn_bfs = S_("idn_bfs", [128, 128], BF16)
        cons = S_("cons", [128, 64], F32)
        wrow = S_("wrow", [128, D], F32)
        onesr = S_("onesr", [1, 128], F32)
        wqt8, wqt8_r = AR(12 * 2048, 4096, FP8)
        S_("dbgt", [128, 64], F32)
        S_("dbgt2", [4, 64], F32)
        S_("vtmp2", [128, 16, 129], F32)
        wg_sb = S_("wg_sb", [128, DC, E], F32)
        gmt = S_("gmt", [8, 128], F32)
        utri = S_("utri", [128, 128], F32)
        utri64 = S_("utri64", [64, 64], F32)
        p8s = S_("p8s", [128, 128], F32)
        lo8 = S_("lo8", [8, 4], F32)
        ge8 = S_("ge8", [8, 4], F32)
        thrB = S_("thrB", [128, 8], F32)
        mid128 = S_("mid128", [128, 1], F32)
        dI = [S_(f"dI{i}", [128, 1], I32) for i in range(2)]
        xls = S_("xls", [128, D + 64], F32)
        gsort = [S_(f"gsort{i}", [128, 2], F32) for i in range(2)]

        wT = {n: S_(f"wT_{n}", [128, DC * D], BF16) for n in ("q", "k", "v")}
        wT["o"] = wT["q"]
        bqk = S_("bqk", [128, 16], F32)
        bv128 = S_("bv128", [128, D], F32)
        bo128 = S_("bo128", [128, D], F32)
        rtc = S_("rtc", [128, 128], BF16)
        ropes, ropes_r = AR(8 * 2048, 8192, F32)
        _lnaT_ap, lnaT_r = AR(0, 8192)
        lnaT = [_lnaT_ap.rearrange("p (a b) -> p a b", a=DC)] * 2
        qstg = [S_(f"qstg{i}", [128, 1024], BF16) for i in range(2)]
        ropt1 = [S_("ropt1_0", [128, 1024], F32)] * 2
        ropt2 = [S_("ropt2_0", [128, 1024], F32)] * 2
        qrt = [S_(f"qrt{i}", [128, 1024], BF16) for i in range(2)]
        qtb = [S_(f"qtb{i}", [64, S], BF16) for i in range(2)]
        ktb = [S_(f"ktb{i}", [64, S], BF16) for i in range(2)]
        vtb = [AR(28672, 1040)[0].rearrange("p (h c) -> p h c", h=16)] * 2
        ptile = [S_("ptile0", [128, S], BF16)] * 2
        otch = [AR(i * 2048, 2048)[0] for i in range(DC)]
        PSP = [En(nc.psum_tensor(f"ps{i}", [128, 1024], F32)) for i in range(4)]
        psB = [PSP[0][:, :], PSP[1][:, :]]
        psT = PSP[2][:, 0:128]
        psU = PSP[3][:, 0:128]

        sems = {e: En(nc.semaphore(f"sem{e}")) for e in ENGINES}
        dsems = {e: [En(nc.semaphore(f"dsem{e}{i}")) for i in range(NDMASEM)]
                 for e in "YGS"}
        dsems["V"] = dsems["Y"]; dsems["T"] = dsems["Y"]
        block = En(nc.Block())

        xres_b = [BT(f"xres{i}") for i in range(NT)]

        def dma(eng, dst, src, deps=(), reads=(), writes=(), tag="dma"):
            return P.add(eng, lambda E_, dst=dst, src=src: E_.dma_start(out=dst, in_=src),
                         deps, dma=True, tag=tag, reads=reads, writes=writes)

        op_idnf = dma("Y", idn_f32[:, :], ident_f32, writes=(BT("idnf"),))
        dma("Y", idn_bfs[:, :], ident_bf, writes=(BT("idnbf"),))
        op_cons = dma("Y", cons[:, :], consts, writes=(BT("cons"),))
        dma("Y", onesr[:, :], ones_row, writes=(BT("ones"),))
        for i in range(NT):
            dma("Y", xres[i * 128:(i + 1) * 128, :], x_in[i * 128:(i + 1) * 128, :],
                writes=(xres_b[i],), tag="xinit")

        def prep_wT(dst, src_ap, tag, extra=()):
            for oc in range(DC):
                ld = dma("G", wrow[:, :], src_ap[oc * 128:(oc + 1) * 128, :],
                         writes=(BT("wrow"),), tag=f"{tag}ld")
                for dc in range(DC):
                    pst = psT if dc % 2 == 0 else psU
                    bpst = BT("ps2") if dc % 2 == 0 else BT("ps3")
                    P.add("T", lambda t, dc=dc, pst=pst: t.transpose(
                        pst[:, :], wrow[:, dc * 128:(dc + 1) * 128], idn_f32[:, :]),
                        reads=(BT("wrow"), BT("idnf")), writes=(bpst,), osz=16384)
                    eng = "V" if dc % 2 == 0 else "S"
                    if eng == "V":
                        P.add("V", lambda v, dc=dc, oc=oc, pst=pst: v.tensor_copy(
                            dst[:, dc * D + oc * 128: dc * D + oc * 128 + 128], pst[:, :]),
                            reads=(bpst,), writes=(BT(f"{tag}_w"),) + extra, osz=16384)
                    else:
                        P.add("S", lambda s, dc=dc, oc=oc, pst=pst: s.copy(
                            dst[:, dc * D + oc * 128: dc * D + oc * 128 + 128], pst[:, :]),
                            reads=(bpst,), writes=(BT(f"{tag}_w"),) + extra, osz=16384)

        if "fp8a" in enable or "fp8b" in enable:
            P.add("V", lambda v: v.memset(amx[:, :], 0.0), writes=(BT("amx"),))
            for oc in range(DC):
                ld = dma("G", wrow[:, :], w["l1f_w"][oc * 128:(oc + 1) * 128, :],
                         writes=(BT("wrow"),), tag="wld")
                P.add("V", lambda v: v.tensor_reduce(
                    tiny[:, 0:1], wrow[:, :], AX.X, ALU.max, apply_absolute_value=True),
                    reads=(BT("wrow"),), writes=(BT("tiny"),))
                P.add("V", lambda v: v.tensor_tensor(
                    amx[:, 0:1], amx[:, 0:1], tiny[:, 0:1], ALU.max),
                    reads=(BT("tiny"), BT("amx")), writes=(BT("amx"),))
            P.add("T", lambda t: t.transpose(psT[0:1, :], amx[:, 0:1], idn_f32[:, :]),
                  reads=(BT("amx"), BT("idnf")), writes=(BT("ps2"),))
            P.add("V", lambda v: v.reduce_max(scrow[0:1, 0:1], psT[0:1, 0:128], AX.X),
                  reads=(BT("ps2"),), writes=(BT("wsc"),))
            P.add("T", lambda t: t.matmul(psT[:, 1:2], onesr[0:1, :],
                  scrow[0:1, 0:1], start=True, stop=True),
                  reads=(BT("wsc"), BT("ones")), writes=(BT("ps2"),))
            wsc128 = S_("wsc128", [128, 4], F32)
            P.add("V", lambda v: v.tensor_scalar(wsc128[:, 0:1], psT[:, 1:2],
                  1e-12, None, ALU.max), reads=(BT("ps2"),), writes=(BT("wsc128"),))
            P.add("V", lambda v: v.reciprocal(wsc128[:, 1:2], wsc128[:, 0:1]),
                  reads=(BT("wsc128"),), writes=(BT("wsc128"),))
            P.add("V", lambda v: v.tensor_scalar(wsc128[:, 1:2], wsc128[:, 1:2],
                  448.0, 448.0, ALU.mult, op1=ALU.min),
                  reads=(BT("wsc128"),), writes=(BT("wsc128"),))
            P.add("V", lambda v: v.tensor_scalar(wsc128[:, 2:3], wsc128[:, 1:2],
                  0.5, None, ALU.mult), reads=(BT("wsc128"),), writes=(BT("wsc128"),))
            P.add("V", lambda v: v.reciprocal(wsc128[:, 3:4], wsc128[:, 1:2]),
                  reads=(BT("wsc128"),), writes=(BT("wsc128"),))
            for oc in range(DC):
                ld2 = dma("G", wrow[:, :], w["l1f_w"][oc * 128:(oc + 1) * 128, :],
                          writes=(BT("wrow"),), tag="wld2")
                for dc in range(DC):
                    pst = psT if dc % 2 == 0 else psU
                    bpst = BT("ps2") if dc % 2 == 0 else BT("ps3")
                    P.add("T", lambda t, dc=dc, pst=pst: t.transpose(
                        pst[:, :], wrow[:, dc * 128:(dc + 1) * 128], idn_f32[:, :]),
                        reads=(BT("wrow"), BT("idnf")), writes=(bpst,), osz=16384)
                    P.add("V" if dc % 2 == 0 else "S",
                          (lambda v, dc=dc, oc=oc, pst=pst: v.tensor_scalar(
                              wqt8[:, dc * D + oc * 128: dc * D + oc * 128 + 128],
                              pst[:, :], wsc128[:, 2:3], None, ALU.mult))
                          if dc % 2 == 0 else
                          (lambda s, dc=dc, oc=oc, pst=pst: s.activation(
                              wqt8[:, dc * D + oc * 128: dc * D + oc * 128 + 128],
                              pst[:, :], AF.Identity, scale=wsc128[:, 2:3])),
                          reads=(bpst, BT("wsc128")), writes=(wqt8_r,), osz=16384)
            dma("G", ropt2[0][0:1, 0:D], w["l1f_b"], writes=(BT("ropt2_0"),))
            for hb in range(2):
                P.add("T", lambda t, hb=hb: t.matmul(psB[0][:, hb * 512:(hb + 1) * 512],
                      onesr[0:1, :], ropt2[0][0:1, hb * 512:hb * 512 + 512],
                      start=True, stop=True),
                      reads=(BT("ropt2_0"), BT("ones")), writes=(BT("ps0"),))
            P.add("V", lambda v: v.tensor_copy(bias128[:, :], psB[0][:, 0:D]),
                  reads=(BT("ps0"),), writes=(BT("bias128"),))

        def ln_stats_pass(phase, with_absmax=False):
            if with_absmax:
                P.add("V", lambda v: v.memset(amx[:, :], 0.0), writes=(BT("amx"),))
            for i in range(NT):
                sl = xt[i % 2]
                bsl = BT(f"xt{i % 2}")
                ld = dma("Y", sl[:, :], xres[i * 128:(i + 1) * 128, :],
                         reads=(xres_b[i],), writes=(bsl,), tag=f"{phase}lnld")
                P.add("V", lambda v, sl=sl: v.reduce_sum(tiny[:, 0:1], sl[:, :], AX.X),
                      reads=(bsl,), writes=(BT("tiny"),))
                P.add("S", lambda s, sl=sl: s.activation(
                    lnt[0][:, :], sl[:, :], AF.Square, accum_out=tiny[:, 1:2]),
                    reads=(bsl,), writes=(BT("lnt0"), BT("tiny")))
                P.add("V", lambda v: v.tensor_scalar(tiny[:, 2:3], tiny[:, 0:1],
                      1.0 / D, None, ALU.mult), reads=(BT("tiny"),), writes=(BT("tiny"),))
                P.add("V", lambda v: v.tensor_scalar(tiny[:, 3:4], tiny[:, 1:2],
                      1.0 / D, None, ALU.mult), reads=(BT("tiny"),), writes=(BT("tiny"),))
                P.add("V", lambda v: v.tensor_tensor(tiny[:, 4:5], tiny[:, 2:3],
                      tiny[:, 2:3], ALU.mult), reads=(BT("tiny"),), writes=(BT("tiny"),))
                P.add("V", lambda v: v.tensor_tensor(tiny[:, 5:6], tiny[:, 3:4],
                      tiny[:, 4:5], ALU.subtract), reads=(BT("tiny"),), writes=(BT("tiny"),))
                P.add("V", lambda v: v.tensor_scalar(tiny[:, 6:7], tiny[:, 5:6],
                      LN_EPS, None, ALU.add), reads=(BT("tiny"),), writes=(BT("tiny"),))
                sq = P.add("S", lambda s: s.activation(tiny2[:, 0:1], tiny[:, 6:7], AF.Sqrt),
                      reads=(BT("tiny"),), writes=(BT("tiny2"),))
                P.add("V", lambda v, i=i: v.reciprocal(stats[:, 2 * i:2 * i + 1], tiny2[:, 0:1]),
                      reads=(BT("tiny2"),), writes=(BT(f"st{i}"),))
                P.add("V", lambda v, i=i: v.tensor_scalar(
                    stats[:, 2 * i + 1:2 * i + 2], tiny[:, 2:3],
                    -1.0, stats[:, 2 * i:2 * i + 1], ALU.mult, op1=ALU.mult),
                    reads=(BT("tiny"), BT(f"st{i}")), writes=(BT(f"st{i}"),))
                if with_absmax:
                    nm = P.add("S", lambda s, sl=sl, i=i: s.activation(
                        xnew[1][:, :], sl[:, :], AF.Identity,
                        bias=stats[:, 2 * i + 1:2 * i + 2],
                        scale=stats[:, 2 * i:2 * i + 1]),
                        reads=(bsl, BT(f"st{i}")), writes=(BT("xnew1"),))
                    P.add("V", lambda v: v.tensor_reduce(
                        tiny[:, 7:8], xnew[1][:, :], AX.X, ALU.max,
                        apply_absolute_value=True),
                        reads=(BT("xnew1"),), writes=(BT("tiny"),))
                    P.add("V", lambda v: v.tensor_tensor(
                        amx[:, 0:1], amx[:, 0:1], tiny[:, 7:8], ALU.max),
                        reads=(BT("tiny"), BT("amx")), writes=(BT("amx"),))

        def fp8_phase(phase):
            amax_loc = nc.dram_tensor(f"amax_loc_{phase}", [1, 8], F32, kind="Internal").ap()
            amax_all = nc.dram_tensor(f"amax_all_{phase}", [1, 8], F32, kind="Internal").ap()
            ln_stats_pass(phase, with_absmax=True)
            P.add("T", lambda t: t.transpose(psT[0:1, :], amx[:, 0:1], idn_f32[:, :]),
                  reads=(BT("amx"), BT("idnf")), writes=(BT("ps2"),))
            P.add("V", lambda v: v.reduce_max(scrow[0:1, 8:9], psT[0:1, 0:128], AX.X),
                  reads=(BT("ps2"),), writes=(BT("xsc"),))
            dma("G", amax_loc[0:1, 0:1], scrow[0:1, 8:9], reads=(BT("xsc"),), writes=(BT("amaxd"),))
            cc = P.add("G", lambda g: g.collective_compute(
                "AllReduce", ALU.max, [list(range(NCORES))], [amax_loc], [amax_all]),
                reads=(BT("amaxd"),), writes=(BT("amaxd"),), coll=True)
            dma("G", scrow[0:1, 8:9], amax_all[0:1, 0:1], reads=(BT("amaxd"),), writes=(BT("xsc"),))
            P.add("T", lambda t: t.matmul(psT[:, 1:2], onesr[0:1, :],
                  scrow[0:1, 8:9], start=True, stop=True),
                  reads=(BT("xsc"), BT("ones")), writes=(BT("ps2"),))
            xsc128 = tiny2[:, 8:10]
            P.add("V", lambda v: v.tensor_scalar(xsc128[:, 0:1], psT[:, 1:2],
                  1e-12, None, ALU.max), reads=(BT("ps2"),), writes=(BT("xsc128"),))
            P.add("V", lambda v: v.reciprocal(xsc128[:, 0:1], xsc128[:, 0:1]),
                  reads=(BT("xsc128"),), writes=(BT("xsc128"),))
            P.add("V", lambda v: v.tensor_scalar(xsc128[:, 0:1], xsc128[:, 0:1],
                  448.0, 448.0, ALU.mult, op1=ALU.min),
                  reads=(BT("xsc128"),), writes=(BT("xsc128"),))
            P.add("V", lambda v: v.tensor_scalar(s2_128[:, 0:1], xsc128[:, 0:1],
                  0.5, None, ALU.mult), reads=(BT("xsc128"),), writes=(BT("s2b"),))
            P.add("V", lambda v: v.reciprocal(xsc128[:, 1:2], xsc128[:, 0:1]),
                  reads=(BT("xsc128"),), writes=(BT("xsc128"),))
            P.add("V", lambda v: v.tensor_tensor(k128[:, 0:1], xsc128[:, 1:2],
                  wsc128[:, 3:4], ALU.mult),
                  reads=(BT("xsc128"), BT("wsc128")), writes=(BT("k4b"),))
            P.add("V", lambda v: v.tensor_scalar(k128[:, 0:1], k128[:, 0:1],
                  4.0, None, ALU.mult), reads=(BT("k4b"),), writes=(BT("k4b"),))
            for i in range(NT):
                sl, bsl = xt[i % 2], BT(f"xt{i % 2}")
                lsl, blsl = lnt[0], BT("lnt0")
                q8, bq8 = xq8[i % 2], BT(f"xq8_{i % 2}")
                ps, bps = psB[i % 2], BT(f"ps{i % 2}")
                yb, byb = ybf[i % 2], BT(f"ybf{i % 2}")
                xn, bxn = xnew[i % 2], BT(f"xnew{i % 2}")
                ld = dma("Y", sl[:, :], xres[i * 128:(i + 1) * 128, :],
                         reads=(xres_b[i],), writes=(bsl,), tag="fp8ld")
                P.add("S", lambda s, sl=sl, lsl=lsl, i=i: s.activation(
                    lsl[:, :], sl[:, :], AF.Identity,
                    bias=stats[:, 2 * i + 1:2 * i + 2], scale=stats[:, 2 * i:2 * i + 1]),
                    reads=(bsl, BT(f"st{i}")), writes=(blsl,))
                for dc in range(DC):
                    pst = psT if dc % 2 == 0 else psU
                    bpst = BT("ps2") if dc % 2 == 0 else BT("ps3")
                    P.add("T", lambda t, lsl=lsl, dc=dc, pst=pst: t.transpose(
                        pst[:, :], lsl[:, dc * 128:(dc + 1) * 128], idn_f32[:, :]),
                        reads=(blsl, BT("idnf")), writes=(bpst,))
                    if dc % 2 == 0:
                        P.add("V", lambda v, q8=q8, dc=dc, pst=pst: v.tensor_scalar(
                            q8[:, dc, :], pst[:, :], s2_128[:, 0:1], None, ALU.mult),
                            reads=(bpst, BT("s2b")), writes=(bq8,))
                    else:
                        P.add("S", lambda s, q8=q8, dc=dc, pst=pst: s.activation(
                            q8[:, dc, :], pst[:, :], AF.Copy, scale=s2_128[:, 0:1]),
                            reads=(bpst, BT("s2b")), writes=(bq8,))
                for dc in range(DC):
                    for hb in range(2):
                        P.add("T", lambda t, q8=q8, dc=dc, ps=ps, hb=hb: t.matmul(
                            ps[:, hb * 512:(hb + 1) * 512], q8[:, dc, :],
                            wqt8[:, dc * D + hb * 512: dc * D + (hb + 1) * 512],
                            start=(dc == 0), stop=(dc == DC - 1)),
                            reads=(bq8, wqt8_r), writes=(bps,))
                P.add("V", lambda v, ps=ps, yb=yb: v.scalar_tensor_tensor(
                    yb[:, :], ps[:, :], k128[:, 0:1], bias128[:, :],
                    ALU.mult, ALU.add),
                    reads=(bps, BT("k4b"), BT("bias128")), writes=(byb,))
                P.add("V", lambda v, sl=sl, yb=yb, xn=xn: v.scalar_tensor_tensor(
                    xn[:, :], yb[:, :], 0.5, sl[:, :], ALU.mult, ALU.add),
                    reads=(byb, bsl), writes=(bxn,))
                dma("Y", xres[i * 128:(i + 1) * 128, :], xn[:, :],
                    reads=(bxn,), writes=(xres_b[i],), tag="fp8st")



        def ssm_setup():
            PI_HI = float(np.float32(np.pi))
            TPI_HI = float(np.float32(2 * np.pi))
            TPI_LO = float(np.float32(2 * np.pi - np.float64(np.float32(2 * np.pi))))
            ldt128 = S_("ldt128", [128, 8], F32)
            P.add("Y", lambda y: y.dma_start(
                out=ldt128[:, :], in_=bass.AP(w["l0a_logdt"].tensor, 0, [[1, 128], [128, 8]]),
                allow_slow_non_contiguous=True), dma=True, writes=(BT("ldt"),))
            drow = wrow[0:1, :]
            dma("Y", drow[:, :], w["l0a_D"], reads=(BT("wrow"),), writes=(BT("wrow"),))
            dmat = bv128
            for hb in range(2):
                P.add("T", lambda t, hb=hb: t.matmul(
                    PSP[0][:, hb * 512:(hb + 1) * 512], onesr[0:1, :],
                    drow[0:1, hb * 512:hb * 512 + 512], start=True, stop=True),
                    reads=(BT("wrow"), BT("ones")), writes=(BT("ps0"),), osz=512)
            P.add("V", lambda v: v.tensor_copy(dmat[:, :], PSP[0][:, 0:D]),
                  reads=(BT("ps0"),), writes=(BT("dmat"), BT("bv128")), osz=D)
            dt8 = S_("dt8", [128, 8], F32)
            P.add("S", lambda s: s.activation(dt8[:, :], ldt128[:, :], AF.Exp),
                  reads=(BT("ldt"),), writes=(BT("dt8"),))
            smv, smv_r = AR(14 * 2048, 1024, F32)
            la = smv[:, 0:32]
            ai = smv[:, 32:64]
            ct = smv[:, 64:128]
            cw = smv[:, 128:192]
            pw2 = smv[:, 192:256]
            t32 = [smv[:, 256 + 32 * i:288 + 32 * i] for i in range(8)]
            tb = [BT(f"t{i}") for i in range(8)]
            _kkv = sb["ptile0"][:, 0:528].bitcast(F32)
            kknl = _kkv[:, 0:132]
            kk2 = _kkv[:, 132:264]
            vand, vand_r = AR(0, 16512, F32)
            vandv = vand.rearrange("p (l r) -> p l r", l=129)
            wblk, wblk_r = AR(16512, 4224, F32)
            wbv = wblk.rearrange("p (r l) -> p r l", r=16)
            vtmpv = wblk[:, 0:2048].rearrange("p (a b) -> p a b", a=64)

            def TT(dst, a, b, op, rd, wr, osz=64):
                P.add("V", lambda v, dst=dst, a=a, b=b, op=op: v.tensor_tensor(
                    dst, a, b, op), reads=rd, writes=wr, osz=osz)

            def TS(dst, a, s1, s2, op, op1, rd, wr, osz=64):
                if op1 is None:
                    P.add("V", lambda v, dst=dst, a=a: v.tensor_scalar(
                        dst, a, s1, None, op), reads=rd, writes=wr, osz=osz)
                else:
                    P.add("V", lambda v, dst=dst, a=a: v.tensor_scalar(
                        dst, a, s1, s2, op, op1=op1), reads=rd, writes=wr, osz=osz)

            def STT(dst, a, sc, b, op0, op1, rd, wr, osz=64):
                P.add("V", lambda v, dst=dst, a=a, sc=sc, b=b: v.scalar_tensor_tensor(
                    dst, a, sc, b, op0, op1), reads=rd, writes=wr, osz=osz)

            P.add("V", lambda v: v.memset(kknl[:, :], 0.0), writes=(BT("kknl"),))
            P.add("V", lambda v: v.memset(kk2[:, :], 0.0), writes=(BT("kk2"),))
            for ht in range(8):
                P.add("Y", lambda y, ht=ht: y.dma_start(
                    out=kpad_d[ht * 128:(ht + 1) * 128, 0:128], in_=kknl[:, 0:128]),
                    dma=True, reads=(BT("kknl"),), writes=(BT("kpad"),), tag="kz")
                P.add("Y", lambda y, ht=ht: y.dma_start(
                    out=kpad_d[ht * 128:(ht + 1) * 128, 128:256], in_=kk2[:, 0:128]),
                    dma=True, reads=(BT("kk2"),), writes=(BT("kpad"),), tag="kz2")
            for ht in range(8):
                r0 = ht * 128
                dma("Y", la[:, :], w["l0a_logA"][r0:r0 + 128, :], writes=(BT("la"),))
                dma("Y", ai[:, :], w["l0a_Aim"][r0:r0 + 128, :], writes=(BT("ai"),))
                dma("Y", ct[:, :], w["l0a_C"][r0:r0 + 128, :], writes=(BT("ct"),))
                P.add("S", lambda s: s.activation(t32[0][:, :], la[:, :], AF.Exp),
                      reads=(BT("la"),), writes=(tb[0],))
                P.add("V", lambda v, ht=ht: v.tensor_scalar(t32[1][:, :], t32[0][:, :],
                      dt8[:, ht:ht + 1], -1.0, ALU.mult, op1=ALU.mult),
                      reads=(tb[0], BT("dt8")), writes=(tb[1],))
                P.add("V", lambda v, ht=ht: v.tensor_scalar(t32[2][:, :], ai[:, :],
                      dt8[:, ht:ht + 1], None, ALU.mult),
                      reads=(BT("ai"), BT("dt8")), writes=(tb[2],))
                for mlt in (16.0, 8.0, 4.0, 2.0, 1.0, 1.0):
                    TS(t32[3][:, :], t32[2][:, :], mlt * PI_HI, None, ALU.is_gt, None,
                       (tb[2],), (tb[3],))
                    STT(t32[2][:, :], t32[3][:, :], -mlt * TPI_HI, t32[2][:, :],
                        ALU.mult, ALU.add, (tb[3], tb[2]), (tb[2],))
                    STT(t32[2][:, :], t32[3][:, :], -mlt * TPI_LO, t32[2][:, :],
                        ALU.mult, ALU.add, (tb[3], tb[2]), (tb[2],))
                    TS(t32[3][:, :], t32[2][:, :], -mlt * PI_HI, None, ALU.is_lt, None,
                       (tb[2],), (tb[3],))
                    STT(t32[2][:, :], t32[3][:, :], mlt * TPI_HI, t32[2][:, :],
                        ALU.mult, ALU.add, (tb[3], tb[2]), (tb[2],))
                    STT(t32[2][:, :], t32[3][:, :], mlt * TPI_LO, t32[2][:, :],
                        ALU.mult, ALU.add, (tb[3], tb[2]), (tb[2],))
                P.add("S", lambda s: s.activation(t32[4][:, :], t32[1][:, :], AF.Exp),
                      reads=(tb[1],), writes=(tb[4],))
                P.add("S", lambda s: s.activation(t32[5][:, :], t32[2][:, :], AF.Sin),
                      reads=(tb[2],), writes=(tb[5],))
                P.add("S", lambda s: s.activation(t32[6][:, :], t32[2][:, :], AF.Abs),
                      reads=(tb[2],), writes=(tb[6],))
                TS(t32[6][:, :], t32[6][:, :], -1.0, PI_HI / 2, ALU.mult, ALU.add,
                   (tb[6],), (tb[6],))
                P.add("S", lambda s: s.activation(t32[6][:, :], t32[6][:, :], AF.Sin),
                      reads=(tb[6],), writes=(tb[6],))
                TT(pw2[:, 0:32], t32[4][:, :], t32[6][:, :], ALU.mult,
                   (tb[4], tb[6]), (BT("pw2"),))
                TT(pw2[:, 32:64], t32[4][:, :], t32[5][:, :], ALU.mult,
                   (tb[4], tb[5]), (BT("pw2"),))
                TT(t32[7][:, :], t32[0][:, :], t32[0][:, :], ALU.mult, (tb[0],), (tb[7],))
                STT(t32[7][:, :], ai[:, :], 1.0, t32[7][:, :], ALU.mult, ALU.mult,
                    (BT("ai"), tb[7]), (tb[7],))
                TT(t32[7][:, :], t32[0][:, :], t32[0][:, :], ALU.mult, (tb[0],), (tb[7],))
                TT(t32[3][:, :], ai[:, :], ai[:, :], ALU.mult, (BT("ai"),), (tb[3],))
                TT(t32[7][:, :], t32[7][:, :], t32[3][:, :], ALU.add, (tb[7], tb[3]), (tb[7],))
                P.add("V", lambda v: v.reciprocal(t32[7][:, :], t32[7][:, :]),
                      reads=(tb[7],), writes=(tb[7],))
                TS(t32[1][:, :], pw2[:, 0:32], 1.0, None, ALU.subtract, None,
                   (BT("pw2"),), (tb[1],))
                ccr = ct.rearrange("p (n two) -> p n two", two=2)
                TT(cw[:, 0:32], ccr[:, :, 0], t32[1][:, :], ALU.mult,
                   (BT("ct"), tb[1]), (BT("cw"),))
                STT(t32[3][:, :], ccr[:, :, 1], -1.0, pw2[:, 32:64], ALU.mult, ALU.mult,
                    (BT("ct"), BT("pw2")), (tb[3],))
                TT(cw[:, 0:32], cw[:, 0:32], t32[3][:, :], ALU.add,
                   (BT("cw"), tb[3]), (BT("cw"),))
                TT(t32[3][:, :], ccr[:, :, 0], pw2[:, 32:64], ALU.mult,
                   (BT("ct"), BT("pw2")), (tb[3],))
                TT(cw[:, 32:64], ccr[:, :, 1], t32[1][:, :], ALU.mult,
                   (BT("ct"), tb[1]), (BT("cw"),))
                TT(cw[:, 32:64], cw[:, 32:64], t32[3][:, :], ALU.add,
                   (BT("cw"), tb[3]), (BT("cw"),))
                STT(t32[3][:, :], cw[:, 0:32], -1.0, t32[0][:, :], ALU.mult, ALU.mult,
                    (BT("cw"), tb[0]), (tb[3],))
                TT(t32[2][:, :], cw[:, 32:64], ai[:, :], ALU.mult,
                   (BT("cw"), BT("ai")), (tb[2],))
                TT(t32[2][:, :], t32[2][:, :], t32[3][:, :], ALU.add, (tb[2], tb[3]), (tb[2],))
                STT(t32[3][:, :], cw[:, 0:32], -1.0, ai[:, :], ALU.mult, ALU.mult,
                    (BT("cw"), BT("ai")), (tb[3],))
                STT(t32[5][:, :], cw[:, 32:64], -1.0, t32[0][:, :], ALU.mult, ALU.mult,
                    (BT("cw"), tb[0]), (tb[5],))
                TT(t32[3][:, :], t32[3][:, :], t32[5][:, :], ALU.add, (tb[3], tb[5]), (tb[3],))
                TT(cw[:, 0:32], t32[2][:, :], t32[7][:, :], ALU.mult, (tb[2], tb[7]), (BT("cw"),))
                TT(cw[:, 32:64], t32[3][:, :], t32[7][:, :], ALU.mult, (tb[3], tb[7]), (BT("cw"),))
                P.add("V", lambda v: v.memset(vandv[:, 0, 0:32], 1.0), writes=(vand_r,))
                P.add("V", lambda v: v.memset(vandv[:, 0, 32:64], 0.0), writes=(vand_r,))
                vt = vtmpv
                Lc = 1
                while Lc < 129:
                    nn = min(Lc, 129 - Lc)
                    pwre = pw2[:, 0:32].unsqueeze(1).to_broadcast([128, nn, 32])
                    pwim = pw2[:, 32:64].unsqueeze(1).to_broadcast([128, nn, 32])
                    osn = nn * 32
                    TT(vandv[:, Lc:Lc + nn, 0:32], vandv[:, 0:nn, 0:32], pwre,
                       ALU.mult, (vand_r, BT("pw2")), (vand_r,), osn)
                    TT(vt[:, 0:nn, 0:32], vandv[:, 0:nn, 32:64], pwim,
                       ALU.mult, (vand_r, BT("pw2")), (wblk_r,), osn)
                    TT(vandv[:, Lc:Lc + nn, 0:32], vandv[:, Lc:Lc + nn, 0:32],
                       vt[:, 0:nn, 0:32], ALU.subtract, (vand_r, wblk_r), (vand_r,), osn)
                    TT(vandv[:, Lc:Lc + nn, 32:64], vandv[:, 0:nn, 0:32], pwim,
                       ALU.mult, (vand_r, BT("pw2")), (vand_r,), osn)
                    TT(vt[:, 0:nn, 0:32], vandv[:, 0:nn, 32:64], pwre,
                       ALU.mult, (vand_r, BT("pw2")), (wblk_r,), osn)
                    TT(vandv[:, Lc:Lc + nn, 32:64], vandv[:, Lc:Lc + nn, 32:64],
                       vt[:, 0:nn, 0:32], ALU.add, (vand_r, wblk_r), (vand_r,), osn)
                    if 2 * Lc < 129:
                        TT(vt[:, 0, 0:32], pw2[:, 0:32], pw2[:, 0:32], ALU.mult,
                           (BT("pw2"),), (wblk_r,))
                        TT(vt[:, 0, 32:64] if False else vt[:, 1, 0:32],
                           pw2[:, 32:64], pw2[:, 32:64], ALU.mult,
                           (BT("pw2"),), (wblk_r,))
                        TT(vt[:, 2, 0:32], pw2[:, 0:32], pw2[:, 32:64], ALU.mult,
                           (BT("pw2"),), (wblk_r,))
                        TT(pw2[:, 32:64], vt[:, 2, 0:32], vt[:, 2, 0:32], ALU.add,
                           (wblk_r,), (BT("pw2"),))
                        TT(pw2[:, 0:32], vt[:, 0, 0:32], vt[:, 1, 0:32], ALU.subtract,
                           (wblk_r,), (BT("pw2"),))
                    Lc *= 2
                P.add("Y", lambda y, ht=ht: y.dma_start(
                    out=vd_d[ht * 128:(ht + 1) * 128, :, :], in_=vandv[:, :, :]),
                    dma=True, reads=(vand_r,), writes=(BT("vd"),), tag="vdst")
                for half in range(2):
                    for q2 in range(2):
                        vre = vandv[:, :, q2 * 16:(q2 + 1) * 16].transpose([0, 2, 1])
                        vim = vandv[:, :, 32 + q2 * 16:32 + (q2 + 1) * 16].transpose([0, 2, 1])
                        cre = cw[:, q2 * 16:(q2 + 1) * 16].unsqueeze(2).to_broadcast(
                            [128, 16, 129])
                        cim = cw[:, 32 + q2 * 16:32 + (q2 + 1) * 16].unsqueeze(2).to_broadcast(
                            [128, 16, 129])
                        if half == 0:
                            TT(wbv[:, :, 0:129], vre, cre, ALU.mult,
                               (vand_r, BT("cw")), (wblk_r,), 2064)
                            TT(sb["vtmp2"][:, 0:16, 0:129], vim, cim, ALU.mult,
                               (vand_r, BT("cw")), (BT("vtmp2"),), 2064)
                            TT(wbv[:, :, 0:129], wbv[:, :, 0:129],
                               sb["vtmp2"][:, 0:16, 0:129],
                               ALU.subtract, (wblk_r, BT("vtmp2")), (wblk_r,), 2064)
                            TS(wbv[:, :, 0:129], wbv[:, :, 0:129], 2.0, None, ALU.mult,
                               None, (wblk_r,), (wblk_r,), 2064)
                            if q2 == 0:
                                P.add("V", lambda v: v.tensor_reduce(
                                    kknl[:, 0:129], wbv[:, :, 0:129].transpose([0, 2, 1]),
                                    AX.X, ALU.add), reads=(wblk_r,),
                                    writes=(BT("kknl"),), osz=129)
                            else:
                                P.add("V", lambda v: v.tensor_reduce(
                                    kk2[:, 0:129], wbv[:, :, 0:129].transpose([0, 2, 1]),
                                    AX.X, ALU.add), reads=(wblk_r,),
                                    writes=(BT("kk2"),), osz=129)
                                TT(kknl[:, 0:129], kknl[:, 0:129], kk2[:, 0:129],
                                   ALU.add, (BT("kknl"), BT("kk2")), (BT("kknl"),), 129)
                        else:
                            TT(wbv[:, :, 0:129], vim, cre, ALU.mult,
                               (vand_r, BT("cw")), (wblk_r,), 2064)
                            TT(sb["vtmp2"][:, 0:16, 0:129], vre, cim, ALU.mult,
                               (vand_r, BT("cw")), (BT("vtmp2"),), 2064)
                            TT(wbv[:, :, 0:129], wbv[:, :, 0:129],
                               sb["vtmp2"][:, 0:16, 0:129],
                               ALU.add, (wblk_r, BT("vtmp2")), (wblk_r,), 2064)
                            TS(wbv[:, :, 0:129], wbv[:, :, 0:129], -2.0, None, ALU.mult,
                               None, (wblk_r,), (wblk_r,), 2064)
                        P.add("Y", lambda y, ht=ht, half=half, q2=q2: y.dma_start(
                            out=wtab_d[ht * 128:(ht + 1) * 128,
                                       half * 32 + q2 * 16:half * 32 + (q2 + 1) * 16,
                                       0:129],
                            in_=wbv[:, :, 0:129]),
                            dma=True, reads=(wblk_r,), writes=(BT("wtab"),), tag="wtst")
                P.add("Y", lambda y, ht=ht: y.dma_start(
                    out=kpad_d[ht * 128:(ht + 1) * 128, 127:255],
                    in_=kknl[:, 0:128]),
                    dma=True, reads=(BT("kknl"),), writes=(BT("kpad"),), tag="kst")
            for j in range(128):
                P.add("G", lambda g, j=j: g.dma_start(
                    out=bass.AP(ktoep_d.tensor, j * 128, [[16384, 1024], [1, 128]]),
                    in_=bass.AP(kpad_d.tensor, 127 - j, [[256, 1024], [1, 128]])),
                    dma=True, reads=(BT("kpad"),), writes=(BT("ktoep"),), tag="tbuild")
            for cc in range(16):
                P.add("G", lambda g, cc=cc: g.dma_start(
                    out=vw_d[cc * 64:(cc + 1) * 64, :, :],
                    in_=bass.AP(vd_d.tensor, cc * 64 * 8256 + 127 * 64,
                                [[8256, 64], [-64, 128], [1, 64]])),
                    dma=True, reads=(BT("vd"),), writes=(BT("vw"),), tag="vwbuild")

        def ssm_phase():
            ln_stats_pass("ssm", with_absmax=False)
            for i in range(NT):
                sl, bsl = xt[i % 2], BT(f"xt{i % 2}")
                lb, blb = ybf[i % 2], BT(f"ybf{i % 2}")
                dma("Y", sl[:, :], xres[i * 128:(i + 1) * 128, :],
                    reads=(xres_b[i],), writes=(bsl,), tag="ssmld")
                P.add("S", lambda s, sl=sl, lb=lb, i=i: s.activation(
                    lb[:, :], sl[:, :], AF.Identity,
                    bias=stats[:, 2 * i + 1:2 * i + 2], scale=stats[:, 2 * i:2 * i + 1]),
                    reads=(bsl, BT(f"st{i}")), writes=(blb,), osz=D)
                dma("Y", lnbf[i * 128:(i + 1) * 128, :], lb[:, :],
                    reads=(blb,), writes=(BT(f"lnbf{i}"),), tag="ssmst")
            stg = [AR(0, 8192), AR(8192, 8192)]
            for dc in range(DC):
                sg, sg_r = stg[dc % 2]
                for tb in range(8):
                    rd = tuple(BT(f"lnbf{8 * tb + k}") for k in range(8))
                    P.add("Y", lambda y, dc=dc, tb=tb, sg=sg: y.dma_start(
                        out=sg[:, tb * 1024:(tb + 1) * 1024],
                        in_=lnbf[tb * 1024:(tb + 1) * 1024, dc * 128:(dc + 1) * 128],
                        transpose=True),
                        dma=True, reads=rd, writes=sg_r, tag="stgT")
                P.add("G", lambda g, dc=dc, sg=sg: g.dma_start(
                    out=lnT_d[dc * 128:(dc + 1) * 128, :], in_=sg[:, :]),
                    dma=True, reads=sg_r, writes=(BT("lnT"),), tag="stgS")
            QRe, QRe_r = AR(0, 8192, F32)
            QIm, QIm_r = AR(8192, 8192, F32)
            xjgs = [AR(16384, 1024), AR(17408, 1024)]
            tblks = [AR(18432, 2048), AR(28672, 2048)]
            wjblk_f, wjblk_r = AR(20480, 2048)
            wjblk = wjblk_f[0:64, :]
            ytmp, ytmp_r = AR(22528, 2048, F32)
            ocm, ocm_r = AR(30720, 1024)
            lamqR = p8s[:, 0:64]
            lamqI = p8s[:, 64:128]
            BLQ = BT("p8s")
            sst = [ropt1[0][:, 0:512], ropt1[0][:, 512:1024]]
            BSS = BT("ropt1_0")
            sscr = ropt2[0][:, 0:512]
            BSC = BT("ropt2_0")
            crhs = qstg[1]
            BCR = BT("qstg1")
            vwb = qstg[0]
            BVW = BT("qstg0")
            xjl = [qrt[0], qrt[1]]
            QRv = QRe.rearrange("p (gc b c) -> p gc b c", gc=64, b=4)
            QIv = QIm.rearrange("p (gc b c) -> p gc b c", gc=64, b=4)
            lamRb = lamqR[:, :].unsqueeze(2).to_broadcast([128, 64, 4])
            lamIb = lamqI[:, :].unsqueeze(2).to_broadcast([128, 64, 4])

            def xj_pass(grp, xjg, dst_r):
                for cp in range(8):
                    xl, bxl = xjl[cp % 2], BT(f"qrt{cp % 2}")
                    P.add("Y", lambda y, grp=grp, cp=cp, xl=xl: y.dma_start(
                        out=xl[:, 0:128],
                        in_=bass.AP(lnT_d.tensor, (grp * 16 + cp * 2) * T,
                                    [[T, 2], [128, 64], [1, 128]])),
                        dma=True, reads=(BT("lnT"),), writes=(bxl,), tag="xjld")
                    pst, bpst = (psT, BT("ps2")) if cp % 2 == 0 else (psU, BT("ps3"))
                    P.add("T", lambda t, xl=xl, pst=pst: t.matmul(
                        pst[:, :], xl[:, 0:128], idn_bfs[:, :], start=True, stop=True),
                        reads=(bxl, BT("idnbf")), writes=(bpst,), osz=16384)
                    P.add("S", lambda s, cp=cp, pst=pst, xjg=xjg: s.copy(
                        xjg[:, cp * 128:(cp + 1) * 128], pst[:, :]),
                        reads=(bpst,), writes=dst_r, osz=16384)

            for q in range(4):
                for gl in range(16):
                    grp = q * 16 + gl
                    gq, glo = gl >> 2, gl & 3
                    P.add("G", lambda g, grp=grp: g.dma_start(
                        out=vwb[:, :],
                        in_=bass.AP(vw_d.tensor, grp * 16 * 8192,
                                    [[64, 128], [8192, 16], [1, 64]])),
                        dma=True, reads=(BT("vw"),), writes=(BVW,), tag="vwld")
                    xjg, xjg_r = xjgs[gl % 2]
                    xj_pass(grp, xjg, (xjg_r,))
                    for ch in range(16):
                        P.add("T", lambda t, ch=ch, xjg=xjg: t.matmul(
                            PSP[0][0:64, ch * 64:(ch + 1) * 64],
                            vwb[:, ch * 64:(ch + 1) * 64],
                            xjg[:, ch * 64:(ch + 1) * 64],
                            start=True, stop=True),
                            reads=(BVW, xjg_r), writes=(BT("ps0"),), osz=64)
                    P.add("V", lambda v, gq=gq, glo=glo: v.tensor_copy(
                        QRe[gq * 32:(gq + 1) * 32, glo * 1024:(glo + 1) * 1024],
                        PSP[0][0:32, 0:1024]),
                        reads=(BT("ps0"),), writes=(QRe_r,), osz=1024)
                    P.add("V", lambda v, gq=gq, glo=glo: v.tensor_copy(
                        QIm[gq * 32:(gq + 1) * 32, glo * 1024:(glo + 1) * 1024],
                        PSP[0][32:64, 0:1024]),
                        reads=(BT("ps0"),), writes=(QIm_r,), osz=1024)
                for half in range(2):
                    lq = wrow[:, 0:64]
                    P.add("Y", lambda y, q=q, half=half, lq=lq: y.dma_start(
                        out=lq,
                        in_=bass.AP(vd_d.tensor,
                                    (q * 256 + half * 128) * 8256 + 8192,
                                    [[8256, 128], [1, 64]])),
                        dma=True, reads=(BT("vd"), BT("wrow")), writes=(BT("wrow"),),
                        tag="lamld")
                    P.add("T", lambda t, lq=lq: t.transpose(
                        psT[0:64, 0:128], lq, idn_f32[:, :]),
                        reads=(BT("wrow"), BT("idnf")), writes=(BT("ps2"),), osz=8192)
                    for q1 in range(2):
                        gQ = half * 2 + q1
                        P.add("V", lambda v, gQ=gQ, q1=q1: v.tensor_copy(
                            lamqR[gQ * 32:(gQ + 1) * 32, 0:64],
                            psT[0:32, q1 * 64:(q1 + 1) * 64]),
                            reads=(BT("ps2"),), writes=(BLQ,), osz=64)
                        P.add("V", lambda v, gQ=gQ, q1=q1: v.tensor_copy(
                            lamqI[gQ * 32:(gQ + 1) * 32, 0:64],
                            psT[32:64, q1 * 64:(q1 + 1) * 64]),
                            reads=(BT("ps2"),), writes=(BLQ,), osz=64)
                P.add("V", lambda v: v.memset(sst[0][:, :], 0.0), writes=(BSS,))
                for c in range(16):
                    cur, nxt = sst[c % 2], sst[(c + 1) % 2]
                    bcur, bnxt = BSS, BSS
                    cRe = cur[:, 0:256].rearrange("p (gc b) -> p gc b", gc=64)
                    cIm = cur[:, 256:512].rearrange("p (gc b) -> p gc b", gc=64)
                    nRe = nxt[:, 0:256].rearrange("p (gc b) -> p gc b", gc=64)
                    nIm = nxt[:, 256:512].rearrange("p (gc b) -> p gc b", gc=64)
                    m1 = sscr[:, 0:256].rearrange("p (gc b) -> p gc b", gc=64)
                    m2 = sscr[:, 256:512].rearrange("p (gc b) -> p gc b", gc=64)
                    uR = QRv[:, :, :, c]
                    uI = QIv[:, :, :, c]
                    P.add("V", lambda v, nRe=nRe, cRe=cRe: v.tensor_tensor(
                        nRe, cRe, lamRb, ALU.mult),
                        reads=(bcur, BLQ), writes=(bnxt,), osz=64)
                    P.add("V", lambda v, nIm=nIm, cIm=cIm: v.tensor_tensor(
                        nIm, cIm, lamRb, ALU.mult),
                        reads=(bcur, BLQ), writes=(bnxt,), osz=64)
                    P.add("V", lambda v, m1=m1, cIm=cIm: v.tensor_tensor(
                        m1, cIm, lamIb, ALU.mult),
                        reads=(bcur, BLQ), writes=(BSC,), osz=64)
                    P.add("V", lambda v, m2=m2, cRe=cRe: v.tensor_tensor(
                        m2, cRe, lamIb, ALU.mult),
                        reads=(bcur, BLQ), writes=(BSC,), osz=64)
                    P.add("V", lambda v, nRe=nRe, m1=m1: v.tensor_tensor(
                        nRe, nRe, m1, ALU.subtract),
                        reads=(bnxt, BSC), writes=(bnxt,), osz=64)
                    P.add("V", lambda v, nIm=nIm, m2=m2: v.tensor_tensor(
                        nIm, nIm, m2, ALU.add),
                        reads=(bnxt, BSC), writes=(bnxt,), osz=64)
                    P.add("V", lambda v, nRe=nRe, uR=uR: v.tensor_tensor(
                        nRe, nRe, uR, ALU.add),
                        reads=(bnxt, QRe_r), writes=(bnxt,), osz=64)
                    P.add("V", lambda v, nIm=nIm, uI=uI: v.tensor_tensor(
                        nIm, nIm, uI, ALU.add),
                        reads=(bnxt, QIm_r), writes=(bnxt,), osz=64)
                    P.add("V", lambda v, uR=uR, cRe=cRe: v.tensor_copy(uR, cRe),
                          reads=(bcur,), writes=QRe_r, osz=64)
                    P.add("V", lambda v, uI=uI, cIm=cIm: v.tensor_copy(uI, cIm),
                          reads=(bcur,), writes=QIm_r, osz=64)
                for gl in range(16):
                    grp = q * 16 + gl
                    gq, glo = gl >> 2, gl & 3
                    tblk, tblk_r = tblks[gl % 2]
                    xjg, xjg_r = xjgs[gl % 2]
                    P.add("Y", lambda y, grp=grp, tblk=tblk: y.dma_start(
                        out=tblk[:, :],
                        in_=bass.AP(ktoep_d.tensor, grp * 16 * 16384,
                                    [[128, 128], [16384, 16], [1, 128]])),
                        dma=True, reads=(BT("ktoep"),), writes=(tblk_r,), tag="tbld")
                    P.add("G", lambda g, grp=grp: g.dma_start(
                        out=wjblk[:, :],
                        in_=bass.AP(wtab_d.tensor, grp * 16 * 8448 + 1,
                                    [[132, 64], [8448, 16], [1, 128]])),
                        dma=True, reads=(BT("wtab"),), writes=(wjblk_r,), tag="wjld")
                    P.add("V", lambda v, gq=gq, glo=glo: v.tensor_copy(
                        crhs[0:32, :],
                        QRe[gq * 32:(gq + 1) * 32, glo * 1024:(glo + 1) * 1024]),
                        reads=(QRe_r,), writes=(BCR,), osz=1024)
                    P.add("V", lambda v, gq=gq, glo=glo: v.tensor_copy(
                        crhs[32:64, :],
                        QIm[gq * 32:(gq + 1) * 32, glo * 1024:(glo + 1) * 1024]),
                        reads=(QIm_r,), writes=(BCR,), osz=1024)
                    xj_pass(grp, xjg, (xjg_r,))
                    for ch in range(16):
                        P.add("T", lambda t, ch=ch, tblk=tblk, xjg=xjg: t.matmul(
                            PSP[1][:, ch * 64:(ch + 1) * 64],
                            tblk[:, ch * 128:(ch + 1) * 128],
                            xjg[:, ch * 64:(ch + 1) * 64],
                            start=True, stop=False),
                            reads=(tblk_r, xjg_r), writes=(BT("ps1"),), osz=64)
                        P.add("T", lambda t, ch=ch: t.matmul(
                            PSP[1][:, ch * 64:(ch + 1) * 64],
                            wjblk[:, ch * 128:(ch + 1) * 128],
                            crhs[0:64, ch * 64:(ch + 1) * 64],
                            start=False, stop=True),
                            reads=(wjblk_r, BCR), writes=(BT("ps1"),), osz=64)
                    P.add("V", lambda v: v.tensor_copy(ytmp[:, :], PSP[1][:, 0:1024]),
                          reads=(BT("ps1"),), writes=(ytmp_r,), osz=1024)
                    for k in range(8):
                        pst, bpst = (psT, BT("ps2")) if k % 2 == 0 else (psU, BT("ps3"))
                        P.add("T", lambda t, k=k, pst=pst: t.transpose(
                            pst[:, :], ytmp[:, k * 128:(k + 1) * 128], idn_f32[:, :]),
                            reads=(ytmp_r, BT("idnf")), writes=(bpst,), osz=16384)
                        P.add("S", lambda s, k=k, pst=pst: s.copy(
                            ocm[:, k * 128:(k + 1) * 128], pst[:, :]),
                            reads=(bpst,), writes=(ocm_r,), osz=16384)
                    P.add("Y", lambda y, grp=grp: y.dma_start(
                        out=bass.AP(ycm_d.tensor, grp * 16 * T,
                                    [[T, 2], [128, 64], [2 * T, 8], [1, 128]]),
                        in_=ocm[:, :].rearrange("p (k j) -> p k j", k=8)),
                        dma=True, reads=(ocm_r,), writes=(BT("ycm"),), tag="ycst")
            for i in range(NT):
                sl, bsl = xt[i % 2], BT(f"xt{i % 2}")
                lb, blb = ybf[i % 2], BT(f"ybf{i % 2}")
                yt, byt = qrt[i % 2], BT(f"qrt{i % 2}")
                xn, bxn = xnew[i % 2], BT(f"xnew{i % 2}")
                dma("Y", sl[:, :], xres[i * 128:(i + 1) * 128, :],
                    reads=(xres_b[i],), writes=(bsl,), tag="ssmr1")
                P.add("Y", lambda y, i=i, yt=yt: y.dma_start(
                    out=yt[:, :], in_=ycm_d[:, i * 128:(i + 1) * 128], transpose=True),
                    dma=True, reads=(BT("ycm"),), writes=(byt,), tag="ssmr2")
                dma("G", lb[:, :], lnbf[i * 128:(i + 1) * 128, :],
                    reads=(BT(f"lnbf{i}"),), writes=(blb,), tag="ssmr3")
                P.add("V", lambda v, xn=xn, sl=sl, yt=yt: v.tensor_tensor(
                    xn[:, :], sl[:, :], yt[:, :], ALU.add),
                    reads=(bsl, byt), writes=(bxn,), osz=D)
                P.add("V", lambda v, lb=lb: v.tensor_tensor(
                    lnt[0][:, :], lb[:, :], bv128[:, :], ALU.mult),
                    reads=(blb, BT("dmat"), BT("bv128")), writes=(BT("lnt0"),), osz=D)
                P.add("V", lambda v, xn=xn: v.tensor_tensor(
                    xn[:, :], xn[:, :], lnt[0][:, :], ALU.add),
                    reads=(bxn, BT("lnt0")), writes=(bxn,), osz=D)
                dma("Y", xres[i * 128:(i + 1) * 128, :], xn[:, :],
                    reads=(bxn,), writes=(xres_b[i],), tag="ssmr4")

        DUMP = float(E * CEXP)

        def moe_setup():
            w1p = [AR(11 * 2048, 1024), AR(11 * 2048 + 1024, 1024)]
            w2p = [AR(15 * 2048, 1024), AR(15 * 2048 + 1024, 1024)]
            for e in range(E):
                for fc in range(FC):
                    k = e * FC + fc
                    wa, wa_r = w1p[k % 2]
                    wb, wb_r = w2p[k % 2]
                    P.add("G", lambda g, wa=wa, e=e, fc=fc: g.dma_start(
                        out=wa[:, :],
                        in_=bass.AP(w["l0f_W1"].tensor, e * D * F_MOE + fc * 128,
                                    [[F_MOE, 128], [128 * F_MOE, 8], [1, 128]])),
                        dma=True, reads=(wa_r,), writes=(wa_r,), tag="w1p")
                    P.add("G", lambda g, wb=wb, e=e, fc=fc: g.dma_start(
                        out=wb[:, :],
                        in_=bass.AP(w["l0f_W2"].tensor, e * F_MOE * D + fc * 128 * D,
                                    [[D, 128], [1, D]])),
                        dma=True, reads=(wb_r,), writes=(wb_r,), tag="w2p")
                    P.add("Y", lambda y, wa=wa, k=k: y.dma_start(
                        out=w1bf_d[k * 128:(k + 1) * 128, :], in_=wa[:, :]),
                        dma=True, reads=(wa_r,), writes=(BT("w1bf"),), tag="w1ps")
                    P.add("Y", lambda y, wb=wb, k=k: y.dma_start(
                        out=w2bf_d[k * 128:(k + 1) * 128, :], in_=wb[:, :]),
                        dma=True, reads=(wb_r,), writes=(BT("w2bf"),), tag="w2ps")
            dma("Y", wg_sb[:, :, :],
                bass.AP(w["l0f_Wg"].tensor, 0, [[8, 128], [1024, 8], [1, 8]]),
                writes=(BT("wg"),), tag="wgld")
            dma("Y", gmt[:, :], gmat_t, writes=(BT("gmt"),))
            dma("Y", utri[:, :], utri_c, writes=(BT("utri"),))
            dma("Y", utri64[:, :], utri64_c, writes=(BT("utri64"),))
            zt = xnew[0]
            zop = P.add("V", lambda v: v.memset(zt[:, :], 0.0), writes=(BT("xnew0"),))
            for r in range(0, E * CEXP + 128, 128):
                n = min(128, E * CEXP + 128 - r)
                P.add("G", lambda g, r=r, n=n: g.dma_start(
                    out=xsort_d[r:r + n, 0:D], in_=zt[0:n, :]),
                    dma=True, reads=(BT("xnew0"),), writes=(BT("xs_z"),), tag="xsz")
                P.add("G", lambda g, r=r, n=n: g.dma_start(
                    out=xsort_d[r:r + n, D:D + 64], in_=zt[0:n, 0:64]),
                    dma=True, reads=(BT("xnew0"),), writes=(BT("xs_z"),), tag="xsz2")
            P.add("V", lambda v: v.memset(ybf[0][0:1, :], 0.0), writes=(BT("ybf0"),))
            P.add("G", lambda g: g.dma_start(out=ysort_d[E * CEXP:E * CEXP + 1, :],
                  in_=ybf[0][0:1, :]), dma=True, reads=(BT("ybf0"),),
                  writes=(BT("ys_z"),), tag="ysz")

        def moe_phase(phase):
            pbis, pbis_r = AR(0, 8192, F32)
            bisj, bisj_r = AR(4 * 2048, 4096)
            mxT = [AR(6 * 2048, 2048, F32), AR(7 * 2048, 2048, F32)]
            probs_all, probs_r = AR(14 * 2048, 1024, F32)
            gate_all, gate_r = AR(14 * 2048 + 1024, 128, F32)
            dest_all, dest_r = AR(14 * 2048 + 1152, 128, F32)
            bpm, bpm_r = AR(15 * 2048, 128, F32)
            bem, bem_r = AR(15 * 2048 + 128, 128, F32)
            bpu, bpu_r = AR(15 * 2048 + 256, 128, F32)
            beu, beu_r = AR(15 * 2048 + 384, 128, F32)
            meq, meq_r = AR(15 * 2048 + 512, 128, F32)
            sc1, sc1_r = AR(15 * 2048 + 576, 128, F32)
            sc2, sc2_r = AR(15 * 2048 + 704, 128, F32)
            rnk, rnk_r = AR(15 * 2048 + 832, 128, F32)
            sc3, sc3_r = AR(15 * 2048 + 960, 128, F32)
            ln_stats_pass(phase, with_absmax=False)
            for i in range(NT):
                sl, bsl = xt[i % 2], BT(f"xt{i % 2}")
                dma("Y", sl[:, :], xres[i * 128:(i + 1) * 128, :],
                    reads=(xres_b[i],), writes=(bsl,), tag="mrout")
                P.add("S", lambda s, sl=sl, i=i: s.activation(
                    lnt[0][:, :], sl[:, :], AF.Identity,
                    bias=stats[:, 2 * i + 1:2 * i + 2], scale=stats[:, 2 * i:2 * i + 1]),
                    reads=(bsl, BT(f"st{i}")), writes=(BT("lnt0"),), osz=D)
                dma("Y", lnx_ext[i * 128:(i + 1) * 128, 0:D], lnt[0][:, :],
                    reads=(BT("lnt0"),), writes=(BT(f"lnx{i}"),), tag="mlnst")

                mx, mx_r = mxT[i % 2]
                mxv = mx.rearrange("p (a b) -> p a b", a=DC)
                for dc in range(DC):
                    pst, bpst = (psT, BT("ps2")) if dc % 2 == 0 else (psU, BT("ps3"))
                    P.add("T", lambda t, dc=dc, pst=pst: t.transpose(
                        pst[:, :], lnt[0][:, dc * 128:(dc + 1) * 128], idn_f32[:, :]),
                        reads=(BT("lnt0"), BT("idnf")), writes=(bpst,), osz=16384)
                    P.add("V" if dc % 2 == 0 else "S",
                          (lambda v, dc=dc, pst=pst, mxv=mxv: v.tensor_copy(
                              mxv[:, dc, :], pst[:, :])) if dc % 2 == 0 else
                          (lambda s, dc=dc, pst=pst, mxv=mxv: s.copy(
                              mxv[:, dc, :], pst[:, :])),
                          reads=(bpst,), writes=(mx_r,), osz=16384)
                for dc in range(DC):
                    P.add("T", lambda t, dc=dc, mxv=mxv: t.matmul(
                        PSP[1][:, 0:8], mxv[:, dc, :], wg_sb[:, dc, :],
                        start=(dc == 0), stop=(dc == DC - 1)),
                        reads=(mx_r, BT("wg")), writes=(BT("ps1"),), osz=8)
                P.add("V", lambda v: v.reduce_max(tiny[:, 9:10], PSP[1][:, 0:8], AX.X),
                      reads=(BT("ps1"),), writes=(BT("tiny"),))
                P.add("V", lambda v: v.tensor_scalar(tiny[:, 10:11], tiny[:, 9:10],
                      -1.0, None, ALU.mult), reads=(BT("tiny"),), writes=(BT("tiny"),))
                P.add("S", lambda s, i=i: s.activation(
                    probs_all[:, i * 8:(i + 1) * 8], PSP[1][:, 0:8], AF.Exp,
                    bias=tiny[:, 10:11]),
                    reads=(BT("ps1"), BT("tiny")), writes=(probs_r,), osz=8)
                P.add("V", lambda v, i=i: v.reduce_sum(tiny[:, 11:12],
                      probs_all[:, i * 8:(i + 1) * 8], AX.X),
                      reads=(probs_r,), writes=(BT("tiny"),))
                P.add("V", lambda v: v.reciprocal(tiny[:, 12:13], tiny[:, 11:12]),
                      reads=(BT("tiny"),), writes=(BT("tiny"),))
                P.add("V", lambda v, i=i: v.tensor_scalar(
                    probs_all[:, i * 8:(i + 1) * 8], probs_all[:, i * 8:(i + 1) * 8],
                    tiny[:, 12:13], None, ALU.mult),
                    reads=(probs_r, BT("tiny")), writes=(probs_r,), osz=8)
                if i % 16 == 15:
                    i0 = i - 15
                    P.add("T", lambda t, i0=i0: t.transpose(
                        PSP[2][:, 0:128], probs_all[:, i0 * 8:(i0 + 16) * 8],
                        idn_f32[:, :]),
                        reads=(probs_r, BT("idnf")), writes=(BT("ps2"),), osz=16384)
                    P.add("V", lambda v: v.tensor_copy(p8s[:, :], PSP[2][:, 0:128]),
                          reads=(BT("ps2"),), writes=(BT("p8s"),), osz=16384)
                    P.add("G", lambda g, i0=i0: g.dma_start(
                        out=bass.AP(prbT.tensor, i0 * 128,
                                    [[128, 16], [T, 8], [1, 128]]),
                        in_=p8s[:, :]),
                        dma=True, reads=(BT("p8s"),), writes=(BT("prbT"),), tag="prbst")
            if phase == "m1":
                dma("Y", sb["dbgt2"][3:4, 0:64], lnx_ext[0:1, 0:64],
                    reads=(BT("lnx0"),), writes=(BT("dbgt2pre"),), tag="dbgpre")
            cc = P.add("G", lambda g: g.collective_compute(
                "AllGather", ALU.bypass, [list(range(NCORES))],
                [prbT.opt()], [prbA.opt()]),
                reads=(BT("prbT"),), writes=(BT("prbA"),), coll=True)
            for e in range(E):
                P.add("Y", lambda y, e=e: y.dma_start(
                    out=pbis[e * 16:(e + 1) * 16, :],
                    in_=bass.AP(prbA.tensor, e * T,
                                [[E * T, 8], [T // 2, 2], [1, T // 2]])),
                    dma=True, reads=(BT("prbA"),), writes=(pbis_r,), tag="pbld")
            P.add("V", lambda v: v.memset(mid128[:, :], 0.5), writes=(BT("mid128"),))
            P.add("V", lambda v: v.memset(lo8[:, 0:1], 0.0), writes=(BT("lo8"),))
            P.add("V", lambda v: v.memset(lo8[:, 1:2], 1.0), writes=(BT("lo8"),))
            P.add("V", lambda v: v.memset(lo8[:, 2:3], 0.5), writes=(BT("lo8"),))
            for it in range(BISECT_ITERS):
                msrc = mid128[:, 0:1] if it == 0 else PSP[3][:, 0:1]
                mb = BT("mid128") if it == 0 else BT("ps3")
                P.add("V", lambda v, msrc=msrc: v.tensor_scalar(
                    bisj[:, :], pbis[:, :], msrc, 0.0, ALU.is_ge, op1=ALU.add,
                    accum_out=tiny[:, 8:9]),
                    reads=(pbis_r, mb), writes=(bisj_r, BT("tiny")), osz=4096)
                P.add("T", lambda t: t.matmul(PSP[1][0:8, 0:1], cons[:, 0:8],
                      tiny[:, 8:9], start=True, stop=True),
                      reads=(BT("tiny"), BT("cons")), writes=(BT("ps1"),))
                P.add("V", lambda v: v.tensor_scalar(ge8[:, 0:1], PSP[1][0:8, 0:1],
                      float(E * CEXP * 0 + 8192), None, ALU.is_ge),
                      reads=(BT("ps1"),), writes=(BT("ge8"),))
                P.add("V", lambda v: v.tensor_tensor(lo8[:, 3:4], lo8[:, 2:3],
                      lo8[:, 0:1], ALU.subtract), reads=(BT("lo8"),), writes=(BT("lo8"),))
                P.add("V", lambda v: v.scalar_tensor_tensor(lo8[:, 0:1], ge8[:, 0:1],
                      lo8[:, 3:4], lo8[:, 0:1], ALU.mult, ALU.add),
                      reads=(BT("ge8"), BT("lo8")), writes=(BT("lo8"),))
                P.add("V", lambda v: v.tensor_scalar(ge8[:, 1:2], ge8[:, 0:1],
                      -1.0, 1.0, ALU.mult, op1=ALU.add),
                      reads=(BT("ge8"),), writes=(BT("ge8"),))
                P.add("V", lambda v: v.tensor_tensor(lo8[:, 3:4], lo8[:, 2:3],
                      lo8[:, 1:2], ALU.subtract), reads=(BT("lo8"),), writes=(BT("lo8"),))
                P.add("V", lambda v: v.scalar_tensor_tensor(lo8[:, 1:2], ge8[:, 1:2],
                      lo8[:, 3:4], lo8[:, 1:2], ALU.mult, ALU.add),
                      reads=(BT("ge8"), BT("lo8")), writes=(BT("lo8"),))
                P.add("V", lambda v: v.tensor_tensor(lo8[:, 3:4], lo8[:, 0:1],
                      lo8[:, 1:2], ALU.add), reads=(BT("lo8"),), writes=(BT("lo8"),))
                P.add("V", lambda v: v.tensor_scalar(lo8[:, 2:3], lo8[:, 3:4],
                      0.5, None, ALU.mult), reads=(BT("lo8"),), writes=(BT("lo8"),))
                P.add("T", lambda t: t.matmul(PSP[3][:, 0:1], gmt[:, :], lo8[:, 2:3],
                      start=True, stop=True),
                      reads=(BT("gmt"), BT("lo8")), writes=(BT("ps3"),))
            P.add("T", lambda t: t.transpose(PSP[2][0:1, 0:8], lo8[0:8, 0:1],
                  idn_f32[0:8, 0:8]), reads=(BT("lo8"), BT("idnf")), writes=(BT("ps2"),))
            P.add("V", lambda v: v.tensor_copy(tiny2[0:1, 0:8], PSP[2][0:1, 0:8]),
                  reads=(BT("ps2"),), writes=(BT("tiny2"),))
            P.add("T", lambda t: t.matmul(PSP[3][:, 0:8], onesr[0:1, :],
                  tiny2[0:1, 0:8], start=True, stop=True),
                  reads=(BT("tiny2"), BT("ones")), writes=(BT("ps3"),))
            P.add("V", lambda v: v.tensor_copy(thrB[:, :], PSP[3][:, 0:8]),
                  reads=(BT("ps3"),), writes=(BT("thrB"),))
            P.add("V", lambda v: v.memset(bpm[:, :], -1e30), writes=(bpm_r,))
            P.add("V", lambda v: v.memset(bem[:, :], 0.0), writes=(bem_r,))
            P.add("V", lambda v: v.memset(bpu[:, :], -1e30), writes=(bpu_r,))
            P.add("V", lambda v: v.memset(beu[:, :], 0.0), writes=(beu_r,))
            pav = probs_all.rearrange("p (t e) -> p t e", e=E)
            for e in range(E):
                pe = pav[:, :, e]
                P.add("V", lambda v, pe=pe, e=e: v.tensor_scalar(
                    sc1[:, 0:64], pe, thrB[:, e:e + 1], 1e30, ALU.is_ge, op1=ALU.mult),
                    reads=(probs_r, BT("thrB")), writes=(sc1_r,), osz=64)
                P.add("V", lambda v, pe=pe: v.scalar_tensor_tensor(
                    sc1[:, 0:64], sc1[:, 0:64], -1e30, pe, ALU.add, ALU.add),
                    reads=(sc1_r, probs_r), writes=(sc1_r,), osz=64)
                for (bp, bp_r, be, be_r, val, val_r) in (
                        (bpm, bpm_r, bem, bem_r, sc1[:, 0:64], sc1_r),
                        (bpu, bpu_r, beu, beu_r, pe, probs_r)):
                    P.add("V", lambda v, val=val, bp=bp: v.tensor_tensor(
                        sc2[:, 0:64], val, bp[:, 0:64], ALU.is_gt),
                        reads=(val_r, bp_r), writes=(sc2_r,), osz=64)
                    P.add("V", lambda v, val=val, bp=bp: v.tensor_tensor(
                        bp[:, 0:64], bp[:, 0:64], val, ALU.max),
                        reads=(bp_r, val_r), writes=(bp_r,), osz=64)
                    P.add("V", lambda v, be=be, e=e: v.tensor_scalar(
                        sc3[:, 0:64], be[:, 0:64], -1.0, float(e), ALU.mult, op1=ALU.add),
                        reads=(be_r,), writes=(sc3_r,), osz=64)
                    P.add("V", lambda v: v.tensor_tensor(sc3[:, 0:64], sc2[:, 0:64],
                          sc3[:, 0:64], ALU.mult), reads=(sc2_r, sc3_r),
                          writes=(sc3_r,), osz=64)
                    P.add("V", lambda v, be=be: v.tensor_tensor(be[:, 0:64], be[:, 0:64],
                          sc3[:, 0:64], ALU.add), reads=(be_r, sc3_r),
                          writes=(be_r,), osz=64)
            P.add("V", lambda v: v.tensor_scalar(sc1[:, 0:64], bpm[:, 0:64],
                  -1e29, None, ALU.is_gt), reads=(bpm_r,), writes=(sc1_r,), osz=64)
            for (dst, dst_r, a, a_r, b_, b_r) in (
                    (gate_all, gate_r, bpm, bpm_r, bpu, bpu_r),
                    (dest_all, dest_r, bem, bem_r, beu, beu_r)):
                P.add("V", lambda v, a=a, b_=b_: v.tensor_tensor(sc2[:, 0:64],
                      a[:, 0:64], b_[:, 0:64], ALU.subtract),
                      reads=(a_r, b_r), writes=(sc2_r,), osz=64)
                P.add("V", lambda v: v.tensor_tensor(sc2[:, 0:64], sc1[:, 0:64],
                      sc2[:, 0:64], ALU.mult), reads=(sc1_r, sc2_r),
                      writes=(sc2_r,), osz=64)
                P.add("V", lambda v, dst=dst, b_=b_: v.tensor_tensor(dst[:, 0:64],
                      b_[:, 0:64], sc2[:, 0:64], ALU.add),
                      reads=(b_r, sc2_r), writes=(dst_r,), osz=64)
            P.add("V", lambda v: v.memset(rnk[:, 0:64], 0.0), writes=(rnk_r,))
            for e in range(E):
                P.add("V", lambda v, e=e: v.tensor_scalar(meq[:, 0:64],
                      dest_all[:, 0:64], float(e), None, ALU.is_equal),
                      reads=(dest_r,), writes=(meq_r,), osz=64)
                P.add("T", lambda t: t.matmul(PSP[0][:, 0:64], utri[:, :],
                      meq[:, 0:64], start=True, stop=True),
                      reads=(meq_r, BT("utri")), writes=(BT("ps0"),), osz=64)
                P.add("T", lambda t: t.matmul(PSP[2][0:1, 0:64], cons[:, 16:17],
                      meq[:, 0:64], start=True, stop=True),
                      reads=(meq_r, BT("cons")), writes=(BT("ps2"),), osz=64)
                P.add("V", lambda v: v.tensor_copy(sc2[0:1, 0:64], PSP[2][0:1, 0:64]),
                      reads=(BT("ps2"),), writes=(sc2_r,), osz=64)
                P.add("T", lambda t: t.transpose(PSP[2][0:64, 0:1], sc2[0:1, 0:64],
                      idn_f32[0:1, 0:1]), reads=(sc2_r, BT("idnf")),
                      writes=(BT("ps2"),), osz=64)
                P.add("V", lambda v: v.tensor_copy(p8s[0:64, 0:1], PSP[2][0:64, 0:1]),
                      reads=(BT("ps2"),), writes=(BT("p8s"),), osz=64)
                P.add("T", lambda t: t.matmul(PSP[2][0:64, 1:2], utri64[:, :],
                      p8s[0:64, 0:1], start=True, stop=True),
                      reads=(BT("p8s"), BT("utri64")), writes=(BT("ps2"),), osz=64)
                P.add("V", lambda v: v.tensor_copy(p8s[0:64, 1:2], PSP[2][0:64, 1:2]),
                      reads=(BT("ps2"),), writes=(BT("p8s"),), osz=64)
                P.add("T", lambda t: t.transpose(PSP[2][0:1, 8:72], p8s[0:64, 1:2],
                      idn_f32[0:64, 0:64]), reads=(BT("p8s"), BT("idnf")),
                      writes=(BT("ps2"),), osz=64)
                P.add("V", lambda v: v.tensor_copy(sc2[0:1, 0:64], PSP[2][0:1, 8:72]),
                      reads=(BT("ps2"),), writes=(sc2_r,), osz=64)
                P.add("T", lambda t: t.matmul(PSP[0][:, 0:64], onesr[0:1, :],
                      sc2[0:1, 0:64], start=False, stop=True),
                      reads=(sc2_r, BT("ones")), writes=(BT("ps0"),), osz=64)
                P.add("V", lambda v: v.tensor_tensor(sc3[:, 0:64], meq[:, 0:64],
                      PSP[0][:, 0:64], ALU.mult), reads=(meq_r, BT("ps0")),
                      writes=(sc3_r,), osz=64)
                P.add("V", lambda v: v.tensor_tensor(rnk[:, 0:64], rnk[:, 0:64],
                      sc3[:, 0:64], ALU.add), reads=(rnk_r, sc3_r),
                      writes=(rnk_r,), osz=64)
            P.add("V", lambda v: v.scalar_tensor_tensor(dest_all[:, 0:64],
                  dest_all[:, 0:64], float(CEXP), rnk[:, 0:64], ALU.mult, ALU.add),
                  reads=(dest_r, rnk_r), writes=(dest_r,), osz=64)
            P.add("V", lambda v: v.tensor_scalar(sc1[:, 0:64], rnk[:, 0:64],
                  float(CEXP), None, ALU.is_ge), reads=(rnk_r,), writes=(sc1_r,), osz=64)
            P.add("V", lambda v: v.tensor_scalar(sc2[:, 0:64], sc1[:, 0:64],
                  -1.0, 1.0, ALU.mult, op1=ALU.add), reads=(sc1_r,), writes=(sc2_r,), osz=64)
            P.add("V", lambda v: v.tensor_tensor(dest_all[:, 0:64], dest_all[:, 0:64],
                  sc2[:, 0:64], ALU.mult), reads=(dest_r, sc2_r), writes=(dest_r,), osz=64)
            P.add("V", lambda v: v.scalar_tensor_tensor(dest_all[:, 0:64], sc1[:, 0:64],
                  DUMP, dest_all[:, 0:64], ALU.mult, ALU.add),
                  reads=(sc1_r, dest_r), writes=(dest_r,), osz=64)
            if phase == "m1":
                P.add("Y", lambda y: y.dma_start(
                    out=dbg[4:5, 96:160], in_=lnx_ext[0:1, 0:64]),
                    dma=True, deps=(P.ops[-1],), reads=(BT("lnx0"), BT("thrB")),
                    tag="dbgmid")
            for i in range(NT):
                dma("Y", xls[:, 0:D], lnx_ext[i * 128:(i + 1) * 128, 0:D],
                    deps=[o for o in bufs[f"lnx{i}"].rdeps()],
                    reads=(BT(f"lnx{i}"),), writes=(BT("xls"),), tag="msc_ld")
                P.add("V", lambda v, i=i: v.tensor_copy(
                    xls[:, D:D + 1], gate_all[:, i:i + 1]),
                    reads=(gate_r,), writes=(BT("xls"),), osz=64)
                di = dI[i % 2]
                P.add("V", lambda v, di=di, i=i: v.tensor_copy(
                    di[:, 0:1], dest_all[:, i:i + 1]),
                    reads=(dest_r,), writes=(BT(f"dI{i % 2}"),), osz=64)
                P.add("G", lambda g, di=di: g.indirect_dma_start(
                    xsort_d[:, :], IndirectOffsetOnAxis(di[:, 0:1], 0),
                    xls[:, :], None),
                    dma=True, reads=(BT("xls"), BT(f"dI{i % 2}"), BT("xs_z")),
                    writes=(BT("xsort"),), tag="mscat")
            xl = [AR(0, 2176, F32), AR(4 * 2048 + 2048, 2176, F32)]
            xeg = [AR(2 * 2048, 2048), AR(3 * 2048, 2048)]
            w1c = [AR(6 * 2048, 1024), AR(6 * 2048 + 1024, 1024)]
            w2c = [AR(7 * 2048, 1024), AR(7 * 2048 + 1024, 1024)]
            hb = [AR(8 * 2048, 256), AR(8 * 2048 + 256, 256)]
            ysr = [AR(9 * 2048, 1024), AR(9 * 2048 + 1024, 1024)]
            NGRP = CEXP // 256
            for e in range(E):
                for grp in range(NGRP):
                    xg, xg_r = xeg[grp % 2]
                    xgv = xg.rearrange("p (a b) -> p a b", a=DC)
                    gs, gs_r = gsort[grp % 2], BT(f"gsort{grp % 2}")
                    for t2 in range(2):
                        slot = e * CEXP + grp * 256 + t2 * 128
                        xla, xla_r = xl[t2]
                        dma("Y", xla[:, :], xsort_d[slot:slot + 128, :],
                            deps=[o for o in bufs["xsort"].rdeps()],
                            reads=(BT("xsort"),), writes=(xla_r,), tag="mxld")
                        P.add("V", lambda v, gs=gs, xla=xla, t2=t2: v.tensor_copy(
                            gs[:, t2:t2 + 1], xla[:, D:D + 1]),
                            reads=(xla_r,), writes=(gs_r,), osz=64)
                        for dc in range(DC):
                            pst, bpst = (psT, BT("ps2")) if dc % 2 == 0 else (psU, BT("ps3"))
                            P.add("T", lambda t, dc=dc, pst=pst, xla=xla: t.transpose(
                                pst[:, :], xla[:, dc * 128:(dc + 1) * 128], idn_f32[:, :]),
                                reads=(xla_r, BT("idnf")), writes=(bpst,), osz=16384)
                            P.add("V" if dc % 2 == 0 else "S",
                                  (lambda v, dc=dc, pst=pst, xgv=xgv, t2=t2: v.tensor_copy(
                                      xgv[:, dc, t2 * 128:(t2 + 1) * 128], pst[:, :]))
                                  if dc % 2 == 0 else
                                  (lambda s, dc=dc, pst=pst, xgv=xgv, t2=t2: s.copy(
                                      xgv[:, dc, t2 * 128:(t2 + 1) * 128], pst[:, :])),
                                  reads=(bpst,), writes=(xg_r,), osz=16384)
                    for fc in range(FC + 1):
                        if fc < FC:
                            wa, wa_r = w1c[fc % 2]
                            wav = wa.rearrange("p (a b) -> p a b", a=DC)
                            wb, wb_r = w2c[fc % 2]
                            kwc = e * FC + fc
                            P.add("G", lambda g, wa=wa, kwc=kwc: g.dma_start(
                                out=wa[:, :], in_=w1bf_d[kwc * 128:(kwc + 1) * 128, :]),
                                dma=True, reads=(BT("w1bf"),), writes=(wa_r,), tag="w1ld")
                            P.add("G", lambda g, wb=wb, kwc=kwc: g.dma_start(
                                out=wb[:, :], in_=w2bf_d[kwc * 128:(kwc + 1) * 128, :]),
                                dma=True, reads=(BT("w2bf"),), writes=(wb_r,), tag="w2ld")
                            ph = "ab"[fc % 2]
                            psl = PSP[0][:, (fc % 2) * 512:(fc % 2) * 512 + 256]
                            for dc in range(DC):
                                P.add("T", lambda t, dc=dc, wav=wav, xgv=xgv, psl=psl: t.matmul(
                                    psl, wav[:, dc, :], xgv[:, dc, :],
                                    start=(dc == 0), stop=(dc == DC - 1)),
                                    reads=(wa_r, xg_r), writes=(BT(f"ps0{ph}"),), osz=256)
                        if fc >= 1:
                            pwb, pwb_r = w2c[(fc - 1) % 2]
                            phb, phb_r = hb[(fc - 1) % 2]
                            for t2 in range(2):
                                for nh_ in range(2):
                                    P.add("T", lambda t, phb=phb, pwb=pwb, t2=t2, nh_=nh_, fc=fc: t.matmul(
                                        PSP[2 + t2][:, nh_ * 512:(nh_ + 1) * 512],
                                        phb[:, t2 * 128:(t2 + 1) * 128],
                                        pwb[:, nh_ * 512:(nh_ + 1) * 512],
                                        start=(fc - 1 == 0), stop=(fc - 1 == FC - 1)),
                                        reads=(phb_r, pwb_r), writes=(BT(f"ps{2 + t2}"),), osz=512)
                        if fc < FC:
                            hbb, hb_r = hb[fc % 2]
                            P.add("S", lambda s, hbb=hbb, psl=psl: s.activation(
                                hbb[:, :], psl, AF.Gelu),
                                reads=(BT(f"ps0{ph}"),), writes=(hb_r,), osz=256)
                    for t2 in range(2):
                        yy, yy_r = ysr[t2]
                        P.add("V", lambda v, yy=yy, gs=gs, t2=t2: v.tensor_scalar(
                            yy[:, :], PSP[2 + t2][:, :], gs[:, t2:t2 + 1], None, ALU.mult),
                            reads=(BT(f"ps{2 + t2}"), gs_r), writes=(yy_r,), osz=1024)
                        slot = e * CEXP + grp * 256 + t2 * 128
                        P.add("Y", lambda y, yy=yy, slot=slot: y.dma_start(
                            out=ysort_d[slot:slot + 128, :], in_=yy[:, :]),
                            dma=True, reads=(yy_r,), writes=(BT("ysort"),), tag="myst")
            for i in range(NT):
                sl, bsl = xt[i % 2], BT(f"xt{i % 2}")
                dma("Y", sl[:, :], xres[i * 128:(i + 1) * 128, :],
                    reads=(xres_b[i],), writes=(bsl,), tag="mrld")
                di = dI[i % 2]
                P.add("V", lambda v, di=di, i=i: v.tensor_copy(
                    di[:, 0:1], dest_all[:, i:i + 1]),
                    reads=(dest_r,), writes=(BT(f"dI{i % 2}"),), osz=64)
                yb, byb = ybf[i % 2], BT(f"ybf{i % 2}")
                P.add("G", lambda g, yb=yb, di=di: g.indirect_dma_start(
                    yb[:, :], None, ysort_d[:, :],
                    IndirectOffsetOnAxis(di[:, 0:1], 0)),
                    dma=True, reads=(BT(f"dI{i % 2}"), BT("ysort"), BT("ys_z")),
                    writes=(byb,), tag="mgat")
                xn, bxn = xnew[i % 2], BT(f"xnew{i % 2}")
                P.add("V", lambda v, yb=yb, sl=sl, xn=xn: v.scalar_tensor_tensor(
                    xn[:, :], yb[:, :], 0.5, sl[:, :], ALU.mult, ALU.add),
                    reads=(byb, bsl), writes=(bxn,), osz=D)
                dma("Y", xres[i * 128:(i + 1) * 128, :], xn[:, :],
                    reads=(bxn,), writes=(xres_b[i],), tag="mrst")

        def attn_phase():
            prep_wT(wT["q"], w["l1a_wq"], "wq")
            prep_wT(wT["k"], w["l1a_wk"], "wk")
            prep_wT(wT["v"], w["l1a_wv"], "wv")
            dma("Y", rtc[:, :], rt_c, writes=(BT("rtc"),))
            dma("Y", ropes[:, 0:S], rope_c[:, 0:S], writes=(ropes_r,))
            dma("Y", ropes[:, S:2 * S], rope_c[:, 2 * S:3 * S], writes=(ropes_r,))
            for ch in range(4):
                lodma = dma("Y", lnt[0][:, :],
                            rope_c[:, S + (ch % 2) * 1024 + (ch // 2) * 2 * S:
                                   S + (ch % 2) * 1024 + (ch // 2) * 2 * S + 1024],
                            writes=(BT("lnt0"),), tag="ropelo")
                P.add("V", lambda v, ch=ch: v.tensor_tensor(
                    ropes[:, ch * 1024:(ch + 1) * 1024],
                    ropes[:, ch * 1024:(ch + 1) * 1024], lnt[0][:, :], ALU.add),
                    reads=(ropes_r, BT("lnt0")), writes=(ropes_r,), osz=1024)
            for j, nm in ((0, "l1a_bq"), (1, "l1a_bk")):
                P.add("Y", lambda y, j=j, nm=nm: y.dma_start(
                    out=bqk[:, j * 8:(j + 1) * 8],
                    in_=bass.AP(w[nm].tensor, 0, [[1, 128], [128, 8]]),
                    allow_slow_non_contiguous=True),
                    dma=True, writes=(BT("bqk"),))
            for nm, dstt, bb in (("l1a_bv", bv128, "bv128"), ("l1a_bo", bo128, "bo128")):
                ldb = dma("G", ropt2[0][0:1, 0:D], w[nm], writes=(BT("ropt2_0"),))
                for hb in range(2):
                    P.add("T", lambda t, hb=hb: t.matmul(
                        psB[0][:, hb * 512:(hb + 1) * 512], onesr[0:1, :],
                        ropt2[0][0:1, hb * 512:hb * 512 + 512], start=True, stop=True),
                        reads=(BT("ropt2_0"), BT("ones")), writes=(BT("ps0"),), osz=512)
                P.add("V", lambda v, dstt=dstt: v.tensor_copy(dstt[:, :], psB[0][:, 0:D]),
                      reads=(BT("ps0"),), writes=(BT(bb),), osz=D)
            ln_stats_pass("attn", with_absmax=False)
            for i in range(NT):
                sl, bsl = xt[i % 2], BT(f"xt{i % 2}")
                lb, blb = ybf[i % 2], BT(f"ybf{i % 2}")
                dma("Y", sl[:, :], xres[i * 128:(i + 1) * 128, :],
                    reads=(xres_b[i],), writes=(bsl,), tag="attnld")
                P.add("S", lambda s, sl=sl, lb=lb, i=i: s.activation(
                    lb[:, :], sl[:, :], AF.Identity,
                    bias=stats[:, 2 * i + 1:2 * i + 2], scale=stats[:, 2 * i:2 * i + 1]),
                    reads=(bsl, BT(f"st{i}")), writes=(blb,), osz=D)
                dma("Y", lnbf[i * 128:(i + 1) * 128, :], lb[:, :],
                    reads=(blb,), writes=(BT(f"lnbf{i}"),), tag="attnst")
            NBLK = T // 1024
            for blk in range(NBLK):
                lt, blt = lnaT[0], lnaT_r
                lnbf_deps = [o for i in range(blk * 8, blk * 8 + 8)
                             for o in bufs[f"lnbf{i}"].rdeps()]
                for dc in range(DC):
                    P.add("Y", lambda y, blk=blk, dc=dc, lt=lt: y.dma_start(
                        out=lt[:, dc, :],
                        in_=lnbf[blk * 1024:(blk + 1) * 1024, dc * 128:(dc + 1) * 128],
                        transpose=True),
                        dma=True, writes=(blt,), deps=lnbf_deps, tag="lnaTld")
                cb = (blk % 2) * 1024
                for j, (nm, rowbase) in enumerate((("q", 0), ("k", D))):
                    for ic in range(DC):
                        ps, bps = psB[ic % 2], BT(f"ps{ic % 2}")
                        for dc in range(DC):
                            for nh_ in range(2):
                                P.add("T", lambda t, nm=nm, ic=ic, dc=dc, ps=ps, lt=lt, nh_=nh_: t.matmul(
                                    ps[:, nh_ * 512:(nh_ + 1) * 512],
                                    wT[nm][:, dc * D + ic * 128: dc * D + ic * 128 + 128],
                                    lt[:, dc, nh_ * 512:(nh_ + 1) * 512],
                                    start=(dc == 0), stop=(dc == DC - 1)),
                                    reads=(blt, BT(f"w{nm}_w")), writes=(bps,), osz=512)
                        qs, bqs = qstg[ic % 2], BT(f"qstg{ic % 2}")
                        P.add("V", lambda v, ps=ps, qs=qs, j=j, ic=ic: v.tensor_scalar(
                            qs[:, :], ps[:, :], bqk[:, j * 8 + ic:j * 8 + ic + 1],
                            None, ALU.add),
                            reads=(bps, BT("bqk")), writes=(bqs,), osz=1024)
                        for nh_ in range(2):
                            P.add("T", lambda t, qs=qs, nh_=nh_: t.matmul(
                                PSP[2][:, nh_ * 512:(nh_ + 1) * 512], rtc[:, :],
                                qs[:, nh_ * 512:(nh_ + 1) * 512], start=True, stop=True),
                                reads=(bqs, BT("rtc")), writes=(BT("ps2"),), osz=512)
                        t1, bt1 = ropt1[0], BT("ropt1_0")
                        t2, bt2 = ropt2[0], BT("ropt2_0")
                        P.add("V", lambda v, qs=qs, t1=t1, cb=cb: v.tensor_tensor(
                            t1[:, :], qs[:, :], ropes[:, cb:cb + 1024], ALU.mult),
                            reads=(bqs, ropes_r), writes=(bt1,), osz=1024)
                        P.add("V", lambda v, t2=t2, cb=cb: v.tensor_tensor(
                            t2[:, :], PSP[2][:, :], ropes[:, S + cb:S + cb + 1024],
                            ALU.mult), reads=(BT("ps2"), ropes_r), writes=(bt2,), osz=1024)
                        qr, bqr = qrt[ic % 2], BT(f"qrt{ic % 2}")
                        P.add("V", lambda v, qr=qr, t1=t1, t2=t2: v.tensor_tensor(
                            qr[:, :], t1[:, :], t2[:, :], ALU.add),
                            reads=(bt1, bt2), writes=(bqr,), osz=1024)
                        P.add("Y", lambda y, qr=qr, rowbase=rowbase, ic=ic, blk=blk: y.dma_start(
                            out=qkT[rowbase + ic * 128: rowbase + (ic + 1) * 128,
                                    blk * 1024:(blk + 1) * 1024], in_=qr[:, :]),
                            dma=True, reads=(bqr,), writes=(BT(f"qk{j}_{ic}_{blk}"),),
                            tag="qkst")
                for tt in range(8):
                    i = blk * 8 + tt
                    ps, bps = psB[tt % 2], BT(f"ps{tt % 2}")
                    for dc in range(DC):
                        for nh_ in range(2):
                            P.add("T", lambda t, dc=dc, tt=tt, ps=ps, lt=lt, nh_=nh_: t.matmul(
                                ps[:, nh_ * 512:(nh_ + 1) * 512],
                                lt[:, dc, tt * 128:(tt + 1) * 128],
                                wT["v"][:, dc * D + nh_ * 512: dc * D + nh_ * 512 + 512],
                                start=(dc == 0), stop=(dc == DC - 1)),
                                reads=(blt, BT("wv_w")), writes=(bps,), osz=512)
                    vb, bvb = qrt[tt % 2], BT(f"qrt{tt % 2}")
                    P.add("V", lambda v, ps=ps, vb=vb: v.tensor_tensor(
                        vb[:, :], ps[:, :], bv128[:, :], ALU.add),
                        reads=(bps, BT("bv128")), writes=(bvb,), osz=1024)
                    P.add("Y", lambda y, vb=vb, i=i: y.dma_start(
                        out=vtok[i * 128:(i + 1) * 128, :], in_=vb[:, :]),
                        dma=True, reads=(bvb,), writes=(BT(f"vtok{i}"),), tag="vst")
            prep_wT(wT["o"], w["l1a_wo"], "wo", extra=(BT("wq_w"),))
            for b in range(B):
                for h in range(NH):
                    sl = (b * NH + h) % 2
                    qt, bqt = qtb[0], BT("qtb0")
                    kt, bkt = ktb[0], BT("ktb0")
                    vt, bvt = vtb[0], BT("vtb0")
                    P.add("Y", lambda y, qt=qt, h=h, b=b: y.dma_start(
                        out=qt[:, :], in_=qkT[h * 64:(h + 1) * 64, b * S:(b + 1) * S]),
                        dma=True,
                        deps=[o for ic in (h // 2,) for blkk in range(b * 2, b * 2 + 2)
                              for o in bufs[f"qk0_{ic}_{blkk}"].rdeps()],
                        writes=(bqt,), tag="qld")
                    P.add("Y", lambda y, kt=kt, h=h, b=b: y.dma_start(
                        out=kt[:, :], in_=qkT[D + h * 64: D + h * 64 + 64, b * S:(b + 1) * S]),
                        dma=True,
                        deps=[o for ic in (h // 2,) for blkk in range(b * 2, b * 2 + 2)
                              for o in bufs[f"qk1_{ic}_{blkk}"].rdeps()],
                        writes=(bkt,), tag="kld")
                    P.add("V", lambda v, vt=vt: v.memset(vt[:, :, :], 1.0),
                          writes=(bvt,), osz=1040)
                    P.add("Y", lambda y, vt=vt, h=h, b=b: y.dma_start(
                        out=vt[:, :, 0:64],
                        in_=vtok[b * S:(b + 1) * S, h * 64:(h + 1) * 64]),
                        dma=True,
                        deps=[o for i in range(b * 16, b * 16 + 16)
                              for o in bufs[f"vtok{i}"].rdeps()],
                        reads=(bvt,), writes=(bvt,), tag="vld")
                    for ktile in range(16):
                        pt = ptile[0]
                        for qh in range(2):
                            for nh_ in range(2):
                                P.add("T", lambda t, kt=kt, qt=qt, ktile=ktile, qh=qh, nh_=nh_: t.matmul(
                                    PSP[qh][:, nh_ * 512:(nh_ + 1) * 512],
                                    kt[:, ktile * 128:(ktile + 1) * 128],
                                    qt[:, qh * 1024 + nh_ * 512: qh * 1024 + nh_ * 512 + 512],
                                    start=True, stop=True),
                                    reads=(bkt, bqt), writes=(BT(f"ps{qh}"),), osz=512)
                            P.add("S", lambda s, pt=pt, qh=qh: s.activation(
                                pt[:, qh * 1024:(qh + 1) * 1024], PSP[qh][:, :],
                                AF.Exp, scale=0.125),
                                reads=(BT(f"ps{qh}"),), writes=(BT(f"pt{qh}"),), osz=1024)
                        for qh in range(2):
                            for nh_ in range(2):
                                P.add("T", lambda t, vt=vt, pt=pt, ktile=ktile, qh=qh, nh_=nh_: t.matmul(
                                    PSP[2 + qh][0:65, nh_ * 512:(nh_ + 1) * 512],
                                    vt[:, ktile, :],
                                    pt[:, qh * 1024 + nh_ * 512: qh * 1024 + nh_ * 512 + 512],
                                    start=(ktile == 0), stop=(ktile == 15)),
                                    reads=(bvt, BT(f"pt{qh}")), writes=(BT(f"ps{2 + qh}"),), osz=512)
                    och, boch = otch[h // 2], BT(f"ARr{h // 2}")
                    hp = (h % 2) * 64
                    for qh in range(2):
                        P.add("V", lambda v, qh=qh: v.tensor_copy(
                            qstg[qh][0:64, :], PSP[2 + qh][0:64, :]),
                            reads=(BT(f"ps{2 + qh}"),), writes=(BT(f"qstg{qh}"),), osz=1024)
                        rc = ropt1[0] if qh == 0 else ropt2[0]
                        brc = BT("ropt1_0") if qh == 0 else BT("ropt2_0")
                        P.add("V", lambda v, qh=qh, rc=rc: v.reciprocal(
                            rc[0:1, :], PSP[2 + qh][64:65, :]),
                            reads=(BT(f"ps{2 + qh}"),), writes=(brc,), osz=1024)
                        for nh_ in range(2):
                            P.add("T", lambda t, qh=qh, rc=rc, nh_=nh_: t.matmul(
                                PSP[qh][0:64, nh_ * 512:(nh_ + 1) * 512], onesr[0:1, 0:64],
                                rc[0:1, nh_ * 512:(nh_ + 1) * 512], start=True, stop=True),
                                reads=(brc, BT("ones")), writes=(BT(f"ps{qh}"),), osz=512)
                        P.add("V", lambda v, och=och, hp=hp, qh=qh: v.tensor_tensor(
                            och[hp:hp + 64, qh * 1024:(qh + 1) * 1024],
                            qstg[qh][0:64, :],
                            PSP[qh][0:64, :], ALU.mult),
                            reads=(BT(f"qstg{qh}"), BT(f"ps{qh}")), writes=(boch,), osz=1024)
                for tt in range(16):
                    i = b * 16 + tt
                    ps, bps = psB[tt % 2], BT(f"ps{tt % 2}")
                    for ic in range(DC):
                        for nh_ in range(2):
                            P.add("T", lambda t, ic=ic, tt=tt, ps=ps, nh_=nh_: t.matmul(
                                ps[:, nh_ * 512:(nh_ + 1) * 512],
                                otch[ic][:, tt * 128:(tt + 1) * 128],
                                wT["o"][:, ic * D + nh_ * 512: ic * D + nh_ * 512 + 512],
                                start=(ic == 0), stop=(ic == DC - 1)),
                                reads=(BT(f"ARr{ic}"), BT("wo_w")), writes=(bps,), osz=512)
                    sl2, bsl2 = xt[tt % 2], BT(f"xt{tt % 2}")
                    dma("Y", sl2[:, :], xres[i * 128:(i + 1) * 128, :],
                        reads=(xres_b[i],), writes=(bsl2,), tag="oxld")
                    yb, byb = ybf[tt % 2], BT(f"ybf{tt % 2}")
                    P.add("V", lambda v, ps=ps, yb=yb: v.tensor_tensor(
                        yb[:, :], ps[:, :], bo128[:, :], ALU.add),
                        reads=(bps, BT("bo128")), writes=(byb,), osz=D)
                    xn, bxn = xnew[tt % 2], BT(f"xnew{tt % 2}")
                    P.add("V", lambda v, yb=yb, sl2=sl2, xn=xn: v.tensor_tensor(
                        xn[:, :], yb[:, :], sl2[:, :], ALU.add),
                        reads=(byb, bsl2), writes=(bxn,), osz=D)
                    dma("Y", xres[i * 128:(i + 1) * 128, :], xn[:, :],
                        reads=(bxn,), writes=(xres_b[i],), tag="oxst")

        if "ssm" in enable:
            ssm_setup()
        if "moe1" in enable or "moe2" in enable:
            moe_setup()
        if "moe1" in enable:
            moe_phase("m1")
        if "ssm" in enable:
            ssm_phase()
        if "moe2" in enable:
            moe_phase("m2")
        if "fp8a" in enable:
            fp8_phase("fa")
        if "attn" in enable:
            attn_phase()
        if "fp8b" in enable:
            fp8_phase("fb")

        if "moe1" in enable or "moe2" in enable:
            pav_d, pav_dr = AR(14 * 2048, 1024, F32)
            gate_d, gate_dr = AR(14 * 2048 + 1024, 128, F32)
            dest_d, dest_dr = AR(14 * 2048 + 1152, 128, F32)
            bpm_d, bpm_dr = AR(15 * 2048, 128, F32)
            bem_d, bem_dr = AR(15 * 2048 + 128, 128, F32)
            bpu_d, bpu_dr = AR(15 * 2048 + 256, 128, F32)
            beu_d, beu_dr = AR(15 * 2048 + 384, 128, F32)
            rnk_d, rnk_dr = AR(15 * 2048 + 832, 128, F32)
            P.add("V", lambda v: v.tensor_copy(sb["dbgt"][:, 0:8], pav_d[:, 0:8]),
                  reads=(pav_dr,), writes=(BT("dbgt"),))
            P.add("V", lambda v: v.tensor_copy(sb["dbgt"][:, 8:16], thrB[:, :]),
                  reads=(BT("thrB"),), writes=(BT("dbgt"),))
            P.add("V", lambda v: v.tensor_copy(sb["dbgt"][:, 16:17], gate_d[:, 0:1]),
                  reads=(gate_dr,), writes=(BT("dbgt"),))
            P.add("V", lambda v: v.tensor_copy(sb["dbgt"][:, 17:18], dest_d[:, 0:1]),
                  reads=(dest_dr,), writes=(BT("dbgt"),))
            P.add("V", lambda v: v.tensor_copy(sb["dbgt"][:, 18:19], rnk_d[:, 0:1]),
                  reads=(rnk_dr,), writes=(BT("dbgt"),))
            P.add("V", lambda v: v.tensor_copy(sb["dbgt"][:, 19:20], bpm_d[:, 0:1]),
                  reads=(bpm_dr,), writes=(BT("dbgt"),))
            P.add("V", lambda v: v.tensor_copy(sb["dbgt"][:, 20:21], bem_d[:, 0:1]),
                  reads=(bem_dr,), writes=(BT("dbgt"),))
            P.add("V", lambda v: v.tensor_copy(sb["dbgt"][:, 21:22], bpu_d[:, 0:1]),
                  reads=(bpu_dr,), writes=(BT("dbgt"),))
            P.add("V", lambda v: v.tensor_copy(sb["dbgt"][:, 22:23], beu_d[:, 0:1]),
                  reads=(beu_dr,), writes=(BT("dbgt"),))
        dma("Y", dbg[0:1, 64:96], scrow[0:1, :], reads=(BT("wsc"), BT("xsc")), tag="dbgrow")
        if "moe1" in enable or "moe2" in enable:
            for row, (src, deps_) in enumerate((
                    (xsort_d[0:1, 0:64], ("xsort",)),
                    (lnx_ext[0:1, 0:64], ("lnx0",)),
            )):
                dma("Y", sb["dbgt2"][row:row + 1, 0:64], src,
                    reads=tuple(BT(k) for k in deps_), writes=(BT("dbgt2"),),
                    tag=f"dbgs{row}")
            P.add("G", lambda g: g.dma_start(
                out=sb["dbgt2"][2:3, 0:64], in_=ysort_d[0:1, 0:64]),
                dma=True, reads=(BT("ysort"),), writes=(BT("dbgt2"),), tag="dbgys")
            dma("Y", dbg[0:4, 96:160], sb["dbgt2"][0:4, 0:64],
                reads=(BT("dbgt2"), BT("dbgt2pre")), tag="dbgs_out")
        P.add("V", lambda v: v.tensor_copy(sb["dbgt"][:, 32:33], amx[:, 0:1]),
              reads=(BT("amx"),), writes=(BT("dbgt"),))
        P.add("V", lambda v: v.tensor_copy(sb["dbgt"][:, 33:37], stats[:, 0:4]),
              reads=(BT("st0"), BT("st1")), writes=(BT("dbgt"),))
        P.add("V", lambda v: v.tensor_copy(sb["dbgt"][:, 37:38], k128[:, 0:1]),
              reads=(BT("k4b"),), writes=(BT("dbgt"),))
        P.add("V", lambda v: v.tensor_copy(sb["dbgt"][:, 38:39], s2_128[:, 0:1]),
              reads=(BT("s2b"),), writes=(BT("dbgt"),))
        dma("Y", dbg[:, 0:64], sb["dbgt"][:, :], reads=(BT("dbgt"),), tag="dbgout")
        for i in range(NT):
            dma("Y", out[i * 128:(i + 1) * 128, :], xres[i * 128:(i + 1) * 128, :],
                reads=(xres_b[i],), tag="outcp")

        P.emit(block, sems, dsems)
    nc.has_collectives = True
    return nc


def make_rope_consts():
    i64 = np.arange(0, 64, 2, dtype=np.float64) / 64.0
    freqs = 10000.0 ** i64
    t = np.arange(S, dtype=np.float64)
    ang = t[:, None] / freqs[None, :]
    cos = np.cos(ang); sin = np.sin(ang)
    def expand(a):
        a64 = np.repeat(a, 2, axis=1).T
        return np.tile(a64, (2, 1))
    ch = expand(cos); sh = expand(sin)
    chi = ch.astype(np.float32); clo = (ch - chi).astype(np.float32)
    shi = sh.astype(np.float32); slo = (sh - shi).astype(np.float32)
    rope = np.concatenate([chi, clo, shi, slo], axis=1).astype(np.float32)
    rt = np.zeros((128, 128), np.float32)
    for i in range(64):
        rt[2 * i + 1, 2 * i] = -1.0
        rt[2 * i, 2 * i + 1] = 1.0
    return rope, rt.astype(ml_dtypes.bfloat16)


def make_consts():
    c = np.zeros((128, 64), np.float32)
    for p in range(128):
        c[p, p // 16] = 1.0
    c[:, 16] = 1.0
    c[:, 17] = np.arange(128)
    return c


_cache = {}
_LAST_DBG = None
_ROPE = None
_ALL_DBG = None


def get_program(enable):
    key = tuple(enable)
    if key not in _cache:
        nc = build_program(enable)
        lower_extended_insts(nc)
        _cache[key] = nc
    return _cache[key]


def _get_rope():
    global _ROPE
    if _ROPE is None:
        _ROPE = make_rope_consts()
    return _ROPE


def kernel(L0_attn_x, L0_attn_log_dt, L0_attn_log_A_real, L0_attn_A_imag,
           L0_attn_C, L0_attn_D, L0_ffn_Wg, L0_ffn_W1, L0_ffn_W2,
           L1_attn_wq, L1_attn_bq, L1_attn_wk, L1_attn_bk, L1_attn_wv,
           L1_attn_bv, L1_attn_wo, L1_attn_bo, L1_ffn_weight, L1_ffn_bias,
           _enable=("moe1", "ssm", "moe2", "fp8a", "attn", "fp8b")):
    _get_rope()
    nc = get_program(_enable)
    xf = np.ascontiguousarray(L0_attn_x.reshape(B_FULL * S, D))
    ident_bf = np.eye(128, dtype=ml_dtypes.bfloat16)
    ident_f32 = np.eye(128, dtype=np.float32)
    consts = make_consts()
    base = {
        "l0a_logdt": np.ascontiguousarray(L0_attn_log_dt.reshape(1, D), np.float32),
        "l0a_logA": np.ascontiguousarray(L0_attn_log_A_real, np.float32),
        "l0a_Aim": np.ascontiguousarray(L0_attn_A_imag, np.float32),
        "l0a_C": np.ascontiguousarray(L0_attn_C.reshape(D, 64), np.float32),
        "l0a_D": np.ascontiguousarray(L0_attn_D.reshape(1, D), np.float32),
        "l0f_Wg": np.ascontiguousarray(L0_ffn_Wg, np.float32),
        "l0f_W1": np.ascontiguousarray(L0_ffn_W1, np.float32),
        "l0f_W2": np.ascontiguousarray(L0_ffn_W2, np.float32),
        "l1a_wq": np.ascontiguousarray(L1_attn_wq, np.float32),
        "l1a_bq": np.ascontiguousarray(L1_attn_bq.reshape(1, D), np.float32),
        "l1a_wk": np.ascontiguousarray(L1_attn_wk, np.float32),
        "l1a_bk": np.ascontiguousarray(L1_attn_bk.reshape(1, D), np.float32),
        "l1a_wv": np.ascontiguousarray(L1_attn_wv, np.float32),
        "l1a_bv": np.ascontiguousarray(L1_attn_bv.reshape(1, D), np.float32),
        "l1a_wo": np.ascontiguousarray(L1_attn_wo, np.float32),
        "l1a_bo": np.ascontiguousarray(L1_attn_bo.reshape(1, D), np.float32),
        "l1f_w": np.ascontiguousarray(L1_ffn_weight, np.float32),
        "l1f_b": np.ascontiguousarray(L1_ffn_bias.reshape(1, D), np.float32),
        "ident_bf": ident_bf, "ident_f32": ident_f32, "consts": consts,
        "ones_row": np.ones((1, 128), np.float32),
        "rope_c": _ROPE[0], "rt_c": _ROPE[1],
        "gmat_t": np.repeat(np.eye(8, dtype=np.float32), 16, axis=1),
        "utri_c": (np.triu(np.ones((128, 128)), 1)).astype(np.float32),
        "utri64_c": (np.triu(np.ones((64, 64)), 1)).astype(np.float32),
    }
    in_maps = []
    for c in range(NCORES):
        m = dict(base)
        m["x_in"] = np.ascontiguousarray(xf[c * T:(c + 1) * T])
        in_maps.append(m)
    res = run_bass_kernel_spmd(nc, in_maps, core_ids=list(range(NCORES)))
    global _LAST_DBG, _ALL_DBG
    _LAST_DBG = res.results[0].get("dbg")
    _ALL_DBG = [r.get("dbg") for r in res.results]
    outs = [r["out"] for r in res.results]
    return np.concatenate(outs, 0).reshape(B_FULL, S, D)
